# version 04
# baseline (speedup 1.0000x reference)
import numpy as np
import ml_dtypes
import concourse.bass as bass
import concourse.mybir as mybir
from concourse.bass_utils import run_bass_kernel_spmd

F32 = mybir.dt.float32
BF16 = mybir.dt.bfloat16
Alu = mybir.AluOpType
Act = mybir.ActivationFunctionType
AXX = mybir.AxisListType.X
MAGIC = float(1.5 * 2.0 ** 23)

T = 8192
H = 1024
II = 2816
IC = II // 128
HC = H // 128
SEQ = 4096
NH = 16
NCHUNK = SEQ // 128
NT128S = SEQ // 128
NCORE = 8

ENGS = ("sync", "vector", "scalar", "tensor", "gpsimd")
QSEM = {"sync": "qS", "gpsimd": "qG", "scalar": "qC", "vector": "qV"}
C_S1, C_RS1, C_S2, C_RS2, C_S2H, C_S3, C_RS3, C_S4, C_RS4, C_S4H = range(10)


class Sched:
    def __init__(self):
        self.prog = {e: [] for e in ENGS}
        self.cnt = {}
        self.bar_handle = None

    def op(self, eng, fn, waits=(), inc=None, n=1):
        ws = [w for w in waits if w is not None]
        tok = None
        if inc is not None:
            self.cnt[inc] = self.cnt.get(inc, 0) + n
            tok = (inc, self.cnt[inc])
        self.prog[eng].append((fn, ws, (inc, n) if inc else None))
        return tok

    def now(self, sem):
        return (sem, self.cnt.get(sem, 0))

    DMA_SEMS = ("qS", "qG", "qC", "qV", "dx", "dx0", "dx1", "dh0", "dh1",
                "dh2", "dh3", "dw", "dk", "dq0", "dq1", "dr", "de", "dc", "dg",
                "dg0", "dg1", "dg2", "dg3")

    def full_barrier(self):
        qs = [self.now(q) for q in self.DMA_SEMS]
        for e in ENGS:
            w = qs if e == "gpsimd" else []
            self.op(e, lambda eng: eng.nop(nofuse=True),
                    waits=w, inc="bar", n=1)
        val = self.cnt["bar"]
        for e in ENGS:
            self.op(e, lambda eng: None, waits=[("bar", val)])


def build_nc(debug=False, upto="all"):
    nc = bass.Bass("TRN2", target_bir_lowering=False)
    STAGES = ["pass2", "pass3a", "pass3b", "attn", "all"]
    SI = STAGES.index(upto)

    def stage(name):
        return SI >= STAGES.index(name)

    ext_dbg = "ExternalOutput" if debug else "Internal"

    x0d = nc.dram_tensor("x0", [T, H], F32, kind="ExternalInput").ap()
    outd = nc.dram_tensor("out", [T, H], F32, kind="ExternalOutput").ap()
    wqtd = nc.dram_tensor("wq_t", [H, H], F32, kind="ExternalInput").ap()
    wktd = nc.dram_tensor("wk_t", [H, H], F32, kind="ExternalInput").ap()
    wvtd = nc.dram_tensor("wv_t", [H, H], F32, kind="ExternalInput").ap()
    bq2d = nc.dram_tensor("bq2", [128, 8], F32, kind="ExternalInput").ap()
    bk2d = nc.dram_tensor("bk2", [128, 8], F32, kind="ExternalInput").ap()
    bvrd = nc.dram_tensor("bv_row", [1, H], F32, kind="ExternalInput").ap()
    wgd = nc.dram_tensor("w_gate", [II, H], F32, kind="ExternalInput").ap()
    wud = nc.dram_tensor("w_up", [II, H], F32, kind="ExternalInput").ap()
    wdd = nc.dram_tensor("w_down", [H, II], F32, kind="ExternalInput").ap()
    clipd = nc.dram_tensor("clip2", [1, 2], F32, kind="ExternalInput").ap()

    X1 = nc.dram_tensor("X1", [T, H], F32, kind=ext_dbg).ap()
    X2 = nc.dram_tensor("X2", [T, H], F32, kind=ext_dbg).ap()
    HP = nc.dram_tensor("HP", [IC, 128, T], F32, kind=ext_dbg).ap()
    QT = nc.dram_tensor("QTsp", [T // 128, 128, HC, 128], BF16, kind=ext_dbg).ap()
    QWG = nc.dram_tensor("QWG", [HC, 128, II], BF16).ap()
    QWU = nc.dram_tensor("QWU", [HC, 128, II], BF16).ap()
    QWD = nc.dram_tensor("QWD", [IC, 128, H], BF16).ap()
    WQB = nc.dram_tensor("WQB", [HC, 128, H], BF16).ap()
    WKB = nc.dram_tensor("WKB", [HC, 128, H], BF16).ap()
    WVB = nc.dram_tensor("WVB", [HC, 128, H], BF16).ap()
    SCAL = nc.dram_tensor("SCAL", [1, 16], F32).ap()
    RED = [nc.dram_tensor(f"RED{k}", [128, 1], F32).ap() for k in range(4)]
    CCI = [nc.dram_tensor(f"cci{k}", [1, 16], F32) for k in range(4)]
    CCO = [nc.dram_tensor(f"cco{k}", [1, 16], F32) for k in range(4)]
    if debug:
        KDd = nc.dram_tensor("KD", [128, HC, SEQ], BF16, kind="ExternalOutput").ap()
        VDd = nc.dram_tensor("VD", [128, NCHUNK, H], BF16, kind="ExternalOutput").ap()
        DBG1 = nc.dram_tensor("DBG1", [128, 256], F32, kind="ExternalOutput").ap()
        DBGM = nc.dram_tensor("DBGM", [1, 16], F32, kind="ExternalOutput").ap()
        DBGW = nc.dram_tensor("DBGW", [128, 2816], BF16, kind="ExternalOutput").ap()
        DBGQ = nc.dram_tensor("DBGQ", [128, HC, 512], BF16, kind="ExternalOutput").ap()
        X1C = nc.dram_tensor("X1C", [T, H], F32, kind="ExternalOutput").ap()
        QHD = nc.dram_tensor("QHD", [32, 2, 128, 256], BF16, kind="ExternalOutput").ap()

    identd = nc.inline_tensor(np.eye(128, dtype=ml_dtypes.bfloat16), name="identc")

    ARENA = 212832
    nc.alloc_sbuf_tensor("bigarena", [128, ARENA], mybir.dt.uint8)
    base = nc.lookup_mloc("bigarena").addr
    LIMIT = base + ARENA
    cur = [base]

    def at(name, shape, dtype, off=None):
        nbytes = int(np.prod(shape[1:])) * mybir.dt.size(dtype)
        if off is None:
            o = cur[0]
            cur[0] = o + ((nbytes + 31) // 32) * 32
        else:
            o = base + off
        assert o + nbytes <= LIMIT, f"SBUF overflow {name}: {o + nbytes - base}"
        return nc.alloc_sbuf_tensor_at(name, shape, dtype, offset=o)

    ident = at("ident", [128, 128], BF16)
    biasq = at("biasq", [128, 8], F32)
    biask = at("biask", [128, 8], F32)
    bvx = at("bvx", [128, H], F32)
    scal_b = at("scal_b", [128, 16], F32)
    m1 = at("m1", [128, 64], F32)
    r1 = at("r1", [128, 64], F32)
    rq1 = at("rq1", [128, 64], F32)
    m3 = at("m3", [128, 64], F32)
    r3 = at("r3", [128, 64], F32)
    rq3 = at("rq3", [128, 64], F32)
    swg = at("swg", [128, 22], F32)
    swu = at("swu", [128, 22], F32)
    gscale = at("gscale", [128, 22], F32)
    uscale = at("uscale", [128, 22], F32)
    amax = [at(f"amax{k}", [128, 1], F32) for k in range(4)]
    xm1 = at("xm1", [128, 64], F32)
    xm3 = at("xm3", [128, 64], F32)
    biasq8 = at("biasq8", [128, 8], F32)
    red_row = at("red_row", [1, 128], F32)
    misc = at("misc", [1, 16], F32)
    n30 = at("n30", [128, 1], F32)
    ssq3 = at("ssq3", [128, 64], F32)
    GLOB_END = cur[0] - base
    assert GLOB_END < 9500, GLOB_END

    WREG = ((GLOB_END + 63) // 64) * 64
    qwgT = at("qwgT", [128, HC, II], BF16, off=WREG)
    quT = at("quT", [128, HC, II], BF16, off=WREG + 45056)
    qwdT = at("qwdT", [128, IC, H], BF16, off=WREG + 90112)
    kT = at("kT", [128, HC, SEQ], BF16, off=WREG)
    Vt = at("Vt", [128, NCHUNK, H], BF16, off=WREG + 65536)
    WORK = WREG + 135168

    def wat(name, shape, dtype, off):
        return at(name, shape, dtype, off=WORK + off)

    wstage = [wat(f"wstage{i}", [128, II], F32, 11264 * i) for i in range(2)]
    qwnat = wat("qwnat", [128, II], BF16, 22528)
    qkvb16 = wat("qkvb16", [128, H], BF16, 28160)
    wsc = wat("wsc", [128, 8], F32, 30208)
    px = [wat("px0", [128, 4, H], F32, 30976)]
    ptmp = wat("ptmp", [128, H], F32, 47360)
    ptmp2 = wat("ptmp2", [128, H], F32, 51456)
    pcol = wat("pcol", [128, 4], F32, 55552)

    p2_x = [wat(f"p2x{i}", [128, 4, H], F32, 16384 * i) for i in range(2)]
    p2_qx = wat("p2qx", [128, 4, H], BF16, 32768)
    p2_qxT = wat("p2qxT", [128, HC, 512], BF16, 40960)
    p2_t = wat("p2t", [128, H], F32, 49152)
    p2_sg = [wat(f"p2sg{i}", [128, 512], F32, 53248 + 2048 * i) for i in range(2)]
    p2_hp = [wat(f"p2hp{i}", [128, 512], F32, 57344 + 2048 * i) for i in range(4)]
    p2_col = wat("p2col", [128, 24], F32, 65536)

    p3_hp = [wat(f"p3hp{i}", [128, 256], F32, 1024 * i) for i in range(4)]
    p3_qh = [wat(f"p3qh{i}", [128, 256], BF16, 4096 + 512 * i) for i in range(4)]
    p3_x0 = [wat("p3x0", [128, 2, H], F32, 6144)]
    p3_x1 = wat("p3x1", [128, 2, H], F32, 14336)

    b_wq = wat("bwq", [128, HC, H], BF16, 0)
    b_wk = wat("bwk", [128, HC, H], BF16, 16384)
    b_wv = wat("bwv", [128, HC, H], BF16, 32768)
    b_x1 = [wat("bx1", [128, 2, H], F32, 49152)]
    b_ln = wat("bln", [128, H], BF16, 57344)
    b_xT = wat("bxT", [128, HC, 256], BF16, 59392)
    b_col = wat("bcol", [128, 8], F32, 63488)
    b_pt = wat("bpt", [128, H], F32, 63520)
    b_qs = wat("bqs", [128, 2, HC, 128], BF16, 63520)

    a_kTe = wat("akTe", [128, HC, 128], BF16, 0)
    a_VtE = wat("aVtE", [128, H], BF16, 2048)
    a_qt = [wat(f"aqt{i}", [128, HC, 128], BF16, 4096 + 2048 * i) for i in range(2)]
    a_A = [wat(f"aA{i}", [128, 512], BF16, 8192 + 1024 * i) for i in range(4)]
    a_aT = [wat(f"aaT{i}", [128, 512], BF16, 12288 + 1024 * i) for i in range(2)]
    a_ctx = wat("actx", [128, H], F32, 14336)
    a_x1 = wat("ax1", [128, H], F32, 18432)
    a_x2 = wat("ax2", [128, H], F32, 22528)
    a_Sd = [wat(f"aS{i}", [128, SEQ], F32, 26624 + 16384 * i) for i in range(2)]
    a_AF = wat("aAF", [128, SEQ], BF16, 59392)
    a_xc = wat("axc", [128, H], F32, 59392)
    a_col = wat("acol", [128, 16], F32, 67584)
    g_col = wat("gcol", [128, 4], F32, 67648)

    pb = [nc.alloc_psum_tensor(f"pb{i}", [128, 512], F32) for i in range(8)]
    pbh = [p.ap().bitcast(BF16) for p in pb]

    s = Sched()

    def dma(q, out, in_, waits=(), sem=None, **kw):
        return s.op(q, lambda eng, o=out, i=in_, kw=kw: eng.dma_start(out=o, in_=i, **kw),
                    waits=waits, inc=(sem or QSEM[q]), n=16)

    def vop(fn, waits=()):
        return s.op("vector", fn, waits=waits, inc="tV")

    def gop(fn, waits=()):
        return s.op("gpsimd", fn, waits=waits, inc="tG")

    def gdrain():
        s.op("gpsimd", lambda eng: eng.drain())

    def sop(fn, waits=()):
        return s.op("scalar", fn, waits=waits, inc="tS")

    def top(fn, waits=()):
        return s.op("tensor", fn, waits=waits, inc="tP")

    def vdrain():
        s.op("vector", lambda eng: eng.drain())

    def sdrain():
        s.op("scalar", lambda eng: eng.drain())

    cast_tok = None
    spill_tok = None
    for wi, (src, dst) in enumerate(((wqtd, WQB), (wktd, WKB), (wvtd, WVB))):
        for hc in range(HC):
            b = hc % 2
            ldt = dma("sync", wstage[b][:, 0:H], src[128 * hc:128 * hc + 128, :],
                      waits=[cast_tok], sem="dw")
            cast_tok = vop(lambda eng, b=b: eng.tensor_copy(qkvb16[:], wstage[b][:, 0:H]),
                           waits=[ldt, spill_tok])
            spill_tok = dma("gpsimd", dst[hc], qkvb16[:], waits=[cast_tok], sem="dg")
    dma("sync", biasq[:], bq2d[:, :], sem="dk")
    dma("sync", biask[:], bk2d[:, :], sem="dk")
    bvx_ld = dma("sync", bvx[:], bvrd[0:1, :].broadcast_to((128, H)), sem="dk")
    dma("sync", ident[:], identd.ap(), sem="dk")
    kc_ld = dma("sync", misc[0:1, 0:2], clipd[0:1, :], sem="dk")
    id_ld = kc_ld
    clip_ld = kc_ld
    vop(lambda eng: eng.tensor_scalar(
        biasq8[:], biasq[:], 0.125, None, Alu.mult), waits=[kc_ld])
    sig_tok = sop(lambda eng: eng.activation(misc[0:1, 2:4], misc[0:1, 0:2],
                                             Act.Sigmoid), waits=[clip_ld])
    vop(lambda eng: eng.memset(n30[:], -1e30))

    def quant_weight_gu(srcd, dstT, sw_dst):
        q_tok = None
        ev_tok = None
        for ic in range(IC):
            b = ic % 2
            ldt = dma("sync", wstage[b][:, 0:H], srcd[128 * ic:128 * ic + 128, :],
                      waits=[q_tok], sem="dw")
            rm = vop(lambda eng, b=b: eng.reduce_max(wsc[:, 0:1], wstage[b][:, 0:H], AXX,
                                                     apply_absolute_value=True),
                     waits=[ldt])
            vdrain()
            vop(lambda eng, ic=ic, d=sw_dst: eng.tensor_scalar(
                d[:, ic:ic + 1], wsc[:, 0:1], 1.0 / 127.0, None, Alu.mult))
            vop(lambda eng: eng.reciprocal(wsc[:, 1:2], wsc[:, 0:1]))
            vdrain()
            vop(lambda eng, b=b: eng.tensor_scalar(
                wstage[b][:, 0:H], wstage[b][:, 0:H], wsc[:, 1:2], 127.0,
                Alu.mult, Alu.mult))
            q_tok = vop(lambda eng, b=b: eng.tensor_scalar(
                qwnat[:, 0:H], wstage[b][:, 0:H], MAGIC, MAGIC, Alu.add, Alu.subtract),
                waits=[ev_tok])
            for hcx in range(HC):
                pbk = 6 + (hcx % 2)
                tr = top(lambda eng, hcx=hcx, pbk=pbk: eng.transpose(
                    pbh[pbk][:, 0:128], qwnat[:, 128 * hcx:128 * hcx + 128], ident[:]),
                    waits=[q_tok, ev_tok, id_ld])
                ev_tok = vop(lambda eng, hcx=hcx, ic=ic, pbk=pbk, d=dstT: eng.tensor_copy(
                    d[:, hcx, 128 * ic:128 * ic + 128], pbh[pbk][:, 0:128]), waits=[tr])
        return ev_tok

    gu_ev = quant_weight_gu(wgd, qwgT, swg)
    gu_ev2 = quant_weight_gu(wud, quT, swu)

    def quant_weight_down():
        ev_tok = None
        q_tok = None
        for hcx in range(HC):
            b = hcx % 2
            ldt = dma("sync", wstage[b][:], wdd[128 * hcx:128 * hcx + 128, :],
                      waits=[q_tok], sem="dw")
            rm = vop(lambda eng, b=b: eng.reduce_max(wsc[:, 0:1], wstage[b][:], AXX,
                                                     apply_absolute_value=True),
                     waits=[ldt])
            vdrain()
            vop(lambda eng: eng.reciprocal(wsc[:, 1:2], wsc[:, 0:1]))
            vop(lambda eng: eng.tensor_scalar(
                wsc[:, 2:3], wsc[:, 0:1], 1.0 / 127.0, None, Alu.mult))
            vdrain()
            vop(lambda eng, b=b: eng.tensor_scalar(
                wstage[b][:], wstage[b][:], wsc[:, 1:2], 127.0, Alu.mult, Alu.mult))
            vop(lambda eng, b=b: eng.tensor_scalar(
                wstage[b][:], wstage[b][:], MAGIC, MAGIC, Alu.add, Alu.subtract))
            q_tok = vop(lambda eng, b=b: eng.tensor_scalar(
                qwnat[:], wstage[b][:], wsc[:, 2:3], None, Alu.mult),
                waits=[ev_tok])
            for ic in range(IC):
                pbk = 6 + (ic % 2)
                tr = top(lambda eng, ic=ic, pbk=pbk: eng.transpose(
                    pbh[pbk][:, 0:128], qwnat[:, 128 * ic:128 * ic + 128], ident[:]),
                    waits=[q_tok, ev_tok, id_ld])
                ev_tok = vop(lambda eng, ic=ic, hcx=hcx, pbk=pbk: eng.tensor_copy(
                    qwdT[:, ic, 128 * hcx:128 * hcx + 128], pbh[pbk][:, 0:128]),
                    waits=[tr])
        return ev_tok

    dn_ev = quant_weight_down()
    for hc in range(HC):
        dma("gpsimd", QWG[hc], qwgT[:, hc, :], waits=[gu_ev], sem="dg")
        dma("gpsimd", QWU[hc], quT[:, hc, :], waits=[gu_ev2], sem="dg")
    for ic in range(IC):
        dma("gpsimd", QWD[ic], qwdT[:, ic, :], waits=[dn_ev], sem="dg")

    ld_tok = [None, None]
    q_done = [None, None]
    sq_prev = None
    for t in range(16):
        b = 0
        ld_tok[b] = dma("sync", px[b][:],
                        x0d[512 * t:512 * t + 512, :].rearrange("(a p) h -> p a h", p=128),
                        waits=[q_done[b]], sem="dx")
        for sub in range(4):
            tt = 4 * t + sub
            vop(lambda eng, b=b, sub=sub: eng.tensor_reduce(
                pcol[:, 0:1], px[b][:, sub, :], AXX, Alu.add), waits=[ld_tok[b]])
            vdrain()
            vop(lambda eng, tt=tt: eng.tensor_scalar(
                m1[:, tt:tt + 1], pcol[:, 0:1], 1.0 / float(H), None, Alu.mult))
            vdrain()
            xc_t = vop(lambda eng, b=b, sub=sub, tt=tt: eng.tensor_scalar(
                ptmp[:], px[b][:, sub, :], m1[:, tt:tt + 1], None, Alu.subtract),
                waits=[sq_prev])
            sq_prev = sop(lambda eng: eng.activation(ptmp2[:], ptmp[:], Act.Square,
                                                     accum_out=pcol[:, 1:2]),
                          waits=[xc_t])
            vop(lambda eng: eng.tensor_scalar(
                pcol[:, 1:2], pcol[:, 1:2], 1.0 / float(H), 1e-5, Alu.mult, Alu.add),
                waits=[sq_prev])
            vdrain()
            vop(lambda eng: eng.reciprocal(pcol[:, 1:2], pcol[:, 1:2]))
            vdrain()
            r_t = sop(lambda eng, tt=tt: eng.activation(
                r1[:, tt:tt + 1], pcol[:, 1:2], Act.Sqrt), waits=[s.now("tV")])
            q_done[b] = vop(lambda eng, tt=tt: eng.reduce_max(
                xm1[:, tt:tt + 1], ptmp[:], AXX,
                apply_absolute_value=True), waits=[sq_prev])
    vdrain()
    vop(lambda eng: eng.tensor_tensor(xm1[:], xm1[:], r1[:], Alu.mult),
        waits=[s.now("tS")])
    vdrain()
    vop(lambda eng: eng.reduce_max(amax[0][:], xm1[:], AXX))
    vdrain()

    def allreduce_scalar(k, amax_tok, extra_cols):
        sig_col = 2 if k in (0, 2) else 3
        t1 = dma("gpsimd", RED[k][:, :], amax[k][:, 0:1], waits=[amax_tok], sem="dc")
        t2 = dma("gpsimd", red_row[0:1, :], RED[k].transpose([1, 0]), waits=[t1],
                 allow_slow_non_contiguous=True, sem="dc")
        mx = vop(lambda eng: eng.reduce_max(misc[0:1, 4:5], red_row[0:1, :], AXX),
                 waits=[t2])
        t3 = dma("gpsimd", CCI[k].ap()[0:1, 0:1], misc[0:1, 4:5], waits=[mx], sem="dc")
        cc_t = s.op("gpsimd", lambda eng, k=k: eng.collective_compute(
            "AllReduce", Alu.max, replica_groups=[list(range(NCORE))],
            ins=[CCI[k].ap().opt()], outs=[CCO[k].ap().opt()]),
            waits=[t3], inc="cc", n=1)
        t4 = dma("gpsimd", misc[0:1, 4:5], CCO[k].ap()[0:1, 0:1], waits=[cc_t], sem="dc")
        vop(lambda eng, sc=sig_col: eng.tensor_tensor(
            misc[0:1, 5:6], misc[0:1, 4:5], misc[0:1, sc:sc + 1], Alu.mult),
            waits=[t4, sig_tok])
        vdrain()
        vop(lambda eng: eng.tensor_scalar(
            misc[0:1, 6:7], misc[0:1, 5:6], 1.0 / 127.0, None, Alu.mult))
        vdrain()
        rsv = vop(lambda eng: eng.reciprocal(misc[0:1, 7:8], misc[0:1, 6:7]))
        shv = vop(lambda eng: eng.tensor_scalar(
            misc[0:1, 8:9], misc[0:1, 6:7], 0.5, None, Alu.mult))
        vdrain()
        toks = []
        for (col, tag) in extra_cols:
            srcc = {"s": 6, "rs": 7, "sh": 8}[tag]
            t5 = dma("gpsimd", SCAL[0:1, col:col + 1], misc[0:1, srcc:srcc + 1],
                     waits=[rsv, shv], sem="dc")
            toks.append(dma("gpsimd", scal_b[:, col:col + 1],
                            SCAL[0:1, col:col + 1].broadcast_to((128, 1)), waits=[t5],
                            sem="dc"))
        return toks

    bc1 = allreduce_scalar(0, s.now("tV"), [(C_S1, "s"), (C_RS1, "rs")])
    vop(lambda eng: eng.tensor_scalar(
        rq1[:], r1[:], scal_b[:, C_RS1:C_RS1 + 1], None, Alu.mult), waits=bc1)
    vop(lambda eng: eng.tensor_scalar(
        gscale[:], swg[:], scal_b[:, C_S1:C_S1 + 1], None, Alu.mult),
        waits=bc1 + [gu_ev])
    vop(lambda eng: eng.tensor_scalar(
        uscale[:], swu[:], scal_b[:, C_S1:C_S1 + 1], None, Alu.mult),
        waits=bc1 + [gu_ev2])
    s.full_barrier()

    def pass2(xsrc, mcache, rqcache, amax_idx):
        vop(lambda eng, k=amax_idx: eng.memset(amax[k][:], 0.0))
        ld_tok = [None, None]
        lastq = [None, None]
        silu_tok = [None, None]
        hp_tok = [None, None, None, None]
        hpdma_tok = [None, None, None, None]
        evq_tok = [None, None]
        tr_last = None
        amax_tok = None
        for t in range(16):
            b = t % 2
            ld_tok[b] = dma("sync", p2_x[b][:],
                            xsrc[512 * t:512 * t + 512, :].rearrange("(a p) h -> p a h",
                                                                     p=128),
                            waits=[lastq[b]], sem=f"dx{b}")
            for sub in range(4):
                tt = 4 * t + sub
                vop(lambda eng, b=b, sub=sub, tt=tt, mc=mcache, rc=rqcache:
                    eng.tensor_scalar(p2_t[:], p2_x[b][:, sub, :], mc[:, tt:tt + 1],
                                      rc[:, tt:tt + 1], Alu.subtract, Alu.mult),
                    waits=[ld_tok[b]] + ([tr_last] if sub == 0 else []))
                vop(lambda eng: eng.tensor_scalar(
                    p2_t[:], p2_t[:], MAGIC, MAGIC, Alu.add, Alu.subtract))
                lastq[b] = vop(lambda eng, sub=sub: eng.tensor_scalar(
                    p2_qx[:, sub, :], p2_t[:], 127.0, -128.0, Alu.min, Alu.max))
            for sub in range(4):
                for hc in range(HC):
                    j = (sub * HC + hc) % 2
                    pbk = 4 + j
                    tr = top(lambda eng, sub=sub, hc=hc, pbk=pbk: eng.transpose(
                        pbh[pbk][:, 0:128], p2_qx[:, sub, 128 * hc:128 * hc + 128],
                        ident[:]), waits=[lastq[b], evq_tok[j]])
                    evq_tok[j] = vop(lambda eng, sub=sub, hc=hc, pbk=pbk: eng.tensor_copy(
                        p2_qxT[:, hc, 128 * sub:128 * sub + 128], pbh[pbk][:, 0:128]),
                        waits=[tr])
            tr_ev = [evq_tok[0], evq_tok[1]]
            mm_g = None
            mm_u = None
            for ic in range(IC):
                pg = ic % 2
                pu = 2 + ic % 2
                for hc in range(HC):
                    mm_g = top(lambda eng, ic=ic, hc=hc, pg=pg: eng.matmul(
                        pb[pg][:], qwgT[:, hc, 128 * ic:128 * ic + 128],
                        p2_qxT[:, hc, :],
                        start=(hc == 0), stop=(hc == HC - 1)),
                        waits=(tr_ev + [silu_tok[ic % 2]]) if hc == 0 else [])
                for hc in range(HC):
                    mm_u = top(lambda eng, ic=ic, hc=hc, pu=pu: eng.matmul(
                        pb[pu][:], quT[:, hc, 128 * ic:128 * ic + 128],
                        p2_qxT[:, hc, :],
                        start=(hc == 0), stop=(hc == HC - 1)),
                        waits=[hp_tok[(ic + 2) % 4]] if hc == 0 else [])
                silu_tok[ic % 2] = sop(lambda eng, ic=ic, pg=pg: eng.activation(
                    p2_sg[ic % 2][:], pb[pg][:], Act.Silu, scale=gscale[:, ic:ic + 1]),
                    waits=[mm_g, hp_tok[(ic + 2) % 4]])
                hb = ic % 4
                hp_tok[hb] = vop(lambda eng, ic=ic, pu=pu, hb=hb:
                                 eng.scalar_tensor_tensor(
                                     p2_hp[hb][:], pb[pu][:], uscale[:, ic:ic + 1],
                                     p2_sg[ic % 2][:], Alu.mult, Alu.mult),
                                 waits=[mm_u, silu_tok[ic % 2], hpdma_tok[hb]])
                vop(lambda eng, hb=hb, ic=ic: eng.reduce_max(
                    p2_col[:, ic:ic + 1], p2_hp[hb][:], AXX,
                    apply_absolute_value=True))
                hpdma_tok[hb] = dma("gpsimd", HP[ic, :, 512 * t:512 * t + 512],
                                    p2_hp[hb][:], waits=[hp_tok[hb]], sem=f"dg{hb}")
            vdrain()
            vop(lambda eng: eng.reduce_max(p2_col[:, 22:23], p2_col[:, 0:IC], AXX))
            vdrain()
            amax_tok = vop(lambda eng, k=amax_idx: eng.tensor_tensor(
                amax[k][:], amax[k][:], p2_col[:, 22:23], Alu.max))
            vdrain()
            tr_last = mm_g
        return amax_tok

    amax2_tok = pass2(x0d, m1, rq1, 1)
    bc2 = allreduce_scalar(1, amax2_tok, [(C_RS2, "rs"), (C_S2H, "sh")])
    s.full_barrier()
    if debug:
        dma("gpsimd", DBG1[:, 0:1], amax[0][:], allow_slow_non_contiguous=True, sem="dg")
        dma("gpsimd", DBG1[:, 1:2], amax[1][:], allow_slow_non_contiguous=True, sem="dg")
        dma("gpsimd", DBG1[:, 16:32], scal_b[:], allow_slow_non_contiguous=True, sem="dg")
        dma("gpsimd", DBG1[:, 32:96], m1[:], allow_slow_non_contiguous=True, sem="dg")
        dma("gpsimd", DBG1[:, 96:160], r1[:], allow_slow_non_contiguous=True, sem="dg")
        dma("gpsimd", DBG1[:, 160:224], rq1[:], allow_slow_non_contiguous=True, sem="dg")
        dma("gpsimd", DBG1[:, 224:246], gscale[:], allow_slow_non_contiguous=True, sem="dg")
        dma("gpsimd", DBGM[:, :], misc[:], sem="dg")
        dma("gpsimd", DBGW[:, :], qwgT[:, 0, :], sem="dg")
        dma("gpsimd", DBGQ[:, :, :], p2_qxT[:], sem="dg")
        s.full_barrier()

    if stage("pass3a"):
        def pass3a(xres, dstd, shcol, rscol):
            ld_tok = [None] * 4
            qh_tok = [None] * 4
            mmh = [None] * 4
            mm_tok = None
            ev_toks = [None] * 4
            stq = None
            last_store = None
            for u in range(32):
                x0ld = dma("sync", p3_x0[0][:],
                           xres[256 * u:256 * u + 256, :].rearrange(
                               "(a p) h -> p a h", p=128),
                           waits=[stq], sem="dx")
                for ic in range(IC):
                    hb = ic % 4
                    ld_tok[hb] = dma("sync", p3_hp[hb][:],
                                     HP[ic, :, 256 * u:256 * u + 256],
                                     waits=[qh_tok[hb]], sem=f"dh{hb}")
                    vop(lambda eng, hb=hb, rs=rscol: eng.tensor_scalar(
                        p3_hp[hb][:], p3_hp[hb][:], scal_b[:, rs:rs + 1], MAGIC,
                        Alu.mult, Alu.add), waits=[ld_tok[hb]])
                    vop(lambda eng, hb=hb: eng.tensor_scalar(
                        p3_hp[hb][:], p3_hp[hb][:], MAGIC, 127.0,
                        Alu.subtract, Alu.min))
                    qh_tok[hb] = vop(lambda eng, hb=hb: eng.tensor_scalar(
                        p3_qh[hb][:], p3_hp[hb][:], -128.0, None, Alu.max),
                        waits=[mmh[hb]])
                    if debug and ic in (0, 1) and dstd is X1:
                        dma("gpsimd", QHD[u, ic], p3_qh[hb][:], waits=[qh_tok[hb]],
                            sem="dg")
                    for th in range(2):
                        for fh in range(2):
                            pbk = 2 * th + fh
                            mm_tok = top(lambda eng, ic=ic, th=th, fh=fh, hb=hb,
                                         pbk=pbk:
                                         eng.matmul(
                                             pb[pbk][:],
                                             p3_qh[hb][:, 128 * th:128 * th + 128],
                                             qwdT[:, ic, 512 * fh:512 * fh + 512],
                                             start=(ic == 0), stop=(ic == IC - 1)),
                                         waits=[qh_tok[hb]] +
                                         ([ev_toks[pbk]] if ic == 0 else []))
                    mmh[hb] = mm_tok
                for th in range(2):
                    for fh in range(2):
                        pbk = 2 * th + fh
                        ev_toks[pbk] = vop(
                            lambda eng, th=th, fh=fh, pbk=pbk, sh=shcol:
                            eng.scalar_tensor_tensor(
                                p3_x1[:, th, 512 * fh:512 * fh + 512],
                                pb[pbk][:], scal_b[:, sh:sh + 1],
                                p3_x0[0][:, th, 512 * fh:512 * fh + 512],
                                Alu.mult, Alu.add),
                            waits=[mm_tok, x0ld, stq])
                stq = dma("gpsimd",
                          dstd[256 * u:256 * u + 256, :].rearrange(
                              "(a p) h -> p a h", p=128),
                          p3_x1[:], waits=[ev_toks[3], ev_toks[0], ev_toks[1],
                                           ev_toks[2]], sem="dg")
                last_store = stq
            return last_store

        pass3a(x0d, X1, C_S2H, C_RS2)
        s.full_barrier()
        if debug:
            for t in range(16):
                dma("gpsimd", X1C[512 * t:512 * t + 512, :], X1[512 * t:512 * t + 512, :],
                    sem="dg")
            s.full_barrier()

    if stage("pass3b"):
        def pass3b(seq):
            base_t = SEQ * seq
            wtoks = []
            for hc in range(HC):
                wtoks.append(dma("sync", b_wq[:, hc, :], WQB[hc], sem="dw"))
                wtoks.append(dma("sync", b_wk[:, hc, :], WKB[hc], sem="dw"))
                wtoks.append(dma("sync", b_wv[:, hc, :], WVB[hc], sem="dw"))
            wq_ld = wtoks[-1]
            lastuse = None
            ev_xT = [None, None]
            qk_ev = [None, None]
            v_ev = [None, None]
            qs_dma = None
            sq_prev2 = None
            k_last = None
            v_last = None
            for u in range(16):
                ld_tok = dma("sync", b_x1[0][:],
                             X1[base_t + 256 * u: base_t + 256 * u + 256, :]
                             .rearrange("(a p) h -> p a h", p=128),
                             waits=[lastuse], sem="dx")
                for th in range(2):
                    vop(lambda eng, th=th: eng.tensor_reduce(
                        b_col[:, 0:1], b_x1[0][:, th, :], AXX, Alu.add),
                        waits=[ld_tok])
                    vdrain()
                    vop(lambda eng: eng.tensor_scalar(
                        b_col[:, 1:2], b_col[:, 0:1], 1.0 / float(H), None, Alu.mult))
                    vdrain()
                    xc_t = vop(lambda eng, th=th: eng.tensor_scalar(
                        b_pt[:], b_x1[0][:, th, :], b_col[:, 1:2], None, Alu.subtract),
                        waits=[sq_prev2, qs_dma])
                    sq_prev2 = sop(lambda eng: eng.activation(
                        b_ln[:], b_pt[:], Act.Square, accum_out=b_col[:, 2:3]),
                        waits=[xc_t, ev_xT[0], ev_xT[1]])
                    vop(lambda eng: eng.tensor_scalar(
                        b_col[:, 2:3], b_col[:, 2:3], 1.0 / float(H), 1e-5,
                        Alu.mult, Alu.add), waits=[sq_prev2])
                    vdrain()
                    vop(lambda eng: eng.reciprocal(b_col[:, 2:3], b_col[:, 2:3]))
                    vdrain()
                    r_t = sop(lambda eng: eng.activation(
                        b_col[:, 3:4], b_col[:, 2:3], Act.Sqrt), waits=[s.now("tV")])
                    lastuse = vop(lambda eng: eng.tensor_scalar(
                        b_ln[:], b_pt[:], b_col[:, 3:4], None, Alu.mult),
                        waits=[r_t])
                    for hc in range(HC):
                        j = hc % 2
                        pbk = 6 + j
                        tr = top(lambda eng, hc=hc, pbk=pbk: eng.transpose(
                            pbh[pbk][:, 0:128], b_ln[:, 128 * hc:128 * hc + 128],
                            ident[:]), waits=[lastuse, ev_xT[j]])
                        ev_xT[j] = vop(lambda eng, th=th, hc=hc, pbk=pbk:
                                       eng.tensor_copy(
                                           b_xT[:, hc, 128 * th:128 * th + 128],
                                           pbh[pbk][:, 0:128]), waits=[tr])
                exT = [ev_xT[0], ev_xT[1]]
                mm_qk = None
                for which in range(2):
                    wsrc = b_wq if which == 0 else b_wk
                    for fc in range(HC):
                        pbk = which
                        for hc in range(HC):
                            mm_qk = top(lambda eng, wsrc=wsrc, fc=fc, hc=hc, pbk=pbk:
                                        eng.matmul(
                                            pb[pbk][:, 0:256],
                                            wsrc[:, hc, 128 * fc:128 * fc + 128],
                                            b_xT[:, hc, :],
                                            start=(hc == 0), stop=(hc == HC - 1)),
                                        waits=(exT + [qk_ev[pbk], wq_ld])
                                        if hc == 0 else [])
                        if which == 0:
                            qk_ev[pbk] = sop(lambda eng, fc=fc, pbk=pbk: eng.activation(
                                b_qs[:, :, fc, :], pb[pbk][:, 0:256], Act.Identity,
                                bias=biasq8[:, fc:fc + 1], scale=0.125),
                                waits=[mm_qk] + ([qs_dma] if fc == 0 else []))
                        else:
                            qk_ev[pbk] = sop(lambda eng, fc=fc, pbk=pbk, u=u:
                                             eng.activation(
                                                 kT[:, fc, 256 * u:256 * u + 256],
                                                 pb[pbk][:, 0:256], Act.Identity,
                                                 bias=biask[:, fc:fc + 1]),
                                             waits=[mm_qk])
                            k_last = qk_ev[pbk]
                for j in range(2):
                    blk = 32 * seq + 2 * u + j
                    qs_dma = dma("gpsimd", QT[blk], b_qs[:, j], waits=[qk_ev[0]],
                                 sem="dg")
                mm_v = None
                for th in range(2):
                    for fh in range(2):
                        pbk = 2 + fh
                        for hc in range(HC):
                            mm_v = top(lambda eng, th=th, fh=fh, hc=hc, pbk=pbk:
                                       eng.matmul(
                                           pb[pbk][:],
                                           b_xT[:, hc, 128 * th:128 * th + 128],
                                           b_wv[:, hc, 512 * fh:512 * fh + 512],
                                           start=(hc == 0), stop=(hc == HC - 1)),
                                       waits=(exT + [v_ev[fh], wq_ld])
                                       if hc == 0 else [])
                        ch = 2 * u + th
                        v_ev[fh] = vop(lambda eng, ch=ch, fh=fh, pbk=pbk:
                                       eng.tensor_tensor(
                                           Vt[:, ch, 512 * fh:512 * fh + 512],
                                           pb[pbk][:], bvx[:, 512 * fh:512 * fh + 512],
                                           Alu.add), waits=[mm_v, bvx_ld])
                        v_last = v_ev[fh]
            return k_last, v_last

        kv_a = pass3b(0)
        if debug:
            for fc in range(HC):
                dma("gpsimd", KDd[:, fc, :], kT[:, fc, :],
                    waits=[kv_a[0], kv_a[1]], sem="dg")
            for ch in range(NCHUNK):
                dma("gpsimd", VDd[:, ch, :], Vt[:, ch, :],
                    waits=[kv_a[0], kv_a[1]], sem="dg")
        s.full_barrier()

    if stage("attn"):

        def attention(seq, kv):
            base_t = SEQ * seq
            e1 = dma("gpsimd", a_VtE[0:64, :], Vt[0:64, 0, :], waits=list(kv),
                     sem="de")
            e2 = dma("gpsimd", a_VtE[64:128, :], Vt[64:128, NCHUNK - 1, :],
                     waits=list(kv), sem="de")
            e3 = vop(lambda eng: eng.tensor_copy(a_kTe[:, :, 0:64], kT[:, :, 0:64]),
                     waits=list(kv))
            e4 = vop(lambda eng: eng.tensor_copy(a_kTe[:, :, 64:128],
                                                 kT[:, :, SEQ - 64:SEQ]))

            st = {
                "qld": [None, None],
                "qt_use": [None, None],
                "s_ev": [None, None, None, None],
                "tr_ev": [None, None],
                "A_use": [None, None, None, None],
                "ctx_ev": [None, None],
                "resid_x1": None, "resid_st": None, "resid_xc": None,
                "resid_sq": None, "resid_rm": None,
            }

            def load_qt(tt):
                b = tt % 2
                qld_t = dma("sync", a_qt[b][:], QT[32 * seq + tt],
                            waits=[st["qt_use"][b], st["qld"][b]], sem=f"dq{b}")
                st["qld"][b] = qld_t
                return b, qld_t

            def do_mid(tt, ctx_dst, gate_dst):
                b, qld_t = load_qt(tt)
                vc = tt - 1
                segs = [("edge", None), ("win", vc), ("win", vc + 1), ("win", vc + 2)]
                mask_list = [(0, 64, 128, 192), (0, 64, 384, 512),
                             (64, 128, 128, 256), (64, 128, 448, 512)]
                SBANK = (0, 1, 6, 7)
                out_toks = []

                def stage1(j):
                    hp, hh = divmod(j, 2)
                    p0 = 64 * hh
                    sl = j % 4
                    pS = SBANK[sl]
                    top(lambda eng, p0=p0, hp=hp, pS=pS, b=b:
                        eng.matmul(pb[pS][:, 0:128],
                                   a_qt[b][p0:p0 + 64, hp, :],
                                   a_kTe[p0:p0 + 64, hp, :],
                                   start=True, stop=False),
                        waits=[qld_t, st["s_ev"][sl], kv[0], kv[1], e4])
                    k0 = 128 * vc
                    mmt = top(lambda eng, p0=p0, hp=hp, pS=pS, b=b, k0=k0:
                              eng.matmul(pb[pS][:, 128:512],
                                         a_qt[b][p0:p0 + 64, hp, :],
                                         kT[p0:p0 + 64, hp, k0:k0 + 384],
                                         start=False, stop=True,
                                         skip_group_check=True))
                    st["qt_use"][b] = mmt
                    mt = None
                    for (r0, r1_, c0, c1) in mask_list:
                        mt = vop(lambda eng, pS=pS, r0=r0, r1_=r1_, c0=c0, c1=c1:
                                 eng.memset(pb[pS][r0:r1_, c0:c1], -1e30),
                                 waits=[mmt])
                    nm = vop(lambda eng, pS=pS, sl=sl: eng.reduce_max(
                        a_col[:, sl:sl + 1], pb[pS][:], AXX, negate=True),
                        waits=[mt])
                    ex = sop(lambda eng, pS=pS, sl=sl: eng.activation(
                        a_A[sl][:], pb[pS][:], Act.Exp,
                        bias=a_col[:, sl:sl + 1],
                        accum_out=a_col[:, 4 + sl:5 + sl]),
                        waits=[nm, st["A_use"][sl]])
                    st["s_ev"][sl] = ex
                    return ex

                def stage2(j, ex):
                    hp, hh = divmod(j, 2)
                    h = 2 * hp + hh
                    sl = j % 4
                    rc = vop(lambda eng, sl=sl: eng.reciprocal(
                        a_col[:, 8 + sl:9 + sl], a_col[:, 4 + sl:5 + sl]),
                        waits=[ex, st["ctx_ev"][hh]])
                    cmm = None
                    for ci in range(4):
                        pT = 2 + (ci % 2)
                        tr = top(lambda eng, sl=sl, ci=ci, pT=pT: eng.transpose(
                            pbh[pT][:, 0:128],
                            a_A[sl][:, 128 * ci:128 * ci + 128], ident[:]),
                            waits=[ex, st["tr_ev"][ci % 2]])
                        st["A_use"][sl] = tr
                        tre = vop(lambda eng, hh=hh, ci=ci, pT=pT:
                                  eng.tensor_copy(
                                      a_aT[hh][:, 128 * ci:128 * ci + 128],
                                      pbh[pT][:, 0:128]), waits=[tr])
                        st["tr_ev"][ci % 2] = tre
                        styp, c = segs[ci]
                        pC = 4 + hh
                        extra = [st["ctx_ev"][hh]] if ci == 0 else []
                        if ci == 0 and hh == 1 and st.get("sev2"):
                            extra.append(st.pop("sev2"))
                        if styp == "edge":
                            cmm = top(lambda eng, hh=hh, ci=ci, h=h, pC=pC:
                                      eng.matmul(
                                          pb[pC][:, 0:64],
                                          a_aT[hh][:, 128 * ci:128 * ci + 128],
                                          a_VtE[:, 64 * h:64 * h + 64],
                                          start=(ci == 0), stop=(ci == 3)),
                                      waits=[tre, e2] + extra)
                        else:
                            cmm = top(lambda eng, hh=hh, ci=ci, h=h, c=c, pC=pC:
                                      eng.matmul(
                                          pb[pC][:, 0:64],
                                          a_aT[hh][:, 128 * ci:128 * ci + 128],
                                          Vt[:, c, 64 * h:64 * h + 64],
                                          start=(ci == 0), stop=(ci == 3)),
                                      waits=[tre] + extra)
                    cev = sop(lambda eng, h=h, sl=sl, pC=4 + hh, dst=ctx_dst:
                              eng.activation(dst[:, 64 * h:64 * h + 64],
                                             pb[pC][:, 0:64], Act.Identity,
                                             scale=a_col[:, 8 + sl:9 + sl]),
                              waits=[cmm, rc, gate_dst])
                    st["ctx_ev"][hh] = cev
                    out_toks.append(cev)

                exs = {}
                for j in range(4):
                    exs[j] = stage1(j)
                for j in range(4, 16):
                    stage2(j - 4, exs[j - 4])
                    exs[j] = stage1(j)
                for j in range(12, 16):
                    stage2(j, exs[j])
                return out_toks

            def do_full(tt, ctx_dst, gate_dst):
                b, qld_t = load_qt(tt)
                if tt == 0:
                    mrow, mcol = (64, 128), (192, 4032)
                else:
                    mrow, mcol = (0, 64), (64, 3904)
                out_toks = []
                sev = [None, None, None]
                SB = (0, 1, 5)
                exF = [None, None]
                rcF = [None, None]
                trlast = [None]

                def s1F(h):
                    hp, hh = divmod(h, 2)
                    p0 = 64 * hh
                    sb = h % 2
                    aS = a_Sd[sb]
                    mmt = None
                    for kc in range(8):
                        pS = SB[kc % 3]
                        mmt = top(lambda eng, hp=hp, p0=p0, kc=kc, pS=pS, b=b:
                                  eng.matmul(
                                      pb[pS][:], a_qt[b][p0:p0 + 64, hp, :],
                                      kT[p0:p0 + 64, hp, 512 * kc:512 * kc + 512],
                                      start=True, stop=True),
                                  waits=[qld_t, sev[kc % 3],
                                         st["ctx_ev"][0], st["ctx_ev"][1],
                                         st["s_ev"][0], st["s_ev"][1]]
                                  if kc < 3 else [sev[kc % 3]])
                        sev[kc % 3] = vop(lambda eng, kc=kc, pS=pS, aS=aS:
                                          eng.tensor_copy(
                                              aS[:, 512 * kc:512 * kc + 512],
                                              pb[pS][:]),
                                          waits=[mmt] +
                                          ([exF[sb]] if kc == 0 else []))
                    st["qt_use"][b] = mmt
                    vop(lambda eng, aS=aS: eng.memset(
                        aS[mrow[0]:mrow[1], mcol[0]:mcol[1]], -1e30))
                    nm = vop(lambda eng, aS=aS, sb=sb: eng.reduce_max(
                        a_col[:, 0 + sb:1 + sb], aS[:], AXX, negate=True),
                        waits=[exF[sb]])
                    return nm

                def s2F(h, nm):
                    hp, hh = divmod(h, 2)
                    sb = h % 2
                    aS = a_Sd[sb]
                    ex = sop(lambda eng, aS=aS, sb=sb: eng.activation(
                        a_AF[:], aS[:], Act.Exp, bias=a_col[:, 0 + sb:1 + sb],
                        accum_out=a_col[:, 4 + sb:5 + sb]),
                        waits=[nm, trlast[0], rcF[sb], st["resid_rm"]])
                    exF[sb] = ex
                    rc = vop(lambda eng, sb=sb: eng.reciprocal(
                        a_col[:, 8 + sb:9 + sb], a_col[:, 4 + sb:5 + sb]),
                        waits=[ex, st["ctx_ev"][0]])
                    rcF[sb] = rc
                    cmm = None
                    for ci in range(NCHUNK):
                        j = ci % 2
                        pT = 2 + j
                        tr = top(lambda eng, ci=ci, pT=pT: eng.transpose(
                            pbh[pT][:, 0:128], a_AF[:, 128 * ci:128 * ci + 128],
                            ident[:]), waits=[ex, st["tr_ev"][j]])
                        trlast[0] = tr
                        tre = vop(lambda eng, j=j, pT=pT: eng.tensor_copy(
                            a_aT[j][:, 0:128], pbh[pT][:, 0:128]), waits=[tr])
                        st["tr_ev"][j] = tre
                        cmm = top(lambda eng, ci=ci, h=h, j=j: eng.matmul(
                            pb[4][:, 0:64], a_aT[j][:, 0:128],
                            Vt[:, ci, 64 * h:64 * h + 64],
                            start=(ci == 0), stop=(ci == NCHUNK - 1)),
                            waits=[tre] + ([st["ctx_ev"][0]] if ci == 0 else []))
                    cev = sop(lambda eng, h=h, sb=sb, dst=ctx_dst: eng.activation(
                        dst[:, 64 * h:64 * h + 64], pb[4][:, 0:64], Act.Identity,
                        scale=a_col[:, 8 + sb:9 + sb]), waits=[cmm, rc, gate_dst])
                    st["ctx_ev"][0] = cev
                    st["ctx_ev"][1] = cev
                    out_toks.append(cev)

                nms = {}
                nms[0] = s1F(0)
                for h in range(1, NH):
                    nms[h] = s1F(h)
                    s2F(h - 1, nms[h - 1])
                s2F(NH - 1, nms[NH - 1])
                st["s_ev"][0] = sev[0]
                st["s_ev"][1] = sev[1]
                st["sev2"] = sev[2]
                return out_toks

            def resid(tt, ctx_toks):
                xt = dma("sync", a_x1[:],
                         X1[base_t + 128 * tt:base_t + 128 * tt + 128, :],
                         waits=[st["resid_x1"]], sem="dr")
                at_ = gop(lambda eng: eng.tensor_tensor(a_x2[:], a_x1[:],
                                                        a_ctx[:], Alu.add),
                          waits=[xt, st["resid_st"]] + ctx_toks)
                st["resid_x1"] = at_
                ttg = NT128S * seq + tt
                stx = dma("gpsimd",
                          X2[base_t + 128 * tt:base_t + 128 * tt + 128, :],
                          a_x2[:], sem="dg")
                st["resid_st"] = stx
                sm = sop(lambda eng: eng.activation(
                    a_ctx[:], a_x2[:], Act.Identity, accum_out=g_col[:, 0:1]),
                    waits=[at_, st["resid_sq"]])
                sdrain()
                mn = sop(lambda eng, ttg=ttg: eng.activation(
                    m3[:, ttg:ttg + 1], g_col[:, 0:1], Act.Identity,
                    scale=1.0 / float(H)))
                xc_t = gop(lambda eng, ttg=ttg: eng.tensor_scalar(
                    a_xc[:], a_x2[:], m3[:, ttg:ttg + 1], None, Alu.subtract),
                    waits=[mn, st["resid_rm"]])
                sq_t = sop(lambda eng, ttg=ttg: eng.activation(
                    a_ctx[:], a_xc[:], Act.Square, accum_out=ssq3[:, ttg:ttg + 1]),
                    waits=[xc_t])
                st["resid_sq"] = sq_t
                rm_t = vop(lambda eng, ttg=ttg: eng.reduce_max(
                    xm3[:, ttg:ttg + 1], a_xc[:], AXX, apply_absolute_value=True),
                    waits=[xc_t])
                st["resid_rm"] = rm_t
                st["resid_xc"] = sq_t
                return sq_t, stx

            free_tok = None
            laststx = None
            for tt in range(NT128S):
                if tt in (0, NT128S - 1):
                    toks = do_full(tt, a_ctx, free_tok)
                else:
                    toks = do_mid(tt, a_ctx, free_tok)
                free_tok, laststx = resid(tt, toks)
            return laststx

        attention(0, kv_a)
        s.full_barrier()
        kv_b = pass3b(1)
        s.full_barrier()
        attention(1, kv_b)
        vop(lambda eng: eng.tensor_scalar(
            r3[:], ssq3[:], 1.0 / float(H), 1e-5, Alu.mult, Alu.add),
            waits=[s.now("tG"), s.now("tS")])
        vdrain()
        vop(lambda eng: eng.reciprocal(r3[:], r3[:]))
        vdrain()
        r3_t = sop(lambda eng: eng.activation(r3[:], r3[:], Act.Sqrt),
                   waits=[s.now("tV")])
        vop(lambda eng: eng.tensor_tensor(xm3[:], xm3[:], r3[:], Alu.mult),
            waits=[r3_t])
        vdrain()
        vop(lambda eng: eng.reduce_max(amax[2][:], xm3[:], AXX))
        vdrain()

    if stage("all"):
        s.full_barrier()
        bc3 = allreduce_scalar(2, s.now("tV"), [(C_S3, "s"), (C_RS3, "rs")])
        vop(lambda eng: eng.tensor_scalar(
            rq3[:], r3[:], scal_b[:, C_RS3:C_RS3 + 1], None, Alu.mult), waits=bc3)
        vop(lambda eng: eng.tensor_scalar(
            gscale[:], swg[:], scal_b[:, C_S3:C_S3 + 1], None, Alu.mult), waits=bc3)
        vop(lambda eng: eng.tensor_scalar(
            uscale[:], swu[:], scal_b[:, C_S3:C_S3 + 1], None, Alu.mult), waits=bc3)
        for hc in range(HC):
            dma("sync", qwgT[:, hc, :], QWG[hc], sem="dw")
            dma("gpsimd", quT[:, hc, :], QWU[hc], sem="dw")
        for ic in range(IC):
            dma("scalar", qwdT[:, ic, :], QWD[ic], sem="dw")
        s.full_barrier()
        amax4_tok = pass2(X2, m3, rq3, 3)
        bc4 = allreduce_scalar(3, amax4_tok, [(C_RS4, "rs"), (C_S4H, "sh")])
        s.full_barrier()
        pass3a(X2, outd, C_S4H, C_RS4)
        s.op("gpsimd", lambda eng: None,
             waits=[s.now(x) for x in Sched.DMA_SEMS])
        s.op("sync", lambda eng: None,
             waits=[s.now(x) for x in Sched.DMA_SEMS])
    else:
        qs = [s.now(q) for q in Sched.DMA_SEMS]
        for e in ENGS:
            s.op(e, lambda eng: None,
                 waits=qs + [s.now("tV"), s.now("tS"), s.now("tP")])

    import contextlib
    es = contextlib.ExitStack()
    with es:
        sems = {}
        for name in ("qS", "qG", "qC", "qV", "dx", "dx0", "dx1", "dh0", "dh1",
                     "dh2", "dh3", "dw", "dk", "dq0", "dq1", "dr", "de", "dc", "dg",
                     "dg0", "dg1", "dg2", "dg3", "tP", "tV", "tS", "tG", "cc", "bar"):
            sems[name] = es.enter_context(nc.semaphore(name))
        s.bar_handle = sems["bar"]
        with nc.Block() as block:
            def mk(ops):
                def body(eng):
                    for fn, ws, inc in ops:
                        wmax = {}
                        for (sname, v) in ws:
                            wmax[sname] = max(wmax.get(sname, 0), v)
                        for idx, (sname, v) in enumerate(wmax.items()):
                            if idx and idx % 2 == 0:
                                eng.nop(nofuse=True)
                            eng.wait_ge(sems[sname], v)
                        if wmax and eng.engine == mybir.EngineType.PE:
                            eng.nop(nofuse=True)
                        inst = fn(eng)
                        if inc is not None:
                            sem, n = inc
                            inst.then_inc(sems[sem], n)
                return body
            block.sync(mk(s.prog["sync"]))
            block.vector(mk(s.prog["vector"]))
            block.scalar(mk(s.prog["scalar"]))
            block.tensor(mk(s.prog["tensor"]))
            block.gpsimd(mk(s.prog["gpsimd"]))
    return nc


def _shard_inputs(inputs):
    x = np.ascontiguousarray(
        np.asarray(inputs["L0_attn_hidden_states"], dtype=np.float32).reshape(-1, H))
    bq = np.asarray(inputs["L0_attn_bq"], dtype=np.float32)
    bk = np.asarray(inputs["L0_attn_bk"], dtype=np.float32)
    bv = np.asarray(inputs["L0_attn_bv"], dtype=np.float32)
    com = {
        "wq_t": np.ascontiguousarray(np.asarray(inputs["L0_attn_Wq"], dtype=np.float32).T),
        "wk_t": np.ascontiguousarray(np.asarray(inputs["L0_attn_Wk"], dtype=np.float32).T),
        "wv_t": np.ascontiguousarray(np.asarray(inputs["L0_attn_Wv"], dtype=np.float32).T),
        "bq2": np.ascontiguousarray(bq.reshape(8, 128).T),
        "bk2": np.ascontiguousarray(bk.reshape(8, 128).T),
        "bv_row": np.ascontiguousarray(bv.reshape(1, H)),
        "w_gate": np.ascontiguousarray(np.asarray(inputs["L0_ffn_w_gate"], dtype=np.float32)),
        "w_up": np.ascontiguousarray(np.asarray(inputs["L0_ffn_w_up"], dtype=np.float32)),
        "w_down": np.ascontiguousarray(np.asarray(inputs["L0_ffn_w_down"], dtype=np.float32)),
        "clip2": np.array([[float(np.asarray(inputs["L0_ffn_clip_x"]).reshape(-1)[0]),
                            float(np.asarray(inputs["L0_ffn_clip_down"]).reshape(-1)[0])]],
                          dtype=np.float32),
    }
    in_maps = []
    for c in range(NCORE):
        m = dict(com)
        m["x0"] = np.ascontiguousarray(x[c * T:(c + 1) * T])
        in_maps.append(m)
    return in_maps


def kernel(**inputs):
    in_maps = _shard_inputs(inputs)
    nc = build_nc(debug=False, upto="all")
    res = run_bass_kernel_spmd(nc, in_maps, core_ids=list(range(NCORE)))
    out = np.concatenate([np.asarray(r["out"], dtype=np.float32) for r in res.results],
                         axis=0)
    return out.reshape(16, 4096, 1024)


# version 05
# speedup vs baseline: 1.0308x; 1.0308x over previous
import numpy as np
import ml_dtypes
import concourse.bass as bass
import concourse.mybir as mybir
from concourse.bass_utils import run_bass_kernel_spmd

F32 = mybir.dt.float32
BF16 = mybir.dt.bfloat16
Alu = mybir.AluOpType
Act = mybir.ActivationFunctionType
AXX = mybir.AxisListType.X
MAGIC = float(1.5 * 2.0 ** 23)

T = 8192
H = 1024
II = 2816
IC = II // 128
HC = H // 128
SEQ = 4096
NH = 16
NCHUNK = SEQ // 128
NT128S = SEQ // 128
NCORE = 8

ENGS = ("sync", "vector", "scalar", "tensor", "gpsimd")
QSEM = {"sync": "qS", "gpsimd": "qG", "scalar": "qC", "vector": "qV"}
C_S1, C_RS1, C_S2, C_RS2, C_S2H, C_S3, C_RS3, C_S4, C_RS4, C_S4H = range(10)


class Sched:
    def __init__(self):
        self.prog = {e: [] for e in ENGS}
        self.cnt = {}
        self.bar_handle = None

    def op(self, eng, fn, waits=(), inc=None, n=1):
        ws = [w for w in waits if w is not None]
        tok = None
        if inc is not None:
            self.cnt[inc] = self.cnt.get(inc, 0) + n
            tok = (inc, self.cnt[inc])
        self.prog[eng].append((fn, ws, (inc, n) if inc else None))
        return tok

    def now(self, sem):
        return (sem, self.cnt.get(sem, 0))

    DMA_SEMS = ("qS", "qG", "qC", "qV", "dx", "dx0", "dx1", "dh0", "dh1",
                "dh2", "dh3", "dw", "dk", "dq0", "dq1", "dr", "de", "dc", "dg",
                "dg0", "dg1", "dg2", "dg3")

    def full_barrier(self):
        qs = [self.now(q) for q in self.DMA_SEMS]
        for e in ENGS:
            w = qs if e == "gpsimd" else []
            self.op(e, lambda eng: eng.nop(nofuse=True),
                    waits=w, inc="bar", n=1)
        val = self.cnt["bar"]
        for e in ENGS:
            self.op(e, lambda eng: None, waits=[("bar", val)])


def build_nc(debug=False, upto="all"):
    nc = bass.Bass("TRN2", target_bir_lowering=False)
    STAGES = ["pass2", "pass3a", "pass3b", "attn", "all"]
    SI = STAGES.index(upto)

    def stage(name):
        return SI >= STAGES.index(name)

    ext_dbg = "ExternalOutput" if debug else "Internal"

    x0d = nc.dram_tensor("x0", [T, H], F32, kind="ExternalInput").ap()
    outd = nc.dram_tensor("out", [T, H], F32, kind="ExternalOutput").ap()
    wqtd = nc.dram_tensor("wq_t", [H, H], F32, kind="ExternalInput").ap()
    wktd = nc.dram_tensor("wk_t", [H, H], F32, kind="ExternalInput").ap()
    wvtd = nc.dram_tensor("wv_t", [H, H], F32, kind="ExternalInput").ap()
    bq2d = nc.dram_tensor("bq2", [128, 8], F32, kind="ExternalInput").ap()
    bk2d = nc.dram_tensor("bk2", [128, 8], F32, kind="ExternalInput").ap()
    bvrd = nc.dram_tensor("bv_row", [1, H], F32, kind="ExternalInput").ap()
    wgd = nc.dram_tensor("w_gate", [II, H], F32, kind="ExternalInput").ap()
    wud = nc.dram_tensor("w_up", [II, H], F32, kind="ExternalInput").ap()
    wdd = nc.dram_tensor("w_down", [H, II], F32, kind="ExternalInput").ap()
    clipd = nc.dram_tensor("clip2", [1, 2], F32, kind="ExternalInput").ap()

    X1 = nc.dram_tensor("X1", [T, H], F32, kind=ext_dbg).ap()
    X2 = nc.dram_tensor("X2", [T, H], F32, kind=ext_dbg).ap()
    HP = nc.dram_tensor("HP", [IC, 128, T], F32, kind=ext_dbg).ap()
    QT = nc.dram_tensor("QTsp", [T // 128, 128, HC, 128], BF16, kind=ext_dbg).ap()
    QWG = nc.dram_tensor("QWG", [HC, 128, II], BF16).ap()
    QWU = nc.dram_tensor("QWU", [HC, 128, II], BF16).ap()
    QWD = nc.dram_tensor("QWD", [IC, 128, H], BF16).ap()
    WQB = nc.dram_tensor("WQB", [HC, 128, H], BF16).ap()
    WKB = nc.dram_tensor("WKB", [HC, 128, H], BF16).ap()
    WVB = nc.dram_tensor("WVB", [HC, 128, H], BF16).ap()
    SCAL = nc.dram_tensor("SCAL", [1, 16], F32).ap()
    RED = [nc.dram_tensor(f"RED{k}", [128, 1], F32).ap() for k in range(4)]
    CCI = [nc.dram_tensor(f"cci{k}", [1, 16], F32) for k in range(4)]
    CCO = [nc.dram_tensor(f"cco{k}", [1, 16], F32) for k in range(4)]
    if debug:
        KDd = nc.dram_tensor("KD", [128, HC, SEQ], BF16, kind="ExternalOutput").ap()
        VDd = nc.dram_tensor("VD", [128, NCHUNK, H], BF16, kind="ExternalOutput").ap()
        DBG1 = nc.dram_tensor("DBG1", [128, 256], F32, kind="ExternalOutput").ap()
        DBGM = nc.dram_tensor("DBGM", [1, 16], F32, kind="ExternalOutput").ap()
        DBGW = nc.dram_tensor("DBGW", [128, 2816], BF16, kind="ExternalOutput").ap()
        DBGQ = nc.dram_tensor("DBGQ", [128, HC, 512], BF16, kind="ExternalOutput").ap()
        X1C = nc.dram_tensor("X1C", [T, H], F32, kind="ExternalOutput").ap()
        QHD = nc.dram_tensor("QHD", [32, 2, 128, 256], BF16, kind="ExternalOutput").ap()

    identd = nc.inline_tensor(np.eye(128, dtype=ml_dtypes.bfloat16), name="identc")

    ARENA = 212832
    nc.alloc_sbuf_tensor("bigarena", [128, ARENA], mybir.dt.uint8)
    base = nc.lookup_mloc("bigarena").addr
    LIMIT = base + ARENA
    cur = [base]

    def at(name, shape, dtype, off=None):
        nbytes = int(np.prod(shape[1:])) * mybir.dt.size(dtype)
        if off is None:
            o = cur[0]
            cur[0] = o + ((nbytes + 31) // 32) * 32
        else:
            o = base + off
        assert o + nbytes <= LIMIT, f"SBUF overflow {name}: {o + nbytes - base}"
        return nc.alloc_sbuf_tensor_at(name, shape, dtype, offset=o)

    ident = at("ident", [128, 128], BF16)
    biasq = at("biasq", [128, 8], F32)
    biask = at("biask", [128, 8], F32)
    bvx = at("bvx", [128, H], F32)
    scal_b = at("scal_b", [128, 16], F32)
    m1 = at("m1", [128, 64], F32)
    r1 = at("r1", [128, 64], F32)
    rq1 = at("rq1", [128, 64], F32)
    m3 = at("m3", [128, 64], F32)
    r3 = at("r3", [128, 64], F32)
    rq3 = at("rq3", [128, 64], F32)
    swg = at("swg", [128, 22], F32)
    swu = at("swu", [128, 22], F32)
    gscale = at("gscale", [128, 22], F32)
    uscale = at("uscale", [128, 22], F32)
    amax = [at(f"amax{k}", [128, 1], F32) for k in range(4)]
    xm1 = at("xm1", [128, 64], F32)
    xm3 = at("xm3", [128, 64], F32)
    biasq8 = at("biasq8", [128, 8], F32)
    red_row = at("red_row", [1, 128], F32)
    misc = at("misc", [1, 16], F32)
    n30 = at("n30", [128, 1], F32)
    ssq3 = at("ssq3", [128, 64], F32)
    GLOB_END = cur[0] - base
    assert GLOB_END < 9500, GLOB_END

    WREG = ((GLOB_END + 63) // 64) * 64
    qwgT = at("qwgT", [128, HC, II], BF16, off=WREG)
    quT = at("quT", [128, HC, II], BF16, off=WREG + 45056)
    qwdT = at("qwdT", [128, IC, H], BF16, off=WREG + 90112)
    kT = at("kT", [128, HC, SEQ], BF16, off=WREG)
    Vt = at("Vt", [128, NCHUNK, H], BF16, off=WREG + 65536)
    WORK = WREG + 135168

    def wat(name, shape, dtype, off):
        return at(name, shape, dtype, off=WORK + off)

    wstage = [wat(f"wstage{i}", [128, II], F32, 11264 * i) for i in range(2)]
    qwnat = wat("qwnat", [128, II], BF16, 22528)
    qkvb16 = wat("qkvb16", [128, H], BF16, 28160)
    wsc = wat("wsc", [128, 8], F32, 30208)
    px = [wat("px0", [128, 4, H], F32, 30976)]
    ptmp = wat("ptmp", [128, H], F32, 47360)
    ptmp2 = wat("ptmp2", [128, H], F32, 51456)
    pcol = wat("pcol", [128, 4], F32, 55552)

    p2_x = [wat(f"p2x{i}", [128, 4, H], F32, 16384 * i) for i in range(2)]
    p2_qx = wat("p2qx", [128, 4, H], BF16, 32768)
    p2_qxT = wat("p2qxT", [128, HC, 512], BF16, 40960)
    p2_t = wat("p2t", [128, H], F32, 49152)
    p2_sg = [wat(f"p2sg{i}", [128, 512], F32, 53248 + 2048 * i) for i in range(2)]
    p2_hp = [wat(f"p2hp{i}", [128, 512], F32, 57344 + 2048 * i) for i in range(4)]
    p2_col = wat("p2col", [128, 24], F32, 65536)

    p3_hp = [wat(f"p3hp{i}", [128, 256], F32, 1024 * i) for i in range(4)]
    p3_qh = [wat(f"p3qh{i}", [128, 256], BF16, 4096 + 512 * i) for i in range(4)]
    p3_x0 = [wat("p3x0", [128, 2, H], F32, 6144)]
    p3_x1 = wat("p3x1", [128, 2, H], F32, 14336)

    b_wq = wat("bwq", [128, HC, H], BF16, 0)
    b_wk = wat("bwk", [128, HC, H], BF16, 16384)
    b_wv = wat("bwv", [128, HC, H], BF16, 32768)
    b_x1 = [wat("bx1", [128, 2, H], F32, 49152)]
    b_ln = wat("bln", [128, H], BF16, 57344)
    b_xT = wat("bxT", [128, HC, 256], BF16, 59392)
    b_col = wat("bcol", [128, 8], F32, 63488)
    b_pt = wat("bpt", [128, H], F32, 63520)
    b_qs = wat("bqs", [128, 2, HC, 128], BF16, 63520)

    a_kTe = wat("akTe", [128, HC, 128], BF16, 0)
    a_VtE = wat("aVtE", [128, H], BF16, 2048)
    a_qt = [wat(f"aqt{i}", [128, HC, 128], BF16, 4096 + 2048 * i) for i in range(2)]
    a_A = [wat(f"aA{i}", [128, 512], BF16, 8192 + 1024 * i) for i in range(4)]
    a_aT = [wat(f"aaT{i}", [128, 512], BF16, 12288 + 1024 * i) for i in range(2)]
    a_ctx = wat("actx", [128, H], F32, 14336)
    a_x1 = wat("ax1", [128, H], F32, 18432)
    a_x2 = wat("ax2", [128, H], F32, 22528)
    a_Sd = [wat(f"aS{i}", [128, SEQ], F32, 26624 + 16384 * i) for i in range(2)]
    a_AF = wat("aAF", [128, SEQ], BF16, 59392)
    a_xc = wat("axc", [128, H], F32, 59392)
    a_col = wat("acol", [128, 16], F32, 67584)
    g_col = wat("gcol", [128, 4], F32, 67648)

    pb = [nc.alloc_psum_tensor(f"pb{i}", [128, 512], F32) for i in range(8)]
    pbh = [p.ap().bitcast(BF16) for p in pb]

    s = Sched()

    def dma(q, out, in_, waits=(), sem=None, **kw):
        return s.op(q, lambda eng, o=out, i=in_, kw=kw: eng.dma_start(out=o, in_=i, **kw),
                    waits=waits, inc=(sem or QSEM[q]), n=16)

    def vop(fn, waits=()):
        return s.op("vector", fn, waits=waits, inc="tV")

    def gop(fn, waits=()):
        return s.op("gpsimd", fn, waits=waits, inc="tG")

    def gdrain():
        s.op("gpsimd", lambda eng: eng.drain())

    def sop(fn, waits=()):
        return s.op("scalar", fn, waits=waits, inc="tS")

    def top(fn, waits=()):
        return s.op("tensor", fn, waits=waits, inc="tP")

    def vdrain():
        s.op("vector", lambda eng: eng.drain())

    def sdrain():
        s.op("scalar", lambda eng: eng.drain())

    cast_tok = None
    spill_tok = None
    for wi, (src, dst) in enumerate(((wqtd, WQB), (wktd, WKB), (wvtd, WVB))):
        for hc in range(HC):
            b = hc % 2
            ldt = dma("sync", wstage[b][:, 0:H], src[128 * hc:128 * hc + 128, :],
                      waits=[cast_tok], sem="dw")
            cast_tok = vop(lambda eng, b=b: eng.tensor_copy(qkvb16[:], wstage[b][:, 0:H]),
                           waits=[ldt, spill_tok])
            spill_tok = dma("gpsimd", dst[hc], qkvb16[:], waits=[cast_tok], sem="dg")
    dma("sync", biasq[:], bq2d[:, :], sem="dk")
    dma("sync", biask[:], bk2d[:, :], sem="dk")
    bvx_ld = dma("sync", bvx[:], bvrd[0:1, :].broadcast_to((128, H)), sem="dk")
    dma("sync", ident[:], identd.ap(), sem="dk")
    kc_ld = dma("sync", misc[0:1, 0:2], clipd[0:1, :], sem="dk")
    id_ld = kc_ld
    clip_ld = kc_ld
    vop(lambda eng: eng.tensor_scalar(
        biasq8[:], biasq[:], 0.125, None, Alu.mult), waits=[kc_ld])
    sig_tok = sop(lambda eng: eng.activation(misc[0:1, 2:4], misc[0:1, 0:2],
                                             Act.Sigmoid), waits=[clip_ld])
    vop(lambda eng: eng.memset(n30[:], -1e30))

    def quant_weight_gu(srcd, dstT, sw_dst):
        q_tok = None
        ev_tok = None
        for ic in range(IC):
            b = ic % 2
            ldt = dma("sync", wstage[b][:, 0:H], srcd[128 * ic:128 * ic + 128, :],
                      waits=[q_tok], sem="dw")
            rm = vop(lambda eng, b=b: eng.reduce_max(wsc[:, 0:1], wstage[b][:, 0:H], AXX,
                                                     apply_absolute_value=True),
                     waits=[ldt])
            vdrain()
            vop(lambda eng, ic=ic, d=sw_dst: eng.tensor_scalar(
                d[:, ic:ic + 1], wsc[:, 0:1], 1.0 / 127.0, None, Alu.mult))
            vop(lambda eng: eng.reciprocal(wsc[:, 1:2], wsc[:, 0:1]))
            vdrain()
            vop(lambda eng, b=b: eng.tensor_scalar(
                wstage[b][:, 0:H], wstage[b][:, 0:H], wsc[:, 1:2], 127.0,
                Alu.mult, Alu.mult))
            q_tok = vop(lambda eng, b=b: eng.tensor_scalar(
                qwnat[:, 0:H], wstage[b][:, 0:H], MAGIC, MAGIC, Alu.add, Alu.subtract),
                waits=[ev_tok])
            for hcx in range(HC):
                pbk = 6 + (hcx % 2)
                tr = top(lambda eng, hcx=hcx, pbk=pbk: eng.transpose(
                    pbh[pbk][:, 0:128], qwnat[:, 128 * hcx:128 * hcx + 128], ident[:]),
                    waits=[q_tok, ev_tok, id_ld])
                ev_tok = vop(lambda eng, hcx=hcx, ic=ic, pbk=pbk, d=dstT: eng.tensor_copy(
                    d[:, hcx, 128 * ic:128 * ic + 128], pbh[pbk][:, 0:128]), waits=[tr])
        return ev_tok

    gu_ev = quant_weight_gu(wgd, qwgT, swg)
    gu_ev2 = quant_weight_gu(wud, quT, swu)

    def quant_weight_down():
        ev_tok = None
        q_tok = None
        for hcx in range(HC):
            b = hcx % 2
            ldt = dma("sync", wstage[b][:], wdd[128 * hcx:128 * hcx + 128, :],
                      waits=[q_tok], sem="dw")
            rm = vop(lambda eng, b=b: eng.reduce_max(wsc[:, 0:1], wstage[b][:], AXX,
                                                     apply_absolute_value=True),
                     waits=[ldt])
            vdrain()
            vop(lambda eng: eng.reciprocal(wsc[:, 1:2], wsc[:, 0:1]))
            vop(lambda eng: eng.tensor_scalar(
                wsc[:, 2:3], wsc[:, 0:1], 1.0 / 127.0, None, Alu.mult))
            vdrain()
            vop(lambda eng, b=b: eng.tensor_scalar(
                wstage[b][:], wstage[b][:], wsc[:, 1:2], 127.0, Alu.mult, Alu.mult))
            vop(lambda eng, b=b: eng.tensor_scalar(
                wstage[b][:], wstage[b][:], MAGIC, MAGIC, Alu.add, Alu.subtract))
            q_tok = vop(lambda eng, b=b: eng.tensor_scalar(
                qwnat[:], wstage[b][:], wsc[:, 2:3], None, Alu.mult),
                waits=[ev_tok])
            for ic in range(IC):
                pbk = 6 + (ic % 2)
                tr = top(lambda eng, ic=ic, pbk=pbk: eng.transpose(
                    pbh[pbk][:, 0:128], qwnat[:, 128 * ic:128 * ic + 128], ident[:]),
                    waits=[q_tok, ev_tok, id_ld])
                ev_tok = vop(lambda eng, ic=ic, hcx=hcx, pbk=pbk: eng.tensor_copy(
                    qwdT[:, ic, 128 * hcx:128 * hcx + 128], pbh[pbk][:, 0:128]),
                    waits=[tr])
        return ev_tok

    dn_ev = quant_weight_down()
    for hc in range(HC):
        dma("gpsimd", QWG[hc], qwgT[:, hc, :], waits=[gu_ev], sem="dg")
        dma("gpsimd", QWU[hc], quT[:, hc, :], waits=[gu_ev2], sem="dg")
    for ic in range(IC):
        dma("gpsimd", QWD[ic], qwdT[:, ic, :], waits=[dn_ev], sem="dg")

    ld_tok = [None, None]
    q_done = [None, None]
    sq_prev = None
    for t in range(16):
        b = 0
        ld_tok[b] = dma("sync", px[b][:],
                        x0d[512 * t:512 * t + 512, :].rearrange("(a p) h -> p a h", p=128),
                        waits=[q_done[b]], sem="dx")
        for sub in range(4):
            tt = 4 * t + sub
            vop(lambda eng, b=b, sub=sub: eng.tensor_reduce(
                pcol[:, 0:1], px[b][:, sub, :], AXX, Alu.add), waits=[ld_tok[b]])
            vdrain()
            vop(lambda eng, tt=tt: eng.tensor_scalar(
                m1[:, tt:tt + 1], pcol[:, 0:1], 1.0 / float(H), None, Alu.mult))
            vdrain()
            xc_t = vop(lambda eng, b=b, sub=sub, tt=tt: eng.tensor_scalar(
                ptmp[:], px[b][:, sub, :], m1[:, tt:tt + 1], None, Alu.subtract),
                waits=[sq_prev])
            sq_prev = sop(lambda eng: eng.activation(ptmp2[:], ptmp[:], Act.Square,
                                                     accum_out=pcol[:, 1:2]),
                          waits=[xc_t])
            vop(lambda eng: eng.tensor_scalar(
                pcol[:, 1:2], pcol[:, 1:2], 1.0 / float(H), 1e-5, Alu.mult, Alu.add),
                waits=[sq_prev])
            vdrain()
            vop(lambda eng: eng.reciprocal(pcol[:, 1:2], pcol[:, 1:2]))
            vdrain()
            r_t = sop(lambda eng, tt=tt: eng.activation(
                r1[:, tt:tt + 1], pcol[:, 1:2], Act.Sqrt), waits=[s.now("tV")])
            q_done[b] = vop(lambda eng, tt=tt: eng.reduce_max(
                xm1[:, tt:tt + 1], ptmp[:], AXX,
                apply_absolute_value=True), waits=[sq_prev])
    vdrain()
    vop(lambda eng: eng.tensor_tensor(xm1[:], xm1[:], r1[:], Alu.mult),
        waits=[s.now("tS")])
    vdrain()
    vop(lambda eng: eng.reduce_max(amax[0][:], xm1[:], AXX))
    vdrain()

    def allreduce_scalar(k, amax_tok, extra_cols):
        sig_col = 2 if k in (0, 2) else 3
        t1 = dma("gpsimd", RED[k][:, :], amax[k][:, 0:1], waits=[amax_tok], sem="dc")
        t2 = dma("gpsimd", red_row[0:1, :], RED[k].transpose([1, 0]), waits=[t1],
                 allow_slow_non_contiguous=True, sem="dc")
        mx = vop(lambda eng: eng.reduce_max(misc[0:1, 4:5], red_row[0:1, :], AXX),
                 waits=[t2])
        t3 = dma("gpsimd", CCI[k].ap()[0:1, 0:1], misc[0:1, 4:5], waits=[mx], sem="dc")
        cc_t = s.op("gpsimd", lambda eng, k=k: eng.collective_compute(
            "AllReduce", Alu.max, replica_groups=[list(range(NCORE))],
            ins=[CCI[k].ap().opt()], outs=[CCO[k].ap().opt()]),
            waits=[t3], inc="cc", n=1)
        t4 = dma("gpsimd", misc[0:1, 4:5], CCO[k].ap()[0:1, 0:1], waits=[cc_t], sem="dc")
        vop(lambda eng, sc=sig_col: eng.tensor_tensor(
            misc[0:1, 5:6], misc[0:1, 4:5], misc[0:1, sc:sc + 1], Alu.mult),
            waits=[t4, sig_tok])
        vdrain()
        vop(lambda eng: eng.tensor_scalar(
            misc[0:1, 6:7], misc[0:1, 5:6], 1.0 / 127.0, None, Alu.mult))
        vdrain()
        rsv = vop(lambda eng: eng.reciprocal(misc[0:1, 7:8], misc[0:1, 6:7]))
        shv = vop(lambda eng: eng.tensor_scalar(
            misc[0:1, 8:9], misc[0:1, 6:7], 0.5, None, Alu.mult))
        vdrain()
        toks = []
        for (col, tag) in extra_cols:
            srcc = {"s": 6, "rs": 7, "sh": 8}[tag]
            t5 = dma("gpsimd", SCAL[0:1, col:col + 1], misc[0:1, srcc:srcc + 1],
                     waits=[rsv, shv], sem="dc")
            toks.append(dma("gpsimd", scal_b[:, col:col + 1],
                            SCAL[0:1, col:col + 1].broadcast_to((128, 1)), waits=[t5],
                            sem="dc"))
        return toks

    bc1 = allreduce_scalar(0, s.now("tV"), [(C_S1, "s"), (C_RS1, "rs")])
    vop(lambda eng: eng.tensor_scalar(
        rq1[:], r1[:], scal_b[:, C_RS1:C_RS1 + 1], None, Alu.mult), waits=bc1)
    vop(lambda eng: eng.tensor_scalar(
        gscale[:], swg[:], scal_b[:, C_S1:C_S1 + 1], None, Alu.mult),
        waits=bc1 + [gu_ev])
    vop(lambda eng: eng.tensor_scalar(
        uscale[:], swu[:], scal_b[:, C_S1:C_S1 + 1], None, Alu.mult),
        waits=bc1 + [gu_ev2])
    s.full_barrier()

    def pass2(xsrc, mcache, rqcache, amax_idx):
        vop(lambda eng, k=amax_idx: eng.memset(amax[k][:], 0.0))
        ld_tok = [None, None]
        lastq = [None, None]
        silu_tok = [None, None]
        hp_tok = [None, None, None, None]
        hpdma_tok = [None, None, None, None]
        evq_tok = [None, None]
        tr_last = None
        amax_tok = None
        for t in range(16):
            b = t % 2
            ld_tok[b] = dma("sync", p2_x[b][:],
                            xsrc[512 * t:512 * t + 512, :].rearrange("(a p) h -> p a h",
                                                                     p=128),
                            waits=[lastq[b]], sem=f"dx{b}")
            for sub in range(4):
                tt = 4 * t + sub
                vop(lambda eng, b=b, sub=sub, tt=tt, mc=mcache, rc=rqcache:
                    eng.tensor_scalar(p2_t[:], p2_x[b][:, sub, :], mc[:, tt:tt + 1],
                                      rc[:, tt:tt + 1], Alu.subtract, Alu.mult),
                    waits=[ld_tok[b]] + ([tr_last] if sub == 0 else []))
                vop(lambda eng: eng.tensor_scalar(
                    p2_t[:], p2_t[:], MAGIC, MAGIC, Alu.add, Alu.subtract))
                lastq[b] = vop(lambda eng, sub=sub: eng.tensor_scalar(
                    p2_qx[:, sub, :], p2_t[:], 127.0, -128.0, Alu.min, Alu.max))
            for sub in range(4):
                for hc in range(HC):
                    j = (sub * HC + hc) % 2
                    pbk = 4 + j
                    tr = top(lambda eng, sub=sub, hc=hc, pbk=pbk: eng.transpose(
                        pbh[pbk][:, 0:128], p2_qx[:, sub, 128 * hc:128 * hc + 128],
                        ident[:]), waits=[lastq[b], evq_tok[j]])
                    evq_tok[j] = vop(lambda eng, sub=sub, hc=hc, pbk=pbk: eng.tensor_copy(
                        p2_qxT[:, hc, 128 * sub:128 * sub + 128], pbh[pbk][:, 0:128]),
                        waits=[tr])
            tr_ev = [evq_tok[0], evq_tok[1]]
            mm_g = None
            mm_u = None
            for ic in range(IC):
                pg = ic % 2
                pu = 2 + ic % 2
                for hc in range(HC):
                    mm_g = top(lambda eng, ic=ic, hc=hc, pg=pg: eng.matmul(
                        pb[pg][:], qwgT[:, hc, 128 * ic:128 * ic + 128],
                        p2_qxT[:, hc, :],
                        start=(hc == 0), stop=(hc == HC - 1)),
                        waits=(tr_ev + [silu_tok[ic % 2]]) if hc == 0 else [])
                for hc in range(HC):
                    mm_u = top(lambda eng, ic=ic, hc=hc, pu=pu: eng.matmul(
                        pb[pu][:], quT[:, hc, 128 * ic:128 * ic + 128],
                        p2_qxT[:, hc, :],
                        start=(hc == 0), stop=(hc == HC - 1)),
                        waits=[hp_tok[(ic + 2) % 4]] if hc == 0 else [])
                silu_tok[ic % 2] = sop(lambda eng, ic=ic, pg=pg: eng.activation(
                    p2_sg[ic % 2][:], pb[pg][:], Act.Silu, scale=gscale[:, ic:ic + 1]),
                    waits=[mm_g, hp_tok[(ic + 2) % 4]])
                hb = ic % 4
                hp_tok[hb] = vop(lambda eng, ic=ic, pu=pu, hb=hb:
                                 eng.scalar_tensor_tensor(
                                     p2_hp[hb][:], pb[pu][:], uscale[:, ic:ic + 1],
                                     p2_sg[ic % 2][:], Alu.mult, Alu.mult),
                                 waits=[mm_u, silu_tok[ic % 2], hpdma_tok[hb]])
                vop(lambda eng, hb=hb, ic=ic: eng.reduce_max(
                    p2_col[:, ic:ic + 1], p2_hp[hb][:], AXX,
                    apply_absolute_value=True))
                hpdma_tok[hb] = dma("gpsimd", HP[ic, :, 512 * t:512 * t + 512],
                                    p2_hp[hb][:], waits=[hp_tok[hb]], sem=f"dg{hb}")
            vdrain()
            vop(lambda eng: eng.reduce_max(p2_col[:, 22:23], p2_col[:, 0:IC], AXX))
            vdrain()
            amax_tok = vop(lambda eng, k=amax_idx: eng.tensor_tensor(
                amax[k][:], amax[k][:], p2_col[:, 22:23], Alu.max))
            vdrain()
            tr_last = mm_g
        return amax_tok

    amax2_tok = pass2(x0d, m1, rq1, 1)
    bc2 = allreduce_scalar(1, amax2_tok, [(C_RS2, "rs"), (C_S2H, "sh")])
    s.full_barrier()
    if debug:
        dma("gpsimd", DBG1[:, 0:1], amax[0][:], allow_slow_non_contiguous=True, sem="dg")
        dma("gpsimd", DBG1[:, 1:2], amax[1][:], allow_slow_non_contiguous=True, sem="dg")
        dma("gpsimd", DBG1[:, 16:32], scal_b[:], allow_slow_non_contiguous=True, sem="dg")
        dma("gpsimd", DBG1[:, 32:96], m1[:], allow_slow_non_contiguous=True, sem="dg")
        dma("gpsimd", DBG1[:, 96:160], r1[:], allow_slow_non_contiguous=True, sem="dg")
        dma("gpsimd", DBG1[:, 160:224], rq1[:], allow_slow_non_contiguous=True, sem="dg")
        dma("gpsimd", DBG1[:, 224:246], gscale[:], allow_slow_non_contiguous=True, sem="dg")
        dma("gpsimd", DBGM[:, :], misc[:], sem="dg")
        dma("gpsimd", DBGW[:, :], qwgT[:, 0, :], sem="dg")
        dma("gpsimd", DBGQ[:, :, :], p2_qxT[:], sem="dg")
        s.full_barrier()

    if stage("pass3a"):
        def pass3a(xres, dstd, shcol, rscol):
            ld_tok = [None] * 4
            qh_tok = [None] * 4
            mmh = [None] * 4
            mm_tok = None
            ev_toks = [None] * 4
            stq = None
            last_store = None
            for u in range(32):
                x0ld = dma("sync", p3_x0[0][:],
                           xres[256 * u:256 * u + 256, :].rearrange(
                               "(a p) h -> p a h", p=128),
                           waits=[stq], sem="dx")
                for ic in range(IC):
                    hb = ic % 4
                    ld_tok[hb] = dma("sync", p3_hp[hb][:],
                                     HP[ic, :, 256 * u:256 * u + 256],
                                     waits=[qh_tok[hb]], sem=f"dh{hb}")
                    vop(lambda eng, hb=hb, rs=rscol: eng.tensor_scalar(
                        p3_hp[hb][:], p3_hp[hb][:], scal_b[:, rs:rs + 1], MAGIC,
                        Alu.mult, Alu.add), waits=[ld_tok[hb]])
                    vop(lambda eng, hb=hb: eng.tensor_scalar(
                        p3_hp[hb][:], p3_hp[hb][:], MAGIC, 127.0,
                        Alu.subtract, Alu.min))
                    qh_tok[hb] = vop(lambda eng, hb=hb: eng.tensor_scalar(
                        p3_qh[hb][:], p3_hp[hb][:], -128.0, None, Alu.max),
                        waits=[mmh[hb]])
                    if debug and ic in (0, 1) and dstd is X1:
                        dma("gpsimd", QHD[u, ic], p3_qh[hb][:], waits=[qh_tok[hb]],
                            sem="dg")
                    for th in range(2):
                        for fh in range(2):
                            pbk = 2 * th + fh
                            mm_tok = top(lambda eng, ic=ic, th=th, fh=fh, hb=hb,
                                         pbk=pbk:
                                         eng.matmul(
                                             pb[pbk][:],
                                             p3_qh[hb][:, 128 * th:128 * th + 128],
                                             qwdT[:, ic, 512 * fh:512 * fh + 512],
                                             start=(ic == 0), stop=(ic == IC - 1)),
                                         waits=[qh_tok[hb]] +
                                         ([ev_toks[pbk]] if ic == 0 else []))
                    mmh[hb] = mm_tok
                for th in range(2):
                    for fh in range(2):
                        pbk = 2 * th + fh
                        ev_toks[pbk] = vop(
                            lambda eng, th=th, fh=fh, pbk=pbk, sh=shcol:
                            eng.scalar_tensor_tensor(
                                p3_x1[:, th, 512 * fh:512 * fh + 512],
                                pb[pbk][:], scal_b[:, sh:sh + 1],
                                p3_x0[0][:, th, 512 * fh:512 * fh + 512],
                                Alu.mult, Alu.add),
                            waits=[mm_tok, x0ld, stq])
                stq = dma("gpsimd",
                          dstd[256 * u:256 * u + 256, :].rearrange(
                              "(a p) h -> p a h", p=128),
                          p3_x1[:], waits=[ev_toks[3], ev_toks[0], ev_toks[1],
                                           ev_toks[2]], sem="dg")
                last_store = stq
            return last_store

        pass3a(x0d, X1, C_S2H, C_RS2)
        s.full_barrier()
        if debug:
            for t in range(16):
                dma("gpsimd", X1C[512 * t:512 * t + 512, :], X1[512 * t:512 * t + 512, :],
                    sem="dg")
            s.full_barrier()

    if stage("pass3b"):
        def pass3b(seq):
            base_t = SEQ * seq
            wtoks = []
            for hc in range(HC):
                wtoks.append(dma("sync", b_wq[:, hc, :], WQB[hc], sem="dw"))
                wtoks.append(dma("sync", b_wk[:, hc, :], WKB[hc], sem="dw"))
                wtoks.append(dma("sync", b_wv[:, hc, :], WVB[hc], sem="dw"))
            wq_ld = wtoks[-1]
            lastuse = None
            ev_xT = [None, None]
            qk_ev = [None, None]
            v_ev = [None, None]
            qs_dma = None
            sq_prev2 = None
            k_last = None
            v_last = None
            for u in range(16):
                ld_tok = dma("sync", b_x1[0][:],
                             X1[base_t + 256 * u: base_t + 256 * u + 256, :]
                             .rearrange("(a p) h -> p a h", p=128),
                             waits=[lastuse], sem="dx")
                for th in range(2):
                    vop(lambda eng, th=th: eng.tensor_reduce(
                        b_col[:, 0:1], b_x1[0][:, th, :], AXX, Alu.add),
                        waits=[ld_tok])
                    vdrain()
                    vop(lambda eng: eng.tensor_scalar(
                        b_col[:, 1:2], b_col[:, 0:1], 1.0 / float(H), None, Alu.mult))
                    vdrain()
                    xc_t = vop(lambda eng, th=th: eng.tensor_scalar(
                        b_pt[:], b_x1[0][:, th, :], b_col[:, 1:2], None, Alu.subtract),
                        waits=[sq_prev2, qs_dma])
                    sq_prev2 = sop(lambda eng: eng.activation(
                        b_ln[:], b_pt[:], Act.Square, accum_out=b_col[:, 2:3]),
                        waits=[xc_t, ev_xT[0], ev_xT[1]])
                    vop(lambda eng: eng.tensor_scalar(
                        b_col[:, 2:3], b_col[:, 2:3], 1.0 / float(H), 1e-5,
                        Alu.mult, Alu.add), waits=[sq_prev2])
                    vdrain()
                    vop(lambda eng: eng.reciprocal(b_col[:, 2:3], b_col[:, 2:3]))
                    vdrain()
                    r_t = sop(lambda eng: eng.activation(
                        b_col[:, 3:4], b_col[:, 2:3], Act.Sqrt), waits=[s.now("tV")])
                    lastuse = vop(lambda eng: eng.tensor_scalar(
                        b_ln[:], b_pt[:], b_col[:, 3:4], None, Alu.mult),
                        waits=[r_t])
                    for hc in range(HC):
                        j = hc % 2
                        pbk = 6 + j
                        tr = top(lambda eng, hc=hc, pbk=pbk: eng.transpose(
                            pbh[pbk][:, 0:128], b_ln[:, 128 * hc:128 * hc + 128],
                            ident[:]), waits=[lastuse, ev_xT[j]])
                        ev_xT[j] = vop(lambda eng, th=th, hc=hc, pbk=pbk:
                                       eng.tensor_copy(
                                           b_xT[:, hc, 128 * th:128 * th + 128],
                                           pbh[pbk][:, 0:128]), waits=[tr])
                exT = [ev_xT[0], ev_xT[1]]
                mm_qk = None
                for which in range(2):
                    wsrc = b_wq if which == 0 else b_wk
                    for fc in range(HC):
                        pbk = which
                        for hc in range(HC):
                            mm_qk = top(lambda eng, wsrc=wsrc, fc=fc, hc=hc, pbk=pbk:
                                        eng.matmul(
                                            pb[pbk][:, 0:256],
                                            wsrc[:, hc, 128 * fc:128 * fc + 128],
                                            b_xT[:, hc, :],
                                            start=(hc == 0), stop=(hc == HC - 1)),
                                        waits=(exT + [qk_ev[pbk], wq_ld])
                                        if hc == 0 else [])
                        if which == 0:
                            qk_ev[pbk] = sop(lambda eng, fc=fc, pbk=pbk: eng.activation(
                                b_qs[:, :, fc, :], pb[pbk][:, 0:256], Act.Identity,
                                bias=biasq8[:, fc:fc + 1], scale=0.125),
                                waits=[mm_qk] + ([qs_dma] if fc == 0 else []))
                        else:
                            qk_ev[pbk] = sop(lambda eng, fc=fc, pbk=pbk, u=u:
                                             eng.activation(
                                                 kT[:, fc, 256 * u:256 * u + 256],
                                                 pb[pbk][:, 0:256], Act.Identity,
                                                 bias=biask[:, fc:fc + 1]),
                                             waits=[mm_qk])
                            k_last = qk_ev[pbk]
                for j in range(2):
                    blk = 32 * seq + 2 * u + j
                    qs_dma = dma("gpsimd", QT[blk], b_qs[:, j], waits=[qk_ev[0]],
                                 sem="dg")
                mm_v = None
                for th in range(2):
                    for fh in range(2):
                        pbk = 2 + fh
                        for hc in range(HC):
                            mm_v = top(lambda eng, th=th, fh=fh, hc=hc, pbk=pbk:
                                       eng.matmul(
                                           pb[pbk][:],
                                           b_xT[:, hc, 128 * th:128 * th + 128],
                                           b_wv[:, hc, 512 * fh:512 * fh + 512],
                                           start=(hc == 0), stop=(hc == HC - 1)),
                                       waits=(exT + [v_ev[fh], wq_ld])
                                       if hc == 0 else [])
                        ch = 2 * u + th
                        v_ev[fh] = vop(lambda eng, ch=ch, fh=fh, pbk=pbk:
                                       eng.tensor_tensor(
                                           Vt[:, ch, 512 * fh:512 * fh + 512],
                                           pb[pbk][:], bvx[:, 512 * fh:512 * fh + 512],
                                           Alu.add), waits=[mm_v, bvx_ld])
                        v_last = v_ev[fh]
            return k_last, v_last

        kv_a = pass3b(0)
        if debug:
            for fc in range(HC):
                dma("gpsimd", KDd[:, fc, :], kT[:, fc, :],
                    waits=[kv_a[0], kv_a[1]], sem="dg")
            for ch in range(NCHUNK):
                dma("gpsimd", VDd[:, ch, :], Vt[:, ch, :],
                    waits=[kv_a[0], kv_a[1]], sem="dg")
        s.full_barrier()

    if stage("attn"):

        def attention(seq, kv):
            base_t = SEQ * seq
            e1 = dma("gpsimd", a_VtE[0:64, :], Vt[0:64, 0, :], waits=list(kv),
                     sem="de")
            e2 = dma("gpsimd", a_VtE[64:128, :], Vt[64:128, NCHUNK - 1, :],
                     waits=list(kv), sem="de")
            e3 = vop(lambda eng: eng.tensor_copy(a_kTe[:, :, 0:64], kT[:, :, 0:64]),
                     waits=list(kv))
            e4 = vop(lambda eng: eng.tensor_copy(a_kTe[:, :, 64:128],
                                                 kT[:, :, SEQ - 64:SEQ]))

            st = {
                "qld": [None, None],
                "qt_use": [None, None],
                "s_ev": [None, None, None, None],
                "tr_ev": [None, None],
                "A_use": [None, None, None, None],
                "ctx_ev": [None, None],
                "resid_x1": None, "resid_st": None, "resid_xc": None,
                "resid_sq": None, "resid_rm": None,
            }

            def load_qt(tt):
                b = tt % 2
                qld_t = dma("sync", a_qt[b][:], QT[32 * seq + tt],
                            waits=[st["qt_use"][b], st["qld"][b]], sem=f"dq{b}")
                st["qld"][b] = qld_t
                return b, qld_t

            def do_mid(tt, ctx_dst, gate_dst):
                b, qld_t = load_qt(tt)
                vc = tt - 1
                segs = [("edge", None), ("win", vc), ("win", vc + 1), ("win", vc + 2)]
                mask_list = [(0, 64, 128, 192), (0, 64, 384, 512),
                             (64, 128, 128, 256), (64, 128, 448, 512)]
                SBANK = (0, 1, 6, 7)
                out_toks = []

                def stage1(j):
                    hp, hh = divmod(j, 2)
                    p0 = 64 * hh
                    sl = j % 4
                    pS = SBANK[sl]
                    top(lambda eng, p0=p0, hp=hp, pS=pS, b=b:
                        eng.matmul(pb[pS][:, 0:128],
                                   a_qt[b][p0:p0 + 64, hp, :],
                                   a_kTe[p0:p0 + 64, hp, :],
                                   start=True, stop=False),
                        waits=[qld_t, st["s_ev"][sl], kv[0], kv[1], e4])
                    k0 = 128 * vc
                    mmt = top(lambda eng, p0=p0, hp=hp, pS=pS, b=b, k0=k0:
                              eng.matmul(pb[pS][:, 128:512],
                                         a_qt[b][p0:p0 + 64, hp, :],
                                         kT[p0:p0 + 64, hp, k0:k0 + 384],
                                         start=False, stop=True,
                                         skip_group_check=True))
                    st["qt_use"][b] = mmt
                    mt = None
                    for (r0, r1_, c0, c1) in mask_list:
                        mt = vop(lambda eng, pS=pS, r0=r0, r1_=r1_, c0=c0, c1=c1:
                                 eng.memset(pb[pS][r0:r1_, c0:c1], -1e30),
                                 waits=[mmt])
                    nm = vop(lambda eng, pS=pS, sl=sl: eng.reduce_max(
                        a_col[:, sl:sl + 1], pb[pS][:], AXX, negate=True),
                        waits=[mt])
                    ex = sop(lambda eng, pS=pS, sl=sl: eng.activation(
                        a_A[sl][:], pb[pS][:], Act.Exp,
                        bias=a_col[:, sl:sl + 1],
                        accum_out=a_col[:, 4 + sl:5 + sl]),
                        waits=[nm, st["A_use"][sl]])
                    st["s_ev"][sl] = ex
                    return ex

                def stage2(j, ex):
                    hp, hh = divmod(j, 2)
                    h = 2 * hp + hh
                    sl = j % 4
                    rc = vop(lambda eng, sl=sl: eng.reciprocal(
                        a_col[:, 8 + sl:9 + sl], a_col[:, 4 + sl:5 + sl]),
                        waits=[ex, st["ctx_ev"][hh]])
                    cmm = None
                    for ci in range(4):
                        pT = 2 + (ci % 2)
                        tr = top(lambda eng, sl=sl, ci=ci, pT=pT: eng.transpose(
                            pbh[pT][:, 0:128],
                            a_A[sl][:, 128 * ci:128 * ci + 128], ident[:]),
                            waits=[ex, st["tr_ev"][ci % 2]])
                        st["A_use"][sl] = tr
                        tre = vop(lambda eng, hh=hh, ci=ci, pT=pT:
                                  eng.tensor_copy(
                                      a_aT[hh][:, 128 * ci:128 * ci + 128],
                                      pbh[pT][:, 0:128]), waits=[tr])
                        st["tr_ev"][ci % 2] = tre
                        styp, c = segs[ci]
                        pC = 4 + hh
                        extra = [st["ctx_ev"][hh]] if ci == 0 else []
                        if ci == 0 and hh == 1 and st.get("sev2"):
                            extra.append(st.pop("sev2"))
                        if styp == "edge":
                            cmm = top(lambda eng, hh=hh, ci=ci, h=h, pC=pC:
                                      eng.matmul(
                                          pb[pC][:, 0:64],
                                          a_aT[hh][:, 128 * ci:128 * ci + 128],
                                          a_VtE[:, 64 * h:64 * h + 64],
                                          start=(ci == 0), stop=(ci == 3)),
                                      waits=[tre, e2] + extra)
                        else:
                            cmm = top(lambda eng, hh=hh, ci=ci, h=h, c=c, pC=pC:
                                      eng.matmul(
                                          pb[pC][:, 0:64],
                                          a_aT[hh][:, 128 * ci:128 * ci + 128],
                                          Vt[:, c, 64 * h:64 * h + 64],
                                          start=(ci == 0), stop=(ci == 3)),
                                      waits=[tre] + extra)
                    cev = sop(lambda eng, h=h, sl=sl, pC=4 + hh, dst=ctx_dst:
                              eng.activation(dst[:, 64 * h:64 * h + 64],
                                             pb[pC][:, 0:64], Act.Identity,
                                             scale=a_col[:, 8 + sl:9 + sl]),
                              waits=[cmm, rc, gate_dst])
                    st["ctx_ev"][hh] = cev
                    out_toks.append(cev)

                exs = {}
                for j in range(4):
                    exs[j] = stage1(j)
                for j in range(4, 16):
                    stage2(j - 4, exs[j - 4])
                    exs[j] = stage1(j)
                for j in range(12, 16):
                    stage2(j, exs[j])
                return out_toks

            def do_full(tt, ctx_dst, gate_dst):
                b, qld_t = load_qt(tt)
                if tt == 0:
                    mrow, mcol = (64, 128), (192, 4032)
                else:
                    mrow, mcol = (0, 64), (64, 3904)
                out_toks = []
                sev = [None, None, None]
                SB = (0, 1, 5)
                exF = [None, None]
                rcF = [None, None]
                trlast = [None]

                def s1F(h):
                    hp, hh = divmod(h, 2)
                    p0 = 64 * hh
                    sb = h % 2
                    aS = a_Sd[sb]
                    mmt = None
                    for kc in range(8):
                        pS = SB[kc % 3]
                        mmt = top(lambda eng, hp=hp, p0=p0, kc=kc, pS=pS, b=b:
                                  eng.matmul(
                                      pb[pS][:], a_qt[b][p0:p0 + 64, hp, :],
                                      kT[p0:p0 + 64, hp, 512 * kc:512 * kc + 512],
                                      start=True, stop=True),
                                  waits=[qld_t, sev[kc % 3],
                                         st["ctx_ev"][0], st["ctx_ev"][1],
                                         st["s_ev"][0], st["s_ev"][1]]
                                  if kc < 3 else [sev[kc % 3]])
                        sev[kc % 3] = vop(lambda eng, kc=kc, pS=pS, aS=aS:
                                          eng.tensor_copy(
                                              aS[:, 512 * kc:512 * kc + 512],
                                              pb[pS][:]),
                                          waits=[mmt] +
                                          ([exF[sb]] if kc == 0 else []))
                    st["qt_use"][b] = mmt
                    vop(lambda eng, aS=aS: eng.memset(
                        aS[mrow[0]:mrow[1], mcol[0]:mcol[1]], -1e30))
                    nm = vop(lambda eng, aS=aS, sb=sb: eng.reduce_max(
                        a_col[:, 0 + sb:1 + sb], aS[:], AXX, negate=True),
                        waits=[exF[sb]])
                    return nm

                def s2F(h, nm):
                    hp, hh = divmod(h, 2)
                    sb = h % 2
                    aS = a_Sd[sb]
                    ex = sop(lambda eng, aS=aS, sb=sb: eng.activation(
                        a_AF[:], aS[:], Act.Exp, bias=a_col[:, 0 + sb:1 + sb],
                        accum_out=a_col[:, 4 + sb:5 + sb]),
                        waits=[nm, trlast[0], rcF[sb], st["resid_rm"]])
                    exF[sb] = ex
                    rc = vop(lambda eng, sb=sb: eng.reciprocal(
                        a_col[:, 8 + sb:9 + sb], a_col[:, 4 + sb:5 + sb]),
                        waits=[ex, st["ctx_ev"][0]])
                    rcF[sb] = rc
                    cmm = None
                    for ci in range(NCHUNK):
                        j = ci % 2
                        pT = 2 + j
                        tr = top(lambda eng, ci=ci, pT=pT: eng.transpose(
                            pbh[pT][:, 0:128], a_AF[:, 128 * ci:128 * ci + 128],
                            ident[:]), waits=[ex, st["tr_ev"][j]])
                        trlast[0] = tr
                        tre = vop(lambda eng, j=j, pT=pT: eng.tensor_copy(
                            a_aT[j][:, 0:128], pbh[pT][:, 0:128]), waits=[tr])
                        st["tr_ev"][j] = tre
                        cmm = top(lambda eng, ci=ci, h=h, j=j: eng.matmul(
                            pb[4][:, 0:64], a_aT[j][:, 0:128],
                            Vt[:, ci, 64 * h:64 * h + 64],
                            start=(ci == 0), stop=(ci == NCHUNK - 1)),
                            waits=[tre] + ([st["ctx_ev"][0]] if ci == 0 else []))
                    cev = sop(lambda eng, h=h, sb=sb, dst=ctx_dst: eng.activation(
                        dst[:, 64 * h:64 * h + 64], pb[4][:, 0:64], Act.Identity,
                        scale=a_col[:, 8 + sb:9 + sb]), waits=[cmm, rc, gate_dst])
                    st["ctx_ev"][0] = cev
                    st["ctx_ev"][1] = cev
                    out_toks.append(cev)

                nms = {}
                nms[0] = s1F(0)
                for h in range(1, NH):
                    nms[h] = s1F(h)
                    s2F(h - 1, nms[h - 1])
                s2F(NH - 1, nms[NH - 1])
                st["s_ev"][0] = sev[0]
                st["s_ev"][1] = sev[1]
                st["sev2"] = sev[2]
                return out_toks

            def resid(tt, ctx_toks):
                xt = dma("sync", a_x1[:],
                         X1[base_t + 128 * tt:base_t + 128 * tt + 128, :],
                         waits=[st["resid_x1"]], sem="dr")
                at_ = gop(lambda eng: eng.tensor_tensor(a_x2[:], a_x1[:],
                                                        a_ctx[:], Alu.add),
                          waits=[xt, st["resid_st"]] + ctx_toks)
                st["resid_x1"] = at_
                ttg = NT128S * seq + tt
                stx = dma("gpsimd",
                          X2[base_t + 128 * tt:base_t + 128 * tt + 128, :],
                          a_x2[:], sem="dg")
                st["resid_st"] = stx
                sm = sop(lambda eng: eng.activation(
                    a_ctx[:], a_x2[:], Act.Identity, accum_out=g_col[:, 0:1]),
                    waits=[at_, st["resid_sq"]])
                sdrain()
                mn = sop(lambda eng, ttg=ttg: eng.activation(
                    m3[:, ttg:ttg + 1], g_col[:, 0:1], Act.Identity,
                    scale=1.0 / float(H)))
                xc_t = vop(lambda eng, ttg=ttg: eng.tensor_scalar(
                    a_xc[:], a_x2[:], m3[:, ttg:ttg + 1], None, Alu.subtract),
                    waits=[mn, st["resid_rm"], at_])
                sq_t = sop(lambda eng, ttg=ttg: eng.activation(
                    a_ctx[:], a_xc[:], Act.Square, accum_out=ssq3[:, ttg:ttg + 1]),
                    waits=[xc_t])
                st["resid_sq"] = sq_t
                rm_t = vop(lambda eng, ttg=ttg: eng.reduce_max(
                    xm3[:, ttg:ttg + 1], a_xc[:], AXX, apply_absolute_value=True),
                    waits=[xc_t])
                st["resid_rm"] = rm_t
                st["resid_xc"] = sq_t
                return sq_t, stx

            free_tok = None
            laststx = None
            for tt in range(NT128S):
                if tt in (0, NT128S - 1):
                    toks = do_full(tt, a_ctx, free_tok)
                else:
                    toks = do_mid(tt, a_ctx, free_tok)
                free_tok, laststx = resid(tt, toks)
            return laststx

        attention(0, kv_a)
        s.full_barrier()
        kv_b = pass3b(1)
        s.full_barrier()
        attention(1, kv_b)
        vop(lambda eng: eng.tensor_scalar(
            r3[:], ssq3[:], 1.0 / float(H), 1e-5, Alu.mult, Alu.add),
            waits=[s.now("tG"), s.now("tS")])
        vdrain()
        vop(lambda eng: eng.reciprocal(r3[:], r3[:]))
        vdrain()
        r3_t = sop(lambda eng: eng.activation(r3[:], r3[:], Act.Sqrt),
                   waits=[s.now("tV")])
        vop(lambda eng: eng.tensor_tensor(xm3[:], xm3[:], r3[:], Alu.mult),
            waits=[r3_t])
        vdrain()
        vop(lambda eng: eng.reduce_max(amax[2][:], xm3[:], AXX))
        vdrain()

    if stage("all"):
        s.full_barrier()
        bc3 = allreduce_scalar(2, s.now("tV"), [(C_S3, "s"), (C_RS3, "rs")])
        vop(lambda eng: eng.tensor_scalar(
            rq3[:], r3[:], scal_b[:, C_RS3:C_RS3 + 1], None, Alu.mult), waits=bc3)
        vop(lambda eng: eng.tensor_scalar(
            gscale[:], swg[:], scal_b[:, C_S3:C_S3 + 1], None, Alu.mult), waits=bc3)
        vop(lambda eng: eng.tensor_scalar(
            uscale[:], swu[:], scal_b[:, C_S3:C_S3 + 1], None, Alu.mult), waits=bc3)
        for hc in range(HC):
            dma("sync", qwgT[:, hc, :], QWG[hc], sem="dw")
            dma("gpsimd", quT[:, hc, :], QWU[hc], sem="dw")
        for ic in range(IC):
            dma("scalar", qwdT[:, ic, :], QWD[ic], sem="dw")
        s.full_barrier()
        amax4_tok = pass2(X2, m3, rq3, 3)
        bc4 = allreduce_scalar(3, amax4_tok, [(C_RS4, "rs"), (C_S4H, "sh")])
        s.full_barrier()
        pass3a(X2, outd, C_S4H, C_RS4)
        s.op("gpsimd", lambda eng: None,
             waits=[s.now(x) for x in Sched.DMA_SEMS])
        s.op("sync", lambda eng: None,
             waits=[s.now(x) for x in Sched.DMA_SEMS])
    else:
        qs = [s.now(q) for q in Sched.DMA_SEMS]
        for e in ENGS:
            s.op(e, lambda eng: None,
                 waits=qs + [s.now("tV"), s.now("tS"), s.now("tP")])

    import contextlib
    es = contextlib.ExitStack()
    with es:
        sems = {}
        for name in ("qS", "qG", "qC", "qV", "dx", "dx0", "dx1", "dh0", "dh1",
                     "dh2", "dh3", "dw", "dk", "dq0", "dq1", "dr", "de", "dc", "dg",
                     "dg0", "dg1", "dg2", "dg3", "tP", "tV", "tS", "tG", "cc", "bar"):
            sems[name] = es.enter_context(nc.semaphore(name))
        s.bar_handle = sems["bar"]
        with nc.Block() as block:
            def mk(ops):
                def body(eng):
                    for fn, ws, inc in ops:
                        wmax = {}
                        for (sname, v) in ws:
                            wmax[sname] = max(wmax.get(sname, 0), v)
                        for idx, (sname, v) in enumerate(wmax.items()):
                            if idx and idx % 2 == 0:
                                eng.nop(nofuse=True)
                            eng.wait_ge(sems[sname], v)
                        if wmax and eng.engine == mybir.EngineType.PE:
                            eng.nop(nofuse=True)
                        inst = fn(eng)
                        if inc is not None:
                            sem, n = inc
                            inst.then_inc(sems[sem], n)
                return body
            block.sync(mk(s.prog["sync"]))
            block.vector(mk(s.prog["vector"]))
            block.scalar(mk(s.prog["scalar"]))
            block.tensor(mk(s.prog["tensor"]))
            block.gpsimd(mk(s.prog["gpsimd"]))
    return nc


def _shard_inputs(inputs):
    x = np.ascontiguousarray(
        np.asarray(inputs["L0_attn_hidden_states"], dtype=np.float32).reshape(-1, H))
    bq = np.asarray(inputs["L0_attn_bq"], dtype=np.float32)
    bk = np.asarray(inputs["L0_attn_bk"], dtype=np.float32)
    bv = np.asarray(inputs["L0_attn_bv"], dtype=np.float32)
    com = {
        "wq_t": np.ascontiguousarray(np.asarray(inputs["L0_attn_Wq"], dtype=np.float32).T),
        "wk_t": np.ascontiguousarray(np.asarray(inputs["L0_attn_Wk"], dtype=np.float32).T),
        "wv_t": np.ascontiguousarray(np.asarray(inputs["L0_attn_Wv"], dtype=np.float32).T),
        "bq2": np.ascontiguousarray(bq.reshape(8, 128).T),
        "bk2": np.ascontiguousarray(bk.reshape(8, 128).T),
        "bv_row": np.ascontiguousarray(bv.reshape(1, H)),
        "w_gate": np.ascontiguousarray(np.asarray(inputs["L0_ffn_w_gate"], dtype=np.float32)),
        "w_up": np.ascontiguousarray(np.asarray(inputs["L0_ffn_w_up"], dtype=np.float32)),
        "w_down": np.ascontiguousarray(np.asarray(inputs["L0_ffn_w_down"], dtype=np.float32)),
        "clip2": np.array([[float(np.asarray(inputs["L0_ffn_clip_x"]).reshape(-1)[0]),
                            float(np.asarray(inputs["L0_ffn_clip_down"]).reshape(-1)[0])]],
                          dtype=np.float32),
    }
    in_maps = []
    for c in range(NCORE):
        m = dict(com)
        m["x0"] = np.ascontiguousarray(x[c * T:(c + 1) * T])
        in_maps.append(m)
    return in_maps


def kernel(**inputs):
    in_maps = _shard_inputs(inputs)
    nc = build_nc(debug=False, upto="all")
    res = run_bass_kernel_spmd(nc, in_maps, core_ids=list(range(NCORE)))
    out = np.concatenate([np.asarray(r["out"], dtype=np.float32) for r in res.results],
                         axis=0)
    return out.reshape(16, 4096, 1024)


# version 06
# speedup vs baseline: 1.0371x; 1.0062x over previous
import numpy as np
import ml_dtypes
import concourse.bass as bass
import concourse.mybir as mybir
from concourse.bass_utils import run_bass_kernel_spmd

F32 = mybir.dt.float32
BF16 = mybir.dt.bfloat16
Alu = mybir.AluOpType
Act = mybir.ActivationFunctionType
AXX = mybir.AxisListType.X
MAGIC = float(1.5 * 2.0 ** 23)

T = 8192
H = 1024
II = 2816
IC = II // 128
HC = H // 128
SEQ = 4096
NH = 16
NCHUNK = SEQ // 128
NT128S = SEQ // 128
NCORE = 8

ENGS = ("sync", "vector", "scalar", "tensor", "gpsimd")
QSEM = {"sync": "qS", "gpsimd": "qG", "scalar": "qC", "vector": "qV"}
C_S1, C_RS1, C_S2, C_RS2, C_S2H, C_S3, C_RS3, C_S4, C_RS4, C_S4H = range(10)


class Sched:
    def __init__(self):
        self.prog = {e: [] for e in ENGS}
        self.cnt = {}
        self.bar_handle = None

    def op(self, eng, fn, waits=(), inc=None, n=1):
        ws = [w for w in waits if w is not None]
        tok = None
        if inc is not None:
            self.cnt[inc] = self.cnt.get(inc, 0) + n
            tok = (inc, self.cnt[inc])
        self.prog[eng].append((fn, ws, (inc, n) if inc else None))
        return tok

    def now(self, sem):
        return (sem, self.cnt.get(sem, 0))

    DMA_SEMS = ("qS", "qG", "qC", "qV", "dx", "dx0", "dx1", "dh0", "dh1",
                "dh2", "dh3", "dw", "dk", "dq0", "dq1", "dr", "de", "dc", "dg",
                "dg0", "dg1", "dg2", "dg3")

    def full_barrier(self):
        qs = [self.now(q) for q in self.DMA_SEMS]
        for e in ENGS:
            w = qs if e == "gpsimd" else []
            self.op(e, lambda eng: eng.nop(nofuse=True),
                    waits=w, inc="bar", n=1)
        val = self.cnt["bar"]
        for e in ENGS:
            self.op(e, lambda eng: None, waits=[("bar", val)])


def build_nc(debug=False, upto="all"):
    nc = bass.Bass("TRN2", target_bir_lowering=False)
    STAGES = ["pass2", "pass3a", "pass3b", "attn", "all"]
    SI = STAGES.index(upto)

    def stage(name):
        return SI >= STAGES.index(name)

    ext_dbg = "ExternalOutput" if debug else "Internal"

    x0d = nc.dram_tensor("x0", [T, H], F32, kind="ExternalInput").ap()
    outd = nc.dram_tensor("out", [T, H], F32, kind="ExternalOutput").ap()
    wqtd = nc.dram_tensor("wq_t", [H, H], F32, kind="ExternalInput").ap()
    wktd = nc.dram_tensor("wk_t", [H, H], F32, kind="ExternalInput").ap()
    wvtd = nc.dram_tensor("wv_t", [H, H], F32, kind="ExternalInput").ap()
    bq2d = nc.dram_tensor("bq2", [128, 8], F32, kind="ExternalInput").ap()
    bk2d = nc.dram_tensor("bk2", [128, 8], F32, kind="ExternalInput").ap()
    bvrd = nc.dram_tensor("bv_row", [1, H], F32, kind="ExternalInput").ap()
    wgd = nc.dram_tensor("w_gate", [II, H], F32, kind="ExternalInput").ap()
    wud = nc.dram_tensor("w_up", [II, H], F32, kind="ExternalInput").ap()
    wdd = nc.dram_tensor("w_down", [H, II], F32, kind="ExternalInput").ap()
    clipd = nc.dram_tensor("clip2", [1, 2], F32, kind="ExternalInput").ap()

    X1 = nc.dram_tensor("X1", [T, H], F32, kind=ext_dbg).ap()
    X2 = nc.dram_tensor("X2", [T, H], F32, kind=ext_dbg).ap()
    HP = nc.dram_tensor("HP", [IC, 128, T], F32, kind=ext_dbg).ap()
    QT = nc.dram_tensor("QTsp", [T // 128, 128, HC, 128], BF16, kind=ext_dbg).ap()
    QWG = nc.dram_tensor("QWG", [HC, 128, II], BF16).ap()
    QWU = nc.dram_tensor("QWU", [HC, 128, II], BF16).ap()
    QWD = nc.dram_tensor("QWD", [IC, 128, H], BF16).ap()
    WQB = nc.dram_tensor("WQB", [HC, 128, H], BF16).ap()
    WKB = nc.dram_tensor("WKB", [HC, 128, H], BF16).ap()
    WVB = nc.dram_tensor("WVB", [HC, 128, H], BF16).ap()
    SCAL = nc.dram_tensor("SCAL", [1, 16], F32).ap()
    RED = [nc.dram_tensor(f"RED{k}", [128, 1], F32).ap() for k in range(4)]
    CCI = [nc.dram_tensor(f"cci{k}", [1, 16], F32) for k in range(4)]
    CCO = [nc.dram_tensor(f"cco{k}", [1, 16], F32) for k in range(4)]
    if debug:
        KDd = nc.dram_tensor("KD", [128, HC, SEQ], BF16, kind="ExternalOutput").ap()
        VDd = nc.dram_tensor("VD", [128, NCHUNK, H], BF16, kind="ExternalOutput").ap()
        DBG1 = nc.dram_tensor("DBG1", [128, 256], F32, kind="ExternalOutput").ap()
        DBGM = nc.dram_tensor("DBGM", [1, 16], F32, kind="ExternalOutput").ap()
        DBGW = nc.dram_tensor("DBGW", [128, 2816], BF16, kind="ExternalOutput").ap()
        DBGQ = nc.dram_tensor("DBGQ", [128, HC, 512], BF16, kind="ExternalOutput").ap()
        X1C = nc.dram_tensor("X1C", [T, H], F32, kind="ExternalOutput").ap()
        QHD = nc.dram_tensor("QHD", [32, 2, 128, 256], BF16, kind="ExternalOutput").ap()

    identd = nc.inline_tensor(np.eye(128, dtype=ml_dtypes.bfloat16), name="identc")

    ARENA = 212832
    nc.alloc_sbuf_tensor("bigarena", [128, ARENA], mybir.dt.uint8)
    base = nc.lookup_mloc("bigarena").addr
    LIMIT = base + ARENA
    cur = [base]

    def at(name, shape, dtype, off=None):
        nbytes = int(np.prod(shape[1:])) * mybir.dt.size(dtype)
        if off is None:
            o = cur[0]
            cur[0] = o + ((nbytes + 31) // 32) * 32
        else:
            o = base + off
        assert o + nbytes <= LIMIT, f"SBUF overflow {name}: {o + nbytes - base}"
        return nc.alloc_sbuf_tensor_at(name, shape, dtype, offset=o)

    ident = at("ident", [128, 128], BF16)
    biasq = at("biasq", [128, 8], F32)
    biask = at("biask", [128, 8], F32)
    bvx = at("bvx", [128, H], F32)
    scal_b = at("scal_b", [128, 16], F32)
    m1 = at("m1", [128, 64], F32)
    r1 = at("r1", [128, 64], F32)
    rq1 = at("rq1", [128, 64], F32)
    m3 = at("m3", [128, 64], F32)
    r3 = at("r3", [128, 64], F32)
    rq3 = at("rq3", [128, 64], F32)
    swg = at("swg", [128, 22], F32)
    swu = at("swu", [128, 22], F32)
    gscale = at("gscale", [128, 22], F32)
    uscale = at("uscale", [128, 22], F32)
    amax = [at(f"amax{k}", [128, 1], F32) for k in range(4)]
    xm1 = at("xm1", [128, 64], F32)
    xm3 = at("xm3", [128, 64], F32)
    biasq8 = at("biasq8", [128, 8], F32)
    red_row = at("red_row", [1, 128], F32)
    misc = at("misc", [1, 16], F32)
    n30 = at("n30", [128, 1], F32)
    ssq3 = at("ssq3", [128, 64], F32)
    mx3 = at("mx3", [128, 64], F32)
    mn3 = at("mn3", [128, 64], F32)
    GLOB_END = cur[0] - base
    assert GLOB_END < 9500, GLOB_END

    WREG = ((GLOB_END + 63) // 64) * 64
    qwgT = at("qwgT", [128, HC, II], BF16, off=WREG)
    quT = at("quT", [128, HC, II], BF16, off=WREG + 45056)
    qwdT = at("qwdT", [128, IC, H], BF16, off=WREG + 90112)
    kT = at("kT", [128, HC, SEQ], BF16, off=WREG)
    Vt = at("Vt", [128, NCHUNK, H], BF16, off=WREG + 65536)
    WORK = WREG + 135168

    def wat(name, shape, dtype, off):
        return at(name, shape, dtype, off=WORK + off)

    wstage = [wat(f"wstage{i}", [128, II], F32, 11264 * i) for i in range(2)]
    qwnat = wat("qwnat", [128, II], BF16, 22528)
    qkvb16 = wat("qkvb16", [128, H], BF16, 28160)
    wsc = wat("wsc", [128, 8], F32, 30208)
    px = [wat("px0", [128, 4, H], F32, 30976)]
    ptmp = wat("ptmp", [128, H], F32, 47360)
    ptmp2 = wat("ptmp2", [128, H], F32, 51456)
    pcol = wat("pcol", [128, 4], F32, 55552)

    p2_x = [wat(f"p2x{i}", [128, 4, H], F32, 16384 * i) for i in range(2)]
    p2_qx = wat("p2qx", [128, 4, H], BF16, 32768)
    p2_qxT = wat("p2qxT", [128, HC, 512], BF16, 40960)
    p2_t = wat("p2t", [128, H], F32, 49152)
    p2_sg = [wat(f"p2sg{i}", [128, 512], F32, 53248 + 2048 * i) for i in range(2)]
    p2_hp = [wat(f"p2hp{i}", [128, 512], F32, 57344 + 2048 * i) for i in range(4)]
    p2_col = wat("p2col", [128, 24], F32, 65536)

    p3_hp = [wat(f"p3hp{i}", [128, 256], F32, 1024 * i) for i in range(4)]
    p3_qh = [wat(f"p3qh{i}", [128, 256], BF16, 4096 + 512 * i) for i in range(4)]
    p3_x0 = [wat("p3x0", [128, 2, H], F32, 6144)]
    p3_x1 = wat("p3x1", [128, 2, H], F32, 14336)

    b_wq = wat("bwq", [128, HC, H], BF16, 0)
    b_wk = wat("bwk", [128, HC, H], BF16, 16384)
    b_wv = wat("bwv", [128, HC, H], BF16, 32768)
    b_x1 = [wat("bx1", [128, 2, H], F32, 49152)]
    b_ln = wat("bln", [128, H], BF16, 57344)
    b_xT = wat("bxT", [128, HC, 256], BF16, 59392)
    b_col = wat("bcol", [128, 8], F32, 63488)
    b_pt = wat("bpt", [128, H], F32, 63520)
    b_qs = wat("bqs", [128, 2, HC, 128], BF16, 63520)

    a_kTe = wat("akTe", [128, HC, 128], BF16, 0)
    a_VtE = wat("aVtE", [128, H], BF16, 2048)
    a_qt = [wat(f"aqt{i}", [128, HC, 128], BF16, 4096 + 2048 * i) for i in range(2)]
    a_A = [wat(f"aA{i}", [128, 512], BF16, 8192 + 1024 * i) for i in range(4)]
    a_aT = [wat(f"aaT{i}", [128, 512], BF16, 12288 + 1024 * i) for i in range(2)]
    a_ctx = wat("actx", [128, H], F32, 14336)
    a_x1 = wat("ax1", [128, H], F32, 18432)
    a_x2 = wat("ax2", [128, H], F32, 22528)
    a_Sd = [wat(f"aS{i}", [128, SEQ], F32, 26624 + 16384 * i) for i in range(2)]
    a_AF = wat("aAF", [128, SEQ], BF16, 59392)
    a_xc = wat("axc", [128, H], F32, 59392)
    a_col = wat("acol", [128, 16], F32, 67584)
    g_col = wat("gcol", [128, 4], F32, 67648)

    pb = [nc.alloc_psum_tensor(f"pb{i}", [128, 512], F32) for i in range(8)]
    pbh = [p.ap().bitcast(BF16) for p in pb]

    s = Sched()

    def dma(q, out, in_, waits=(), sem=None, **kw):
        return s.op(q, lambda eng, o=out, i=in_, kw=kw: eng.dma_start(out=o, in_=i, **kw),
                    waits=waits, inc=(sem or QSEM[q]), n=16)

    def vop(fn, waits=()):
        return s.op("vector", fn, waits=waits, inc="tV")

    def gop(fn, waits=()):
        return s.op("gpsimd", fn, waits=waits, inc="tG")

    def gdrain():
        s.op("gpsimd", lambda eng: eng.drain())

    def sop(fn, waits=()):
        return s.op("scalar", fn, waits=waits, inc="tS")

    def top(fn, waits=()):
        return s.op("tensor", fn, waits=waits, inc="tP")

    def vdrain():
        s.op("vector", lambda eng: eng.drain())

    def sdrain():
        s.op("scalar", lambda eng: eng.drain())

    cast_tok = None
    spill_tok = None
    for wi, (src, dst) in enumerate(((wqtd, WQB), (wktd, WKB), (wvtd, WVB))):
        for hc in range(HC):
            b = hc % 2
            ldt = dma("sync", wstage[b][:, 0:H], src[128 * hc:128 * hc + 128, :],
                      waits=[cast_tok], sem="dw")
            cast_tok = vop(lambda eng, b=b: eng.tensor_copy(qkvb16[:], wstage[b][:, 0:H]),
                           waits=[ldt, spill_tok])
            spill_tok = dma("gpsimd", dst[hc], qkvb16[:], waits=[cast_tok], sem="dg")
    dma("sync", biasq[:], bq2d[:, :], sem="dk")
    dma("sync", biask[:], bk2d[:, :], sem="dk")
    bvx_ld = dma("sync", bvx[:], bvrd[0:1, :].broadcast_to((128, H)), sem="dk")
    dma("sync", ident[:], identd.ap(), sem="dk")
    kc_ld = dma("sync", misc[0:1, 0:2], clipd[0:1, :], sem="dk")
    id_ld = kc_ld
    clip_ld = kc_ld
    vop(lambda eng: eng.tensor_scalar(
        biasq8[:], biasq[:], 0.125, None, Alu.mult), waits=[kc_ld])
    sig_tok = sop(lambda eng: eng.activation(misc[0:1, 2:4], misc[0:1, 0:2],
                                             Act.Sigmoid), waits=[clip_ld])
    vop(lambda eng: eng.memset(n30[:], -1e30))

    def quant_weight_gu(srcd, dstT, sw_dst):
        q_tok = None
        ev_tok = None
        for ic in range(IC):
            b = ic % 2
            ldt = dma("sync", wstage[b][:, 0:H], srcd[128 * ic:128 * ic + 128, :],
                      waits=[q_tok], sem="dw")
            rm = vop(lambda eng, b=b: eng.reduce_max(wsc[:, 0:1], wstage[b][:, 0:H], AXX,
                                                     apply_absolute_value=True),
                     waits=[ldt])
            vdrain()
            vop(lambda eng, ic=ic, d=sw_dst: eng.tensor_scalar(
                d[:, ic:ic + 1], wsc[:, 0:1], 1.0 / 127.0, None, Alu.mult))
            vop(lambda eng: eng.reciprocal(wsc[:, 1:2], wsc[:, 0:1]))
            vdrain()
            vop(lambda eng, b=b: eng.tensor_scalar(
                wstage[b][:, 0:H], wstage[b][:, 0:H], wsc[:, 1:2], 127.0,
                Alu.mult, Alu.mult))
            q_tok = vop(lambda eng, b=b: eng.tensor_scalar(
                qwnat[:, 0:H], wstage[b][:, 0:H], MAGIC, MAGIC, Alu.add, Alu.subtract),
                waits=[ev_tok])
            for hcx in range(HC):
                pbk = 6 + (hcx % 2)
                tr = top(lambda eng, hcx=hcx, pbk=pbk: eng.transpose(
                    pbh[pbk][:, 0:128], qwnat[:, 128 * hcx:128 * hcx + 128], ident[:]),
                    waits=[q_tok, ev_tok, id_ld])
                ev_tok = vop(lambda eng, hcx=hcx, ic=ic, pbk=pbk, d=dstT: eng.tensor_copy(
                    d[:, hcx, 128 * ic:128 * ic + 128], pbh[pbk][:, 0:128]), waits=[tr])
        return ev_tok

    gu_ev = quant_weight_gu(wgd, qwgT, swg)
    gu_ev2 = quant_weight_gu(wud, quT, swu)

    def quant_weight_down():
        ev_tok = None
        q_tok = None
        for hcx in range(HC):
            b = hcx % 2
            ldt = dma("sync", wstage[b][:], wdd[128 * hcx:128 * hcx + 128, :],
                      waits=[q_tok], sem="dw")
            rm = vop(lambda eng, b=b: eng.reduce_max(wsc[:, 0:1], wstage[b][:], AXX,
                                                     apply_absolute_value=True),
                     waits=[ldt])
            vdrain()
            vop(lambda eng: eng.reciprocal(wsc[:, 1:2], wsc[:, 0:1]))
            vop(lambda eng: eng.tensor_scalar(
                wsc[:, 2:3], wsc[:, 0:1], 1.0 / 127.0, None, Alu.mult))
            vdrain()
            vop(lambda eng, b=b: eng.tensor_scalar(
                wstage[b][:], wstage[b][:], wsc[:, 1:2], 127.0, Alu.mult, Alu.mult))
            vop(lambda eng, b=b: eng.tensor_scalar(
                wstage[b][:], wstage[b][:], MAGIC, MAGIC, Alu.add, Alu.subtract))
            q_tok = vop(lambda eng, b=b: eng.tensor_scalar(
                qwnat[:], wstage[b][:], wsc[:, 2:3], None, Alu.mult),
                waits=[ev_tok])
            for ic in range(IC):
                pbk = 6 + (ic % 2)
                tr = top(lambda eng, ic=ic, pbk=pbk: eng.transpose(
                    pbh[pbk][:, 0:128], qwnat[:, 128 * ic:128 * ic + 128], ident[:]),
                    waits=[q_tok, ev_tok, id_ld])
                ev_tok = vop(lambda eng, ic=ic, hcx=hcx, pbk=pbk: eng.tensor_copy(
                    qwdT[:, ic, 128 * hcx:128 * hcx + 128], pbh[pbk][:, 0:128]),
                    waits=[tr])
        return ev_tok

    dn_ev = quant_weight_down()
    for hc in range(HC):
        dma("gpsimd", QWG[hc], qwgT[:, hc, :], waits=[gu_ev], sem="dg")
        dma("gpsimd", QWU[hc], quT[:, hc, :], waits=[gu_ev2], sem="dg")
    for ic in range(IC):
        dma("gpsimd", QWD[ic], qwdT[:, ic, :], waits=[dn_ev], sem="dg")

    ld_tok = [None, None]
    q_done = [None, None]
    sq_prev = None
    for t in range(16):
        b = 0
        ld_tok[b] = dma("sync", px[b][:],
                        x0d[512 * t:512 * t + 512, :].rearrange("(a p) h -> p a h", p=128),
                        waits=[q_done[b]], sem="dx")
        for sub in range(4):
            tt = 4 * t + sub
            vop(lambda eng, b=b, sub=sub: eng.tensor_reduce(
                pcol[:, 0:1], px[b][:, sub, :], AXX, Alu.add), waits=[ld_tok[b]])
            vdrain()
            vop(lambda eng, tt=tt: eng.tensor_scalar(
                m1[:, tt:tt + 1], pcol[:, 0:1], 1.0 / float(H), None, Alu.mult))
            vdrain()
            xc_t = vop(lambda eng, b=b, sub=sub, tt=tt: eng.tensor_scalar(
                ptmp[:], px[b][:, sub, :], m1[:, tt:tt + 1], None, Alu.subtract),
                waits=[sq_prev])
            sq_prev = sop(lambda eng: eng.activation(ptmp2[:], ptmp[:], Act.Square,
                                                     accum_out=pcol[:, 1:2]),
                          waits=[xc_t])
            vop(lambda eng: eng.tensor_scalar(
                pcol[:, 1:2], pcol[:, 1:2], 1.0 / float(H), 1e-5, Alu.mult, Alu.add),
                waits=[sq_prev])
            vdrain()
            vop(lambda eng: eng.reciprocal(pcol[:, 1:2], pcol[:, 1:2]))
            vdrain()
            r_t = sop(lambda eng, tt=tt: eng.activation(
                r1[:, tt:tt + 1], pcol[:, 1:2], Act.Sqrt), waits=[s.now("tV")])
            q_done[b] = vop(lambda eng, tt=tt: eng.reduce_max(
                xm1[:, tt:tt + 1], ptmp[:], AXX,
                apply_absolute_value=True), waits=[sq_prev])
    vdrain()
    vop(lambda eng: eng.tensor_tensor(xm1[:], xm1[:], r1[:], Alu.mult),
        waits=[s.now("tS")])
    vdrain()
    vop(lambda eng: eng.reduce_max(amax[0][:], xm1[:], AXX))
    vdrain()

    def allreduce_scalar(k, amax_tok, extra_cols):
        sig_col = 2 if k in (0, 2) else 3
        t1 = dma("gpsimd", RED[k][:, :], amax[k][:, 0:1], waits=[amax_tok], sem="dc")
        t2 = dma("gpsimd", red_row[0:1, :], RED[k].transpose([1, 0]), waits=[t1],
                 allow_slow_non_contiguous=True, sem="dc")
        mx = vop(lambda eng: eng.reduce_max(misc[0:1, 4:5], red_row[0:1, :], AXX),
                 waits=[t2])
        t3 = dma("gpsimd", CCI[k].ap()[0:1, 0:1], misc[0:1, 4:5], waits=[mx], sem="dc")
        cc_t = s.op("gpsimd", lambda eng, k=k: eng.collective_compute(
            "AllReduce", Alu.max, replica_groups=[list(range(NCORE))],
            ins=[CCI[k].ap().opt()], outs=[CCO[k].ap().opt()]),
            waits=[t3], inc="cc", n=1)
        t4 = dma("gpsimd", misc[0:1, 4:5], CCO[k].ap()[0:1, 0:1], waits=[cc_t], sem="dc")
        vop(lambda eng, sc=sig_col: eng.tensor_tensor(
            misc[0:1, 5:6], misc[0:1, 4:5], misc[0:1, sc:sc + 1], Alu.mult),
            waits=[t4, sig_tok])
        vdrain()
        vop(lambda eng: eng.tensor_scalar(
            misc[0:1, 6:7], misc[0:1, 5:6], 1.0 / 127.0, None, Alu.mult))
        vdrain()
        rsv = vop(lambda eng: eng.reciprocal(misc[0:1, 7:8], misc[0:1, 6:7]))
        shv = vop(lambda eng: eng.tensor_scalar(
            misc[0:1, 8:9], misc[0:1, 6:7], 0.5, None, Alu.mult))
        vdrain()
        toks = []
        for (col, tag) in extra_cols:
            srcc = {"s": 6, "rs": 7, "sh": 8}[tag]
            t5 = dma("gpsimd", SCAL[0:1, col:col + 1], misc[0:1, srcc:srcc + 1],
                     waits=[rsv, shv], sem="dc")
            toks.append(dma("gpsimd", scal_b[:, col:col + 1],
                            SCAL[0:1, col:col + 1].broadcast_to((128, 1)), waits=[t5],
                            sem="dc"))
        return toks

    bc1 = allreduce_scalar(0, s.now("tV"), [(C_S1, "s"), (C_RS1, "rs")])
    vop(lambda eng: eng.tensor_scalar(
        rq1[:], r1[:], scal_b[:, C_RS1:C_RS1 + 1], None, Alu.mult), waits=bc1)
    vop(lambda eng: eng.tensor_scalar(
        gscale[:], swg[:], scal_b[:, C_S1:C_S1 + 1], None, Alu.mult),
        waits=bc1 + [gu_ev])
    vop(lambda eng: eng.tensor_scalar(
        uscale[:], swu[:], scal_b[:, C_S1:C_S1 + 1], None, Alu.mult),
        waits=bc1 + [gu_ev2])
    s.full_barrier()

    def pass2(xsrc, mcache, rqcache, amax_idx):
        vop(lambda eng, k=amax_idx: eng.memset(amax[k][:], 0.0))
        ld_tok = [None, None]
        lastq = [None, None]
        silu_tok = [None, None]
        hp_tok = [None, None, None, None]
        hpdma_tok = [None, None, None, None]
        evq_tok = [None, None]
        tr_last = None
        amax_tok = None
        for t in range(16):
            b = t % 2
            ld_tok[b] = dma("sync", p2_x[b][:],
                            xsrc[512 * t:512 * t + 512, :].rearrange("(a p) h -> p a h",
                                                                     p=128),
                            waits=[lastq[b]], sem=f"dx{b}")
            for sub in range(4):
                tt = 4 * t + sub
                vop(lambda eng, b=b, sub=sub, tt=tt, mc=mcache, rc=rqcache:
                    eng.tensor_scalar(p2_t[:], p2_x[b][:, sub, :], mc[:, tt:tt + 1],
                                      rc[:, tt:tt + 1], Alu.subtract, Alu.mult),
                    waits=[ld_tok[b]] + ([tr_last] if sub == 0 else []))
                vop(lambda eng: eng.tensor_scalar(
                    p2_t[:], p2_t[:], MAGIC, MAGIC, Alu.add, Alu.subtract))
                lastq[b] = vop(lambda eng, sub=sub: eng.tensor_scalar(
                    p2_qx[:, sub, :], p2_t[:], 127.0, -128.0, Alu.min, Alu.max))
            for sub in range(4):
                for hc in range(HC):
                    j = (sub * HC + hc) % 2
                    pbk = 4 + j
                    tr = top(lambda eng, sub=sub, hc=hc, pbk=pbk: eng.transpose(
                        pbh[pbk][:, 0:128], p2_qx[:, sub, 128 * hc:128 * hc + 128],
                        ident[:]), waits=[lastq[b], evq_tok[j]])
                    evq_tok[j] = vop(lambda eng, sub=sub, hc=hc, pbk=pbk: eng.tensor_copy(
                        p2_qxT[:, hc, 128 * sub:128 * sub + 128], pbh[pbk][:, 0:128]),
                        waits=[tr])
            tr_ev = [evq_tok[0], evq_tok[1]]
            mm_g = None
            mm_u = None
            for ic in range(IC):
                pg = ic % 2
                pu = 2 + ic % 2
                for hc in range(HC):
                    mm_g = top(lambda eng, ic=ic, hc=hc, pg=pg: eng.matmul(
                        pb[pg][:], qwgT[:, hc, 128 * ic:128 * ic + 128],
                        p2_qxT[:, hc, :],
                        start=(hc == 0), stop=(hc == HC - 1)),
                        waits=(tr_ev + [silu_tok[ic % 2]]) if hc == 0 else [])
                for hc in range(HC):
                    mm_u = top(lambda eng, ic=ic, hc=hc, pu=pu: eng.matmul(
                        pb[pu][:], quT[:, hc, 128 * ic:128 * ic + 128],
                        p2_qxT[:, hc, :],
                        start=(hc == 0), stop=(hc == HC - 1)),
                        waits=[hp_tok[(ic + 2) % 4]] if hc == 0 else [])
                silu_tok[ic % 2] = sop(lambda eng, ic=ic, pg=pg: eng.activation(
                    p2_sg[ic % 2][:], pb[pg][:], Act.Silu, scale=gscale[:, ic:ic + 1]),
                    waits=[mm_g, hp_tok[(ic + 2) % 4]])
                hb = ic % 4
                hp_tok[hb] = vop(lambda eng, ic=ic, pu=pu, hb=hb:
                                 eng.scalar_tensor_tensor(
                                     p2_hp[hb][:], pb[pu][:], uscale[:, ic:ic + 1],
                                     p2_sg[ic % 2][:], Alu.mult, Alu.mult),
                                 waits=[mm_u, silu_tok[ic % 2], hpdma_tok[hb]])
                vop(lambda eng, hb=hb, ic=ic: eng.reduce_max(
                    p2_col[:, ic:ic + 1], p2_hp[hb][:], AXX,
                    apply_absolute_value=True))
                hpdma_tok[hb] = dma("gpsimd", HP[ic, :, 512 * t:512 * t + 512],
                                    p2_hp[hb][:], waits=[hp_tok[hb]], sem=f"dg{hb}")
            vdrain()
            vop(lambda eng: eng.reduce_max(p2_col[:, 22:23], p2_col[:, 0:IC], AXX))
            vdrain()
            amax_tok = vop(lambda eng, k=amax_idx: eng.tensor_tensor(
                amax[k][:], amax[k][:], p2_col[:, 22:23], Alu.max))
            vdrain()
            tr_last = mm_g
        return amax_tok

    amax2_tok = pass2(x0d, m1, rq1, 1)
    bc2 = allreduce_scalar(1, amax2_tok, [(C_RS2, "rs"), (C_S2H, "sh")])
    s.full_barrier()
    if debug:
        dma("gpsimd", DBG1[:, 0:1], amax[0][:], allow_slow_non_contiguous=True, sem="dg")
        dma("gpsimd", DBG1[:, 1:2], amax[1][:], allow_slow_non_contiguous=True, sem="dg")
        dma("gpsimd", DBG1[:, 16:32], scal_b[:], allow_slow_non_contiguous=True, sem="dg")
        dma("gpsimd", DBG1[:, 32:96], m1[:], allow_slow_non_contiguous=True, sem="dg")
        dma("gpsimd", DBG1[:, 96:160], r1[:], allow_slow_non_contiguous=True, sem="dg")
        dma("gpsimd", DBG1[:, 160:224], rq1[:], allow_slow_non_contiguous=True, sem="dg")
        dma("gpsimd", DBG1[:, 224:246], gscale[:], allow_slow_non_contiguous=True, sem="dg")
        dma("gpsimd", DBGM[:, :], misc[:], sem="dg")
        dma("gpsimd", DBGW[:, :], qwgT[:, 0, :], sem="dg")
        dma("gpsimd", DBGQ[:, :, :], p2_qxT[:], sem="dg")
        s.full_barrier()

    if stage("pass3a"):
        def pass3a(xres, dstd, shcol, rscol):
            ld_tok = [None] * 4
            qh_tok = [None] * 4
            mmh = [None] * 4
            mm_tok = None
            ev_toks = [None] * 4
            stq = None
            last_store = None
            for u in range(32):
                x0ld = dma("sync", p3_x0[0][:],
                           xres[256 * u:256 * u + 256, :].rearrange(
                               "(a p) h -> p a h", p=128),
                           waits=[stq], sem="dx")
                for ic in range(IC):
                    hb = ic % 4
                    ld_tok[hb] = dma("sync", p3_hp[hb][:],
                                     HP[ic, :, 256 * u:256 * u + 256],
                                     waits=[qh_tok[hb]], sem=f"dh{hb}")
                    vop(lambda eng, hb=hb, rs=rscol: eng.tensor_scalar(
                        p3_hp[hb][:], p3_hp[hb][:], scal_b[:, rs:rs + 1], MAGIC,
                        Alu.mult, Alu.add), waits=[ld_tok[hb]])
                    vop(lambda eng, hb=hb: eng.tensor_scalar(
                        p3_hp[hb][:], p3_hp[hb][:], MAGIC, 127.0,
                        Alu.subtract, Alu.min))
                    qh_tok[hb] = vop(lambda eng, hb=hb: eng.tensor_scalar(
                        p3_qh[hb][:], p3_hp[hb][:], -128.0, None, Alu.max),
                        waits=[mmh[hb]])
                    if debug and ic in (0, 1) and dstd is X1:
                        dma("gpsimd", QHD[u, ic], p3_qh[hb][:], waits=[qh_tok[hb]],
                            sem="dg")
                    for th in range(2):
                        for fh in range(2):
                            pbk = 2 * th + fh
                            mm_tok = top(lambda eng, ic=ic, th=th, fh=fh, hb=hb,
                                         pbk=pbk:
                                         eng.matmul(
                                             pb[pbk][:],
                                             p3_qh[hb][:, 128 * th:128 * th + 128],
                                             qwdT[:, ic, 512 * fh:512 * fh + 512],
                                             start=(ic == 0), stop=(ic == IC - 1)),
                                         waits=[qh_tok[hb]] +
                                         ([ev_toks[pbk]] if ic == 0 else []))
                    mmh[hb] = mm_tok
                for th in range(2):
                    for fh in range(2):
                        pbk = 2 * th + fh
                        ev_toks[pbk] = vop(
                            lambda eng, th=th, fh=fh, pbk=pbk, sh=shcol:
                            eng.scalar_tensor_tensor(
                                p3_x1[:, th, 512 * fh:512 * fh + 512],
                                pb[pbk][:], scal_b[:, sh:sh + 1],
                                p3_x0[0][:, th, 512 * fh:512 * fh + 512],
                                Alu.mult, Alu.add),
                            waits=[mm_tok, x0ld, stq])
                stq = dma("gpsimd",
                          dstd[256 * u:256 * u + 256, :].rearrange(
                              "(a p) h -> p a h", p=128),
                          p3_x1[:], waits=[ev_toks[3], ev_toks[0], ev_toks[1],
                                           ev_toks[2]], sem="dg")
                last_store = stq
            return last_store

        pass3a(x0d, X1, C_S2H, C_RS2)
        s.full_barrier()
        if debug:
            for t in range(16):
                dma("gpsimd", X1C[512 * t:512 * t + 512, :], X1[512 * t:512 * t + 512, :],
                    sem="dg")
            s.full_barrier()

    if stage("pass3b"):
        def pass3b(seq):
            base_t = SEQ * seq
            wtoks = []
            for hc in range(HC):
                wtoks.append(dma("sync", b_wq[:, hc, :], WQB[hc], sem="dw"))
                wtoks.append(dma("sync", b_wk[:, hc, :], WKB[hc], sem="dw"))
                wtoks.append(dma("sync", b_wv[:, hc, :], WVB[hc], sem="dw"))
            wq_ld = wtoks[-1]
            lastuse = None
            ev_xT = [None, None]
            qk_ev = [None, None]
            v_ev = [None, None]
            qs_dma = None
            sq_prev2 = None
            k_last = None
            v_last = None
            for u in range(16):
                ld_tok = dma("sync", b_x1[0][:],
                             X1[base_t + 256 * u: base_t + 256 * u + 256, :]
                             .rearrange("(a p) h -> p a h", p=128),
                             waits=[lastuse], sem="dx")
                for th in range(2):
                    vop(lambda eng, th=th: eng.tensor_reduce(
                        b_col[:, 0:1], b_x1[0][:, th, :], AXX, Alu.add),
                        waits=[ld_tok])
                    vdrain()
                    vop(lambda eng: eng.tensor_scalar(
                        b_col[:, 1:2], b_col[:, 0:1], 1.0 / float(H), None, Alu.mult))
                    vdrain()
                    xc_t = vop(lambda eng, th=th: eng.tensor_scalar(
                        b_pt[:], b_x1[0][:, th, :], b_col[:, 1:2], None, Alu.subtract),
                        waits=[sq_prev2, qs_dma])
                    sq_prev2 = sop(lambda eng: eng.activation(
                        b_ln[:], b_pt[:], Act.Square, accum_out=b_col[:, 2:3]),
                        waits=[xc_t, ev_xT[0], ev_xT[1]])
                    vop(lambda eng: eng.tensor_scalar(
                        b_col[:, 2:3], b_col[:, 2:3], 1.0 / float(H), 1e-5,
                        Alu.mult, Alu.add), waits=[sq_prev2])
                    vdrain()
                    vop(lambda eng: eng.reciprocal(b_col[:, 2:3], b_col[:, 2:3]))
                    vdrain()
                    r_t = sop(lambda eng: eng.activation(
                        b_col[:, 3:4], b_col[:, 2:3], Act.Sqrt), waits=[s.now("tV")])
                    lastuse = vop(lambda eng: eng.tensor_scalar(
                        b_ln[:], b_pt[:], b_col[:, 3:4], None, Alu.mult),
                        waits=[r_t])
                    for hc in range(HC):
                        j = hc % 2
                        pbk = 6 + j
                        tr = top(lambda eng, hc=hc, pbk=pbk: eng.transpose(
                            pbh[pbk][:, 0:128], b_ln[:, 128 * hc:128 * hc + 128],
                            ident[:]), waits=[lastuse, ev_xT[j]])
                        ev_xT[j] = vop(lambda eng, th=th, hc=hc, pbk=pbk:
                                       eng.tensor_copy(
                                           b_xT[:, hc, 128 * th:128 * th + 128],
                                           pbh[pbk][:, 0:128]), waits=[tr])
                exT = [ev_xT[0], ev_xT[1]]
                mm_qk = None
                for which in range(2):
                    wsrc = b_wq if which == 0 else b_wk
                    for fc in range(HC):
                        pbk = which
                        for hc in range(HC):
                            mm_qk = top(lambda eng, wsrc=wsrc, fc=fc, hc=hc, pbk=pbk:
                                        eng.matmul(
                                            pb[pbk][:, 0:256],
                                            wsrc[:, hc, 128 * fc:128 * fc + 128],
                                            b_xT[:, hc, :],
                                            start=(hc == 0), stop=(hc == HC - 1)),
                                        waits=(exT + [qk_ev[pbk], wq_ld])
                                        if hc == 0 else [])
                        if which == 0:
                            qk_ev[pbk] = sop(lambda eng, fc=fc, pbk=pbk: eng.activation(
                                b_qs[:, :, fc, :], pb[pbk][:, 0:256], Act.Identity,
                                bias=biasq8[:, fc:fc + 1], scale=0.125),
                                waits=[mm_qk] + ([qs_dma] if fc == 0 else []))
                        else:
                            qk_ev[pbk] = sop(lambda eng, fc=fc, pbk=pbk, u=u:
                                             eng.activation(
                                                 kT[:, fc, 256 * u:256 * u + 256],
                                                 pb[pbk][:, 0:256], Act.Identity,
                                                 bias=biask[:, fc:fc + 1]),
                                             waits=[mm_qk])
                            k_last = qk_ev[pbk]
                for j in range(2):
                    blk = 32 * seq + 2 * u + j
                    qs_dma = dma("gpsimd", QT[blk], b_qs[:, j], waits=[qk_ev[0]],
                                 sem="dg")
                mm_v = None
                for th in range(2):
                    for fh in range(2):
                        pbk = 2 + fh
                        for hc in range(HC):
                            mm_v = top(lambda eng, th=th, fh=fh, hc=hc, pbk=pbk:
                                       eng.matmul(
                                           pb[pbk][:],
                                           b_xT[:, hc, 128 * th:128 * th + 128],
                                           b_wv[:, hc, 512 * fh:512 * fh + 512],
                                           start=(hc == 0), stop=(hc == HC - 1)),
                                       waits=(exT + [v_ev[fh], wq_ld])
                                       if hc == 0 else [])
                        ch = 2 * u + th
                        v_ev[fh] = vop(lambda eng, ch=ch, fh=fh, pbk=pbk:
                                       eng.tensor_tensor(
                                           Vt[:, ch, 512 * fh:512 * fh + 512],
                                           pb[pbk][:], bvx[:, 512 * fh:512 * fh + 512],
                                           Alu.add), waits=[mm_v, bvx_ld])
                        v_last = v_ev[fh]
            return k_last, v_last

        kv_a = pass3b(0)
        if debug:
            for fc in range(HC):
                dma("gpsimd", KDd[:, fc, :], kT[:, fc, :],
                    waits=[kv_a[0], kv_a[1]], sem="dg")
            for ch in range(NCHUNK):
                dma("gpsimd", VDd[:, ch, :], Vt[:, ch, :],
                    waits=[kv_a[0], kv_a[1]], sem="dg")
        s.full_barrier()

    if stage("attn"):

        def attention(seq, kv):
            base_t = SEQ * seq
            e1 = dma("gpsimd", a_VtE[0:64, :], Vt[0:64, 0, :], waits=list(kv),
                     sem="de")
            e2 = dma("gpsimd", a_VtE[64:128, :], Vt[64:128, NCHUNK - 1, :],
                     waits=list(kv), sem="de")
            e3 = vop(lambda eng: eng.tensor_copy(a_kTe[:, :, 0:64], kT[:, :, 0:64]),
                     waits=list(kv))
            e4 = vop(lambda eng: eng.tensor_copy(a_kTe[:, :, 64:128],
                                                 kT[:, :, SEQ - 64:SEQ]))

            st = {
                "qld": [None, None],
                "qt_use": [None, None],
                "s_ev": [None, None, None, None],
                "tr_ev": [None, None],
                "A_use": [None, None, None, None],
                "ctx_ev": [None, None],
                "resid_x1": None, "resid_st": None, "resid_xc": None,
                "resid_sq": None, "resid_rm": None,
            }

            def load_qt(tt):
                b = tt % 2
                qld_t = dma("sync", a_qt[b][:], QT[32 * seq + tt],
                            waits=[st["qt_use"][b], st["qld"][b]], sem=f"dq{b}")
                st["qld"][b] = qld_t
                return b, qld_t

            def do_mid(tt, ctx_dst, gate_dst):
                b, qld_t = load_qt(tt)
                vc = tt - 1
                segs = [("edge", None), ("win", vc), ("win", vc + 1), ("win", vc + 2)]
                mask_list = [(0, 64, 128, 192), (0, 64, 384, 512),
                             (64, 128, 128, 256), (64, 128, 448, 512)]
                SBANK = (0, 1, 6, 7)
                out_toks = []

                def stage1(j):
                    hp, hh = divmod(j, 2)
                    p0 = 64 * hh
                    sl = j % 4
                    pS = SBANK[sl]
                    top(lambda eng, p0=p0, hp=hp, pS=pS, b=b:
                        eng.matmul(pb[pS][:, 0:128],
                                   a_qt[b][p0:p0 + 64, hp, :],
                                   a_kTe[p0:p0 + 64, hp, :],
                                   start=True, stop=False),
                        waits=[qld_t, st["s_ev"][sl], kv[0], kv[1], e4])
                    k0 = 128 * vc
                    mmt = top(lambda eng, p0=p0, hp=hp, pS=pS, b=b, k0=k0:
                              eng.matmul(pb[pS][:, 128:512],
                                         a_qt[b][p0:p0 + 64, hp, :],
                                         kT[p0:p0 + 64, hp, k0:k0 + 384],
                                         start=False, stop=True,
                                         skip_group_check=True))
                    st["qt_use"][b] = mmt
                    mt = None
                    for (r0, r1_, c0, c1) in mask_list:
                        mt = vop(lambda eng, pS=pS, r0=r0, r1_=r1_, c0=c0, c1=c1:
                                 eng.memset(pb[pS][r0:r1_, c0:c1], -1e30),
                                 waits=[mmt])
                    nm = vop(lambda eng, pS=pS, sl=sl: eng.reduce_max(
                        a_col[:, sl:sl + 1], pb[pS][:], AXX, negate=True),
                        waits=[mt])
                    ex = sop(lambda eng, pS=pS, sl=sl: eng.activation(
                        a_A[sl][:], pb[pS][:], Act.Exp,
                        bias=a_col[:, sl:sl + 1],
                        accum_out=a_col[:, 4 + sl:5 + sl]),
                        waits=[nm, st["A_use"][sl]])
                    st["s_ev"][sl] = ex
                    return ex

                def stage2(j, ex):
                    hp, hh = divmod(j, 2)
                    h = 2 * hp + hh
                    sl = j % 4
                    rc = vop(lambda eng, sl=sl: eng.reciprocal(
                        a_col[:, 8 + sl:9 + sl], a_col[:, 4 + sl:5 + sl]),
                        waits=[ex, st["ctx_ev"][hh]])
                    cmm = None
                    for ci in range(4):
                        pT = 2 + (ci % 2)
                        tr = top(lambda eng, sl=sl, ci=ci, pT=pT: eng.transpose(
                            pbh[pT][:, 0:128],
                            a_A[sl][:, 128 * ci:128 * ci + 128], ident[:]),
                            waits=[ex, st["tr_ev"][ci % 2]])
                        st["A_use"][sl] = tr
                        tre = vop(lambda eng, hh=hh, ci=ci, pT=pT:
                                  eng.tensor_copy(
                                      a_aT[hh][:, 128 * ci:128 * ci + 128],
                                      pbh[pT][:, 0:128]), waits=[tr])
                        st["tr_ev"][ci % 2] = tre
                        styp, c = segs[ci]
                        pC = 4 + hh
                        extra = [st["ctx_ev"][hh]] if ci == 0 else []
                        if ci == 0 and hh == 1 and st.get("sev2"):
                            extra.append(st.pop("sev2"))
                        if styp == "edge":
                            cmm = top(lambda eng, hh=hh, ci=ci, h=h, pC=pC:
                                      eng.matmul(
                                          pb[pC][:, 0:64],
                                          a_aT[hh][:, 128 * ci:128 * ci + 128],
                                          a_VtE[:, 64 * h:64 * h + 64],
                                          start=(ci == 0), stop=(ci == 3)),
                                      waits=[tre, e2] + extra)
                        else:
                            cmm = top(lambda eng, hh=hh, ci=ci, h=h, c=c, pC=pC:
                                      eng.matmul(
                                          pb[pC][:, 0:64],
                                          a_aT[hh][:, 128 * ci:128 * ci + 128],
                                          Vt[:, c, 64 * h:64 * h + 64],
                                          start=(ci == 0), stop=(ci == 3)),
                                      waits=[tre] + extra)
                    cev = sop(lambda eng, h=h, sl=sl, pC=4 + hh, dst=ctx_dst:
                              eng.activation(dst[:, 64 * h:64 * h + 64],
                                             pb[pC][:, 0:64], Act.Identity,
                                             scale=a_col[:, 8 + sl:9 + sl]),
                              waits=[cmm, rc, gate_dst])
                    st["ctx_ev"][hh] = cev
                    out_toks.append(cev)

                exs = {}
                for j in range(4):
                    exs[j] = stage1(j)
                for j in range(4, 16):
                    stage2(j - 4, exs[j - 4])
                    exs[j] = stage1(j)
                for j in range(12, 16):
                    stage2(j, exs[j])
                return out_toks

            def do_full(tt, ctx_dst, gate_dst):
                b, qld_t = load_qt(tt)
                if tt == 0:
                    mrow, mcol = (64, 128), (192, 4032)
                else:
                    mrow, mcol = (0, 64), (64, 3904)
                out_toks = []
                sev = [None, None, None]
                SB = (0, 1, 5)
                exF = [None, None]
                rcF = [None, None]
                trlast = [None]

                def s1F(h):
                    hp, hh = divmod(h, 2)
                    p0 = 64 * hh
                    sb = h % 2
                    aS = a_Sd[sb]
                    mmt = None
                    for kc in range(8):
                        pS = SB[kc % 3]
                        mmt = top(lambda eng, hp=hp, p0=p0, kc=kc, pS=pS, b=b:
                                  eng.matmul(
                                      pb[pS][:], a_qt[b][p0:p0 + 64, hp, :],
                                      kT[p0:p0 + 64, hp, 512 * kc:512 * kc + 512],
                                      start=True, stop=True),
                                  waits=[qld_t, sev[kc % 3],
                                         st["ctx_ev"][0], st["ctx_ev"][1],
                                         st["s_ev"][0], st["s_ev"][1]]
                                  if kc < 3 else [sev[kc % 3]])
                        sev[kc % 3] = vop(lambda eng, kc=kc, pS=pS, aS=aS:
                                          eng.tensor_copy(
                                              aS[:, 512 * kc:512 * kc + 512],
                                              pb[pS][:]),
                                          waits=[mmt] +
                                          ([exF[sb]] if kc == 0 else []))
                    st["qt_use"][b] = mmt
                    vop(lambda eng, aS=aS: eng.memset(
                        aS[mrow[0]:mrow[1], mcol[0]:mcol[1]], -1e30))
                    nm = vop(lambda eng, aS=aS, sb=sb: eng.reduce_max(
                        a_col[:, 0 + sb:1 + sb], aS[:], AXX, negate=True),
                        waits=[exF[sb]])
                    return nm

                def s2F(h, nm):
                    hp, hh = divmod(h, 2)
                    sb = h % 2
                    aS = a_Sd[sb]
                    ex = sop(lambda eng, aS=aS, sb=sb: eng.activation(
                        a_AF[:], aS[:], Act.Exp, bias=a_col[:, 0 + sb:1 + sb],
                        accum_out=a_col[:, 4 + sb:5 + sb]),
                        waits=[nm, trlast[0], rcF[sb], st["resid_rm"]])
                    exF[sb] = ex
                    rc = vop(lambda eng, sb=sb: eng.reciprocal(
                        a_col[:, 8 + sb:9 + sb], a_col[:, 4 + sb:5 + sb]),
                        waits=[ex, st["ctx_ev"][0]])
                    rcF[sb] = rc
                    cmm = None
                    for ci in range(NCHUNK):
                        j = ci % 2
                        pT = 2 + j
                        tr = top(lambda eng, ci=ci, pT=pT: eng.transpose(
                            pbh[pT][:, 0:128], a_AF[:, 128 * ci:128 * ci + 128],
                            ident[:]), waits=[ex, st["tr_ev"][j]])
                        trlast[0] = tr
                        tre = vop(lambda eng, j=j, pT=pT: eng.tensor_copy(
                            a_aT[j][:, 0:128], pbh[pT][:, 0:128]), waits=[tr])
                        st["tr_ev"][j] = tre
                        cmm = top(lambda eng, ci=ci, h=h, j=j: eng.matmul(
                            pb[4][:, 0:64], a_aT[j][:, 0:128],
                            Vt[:, ci, 64 * h:64 * h + 64],
                            start=(ci == 0), stop=(ci == NCHUNK - 1)),
                            waits=[tre] + ([st["ctx_ev"][0]] if ci == 0 else []))
                    cev = sop(lambda eng, h=h, sb=sb, dst=ctx_dst: eng.activation(
                        dst[:, 64 * h:64 * h + 64], pb[4][:, 0:64], Act.Identity,
                        scale=a_col[:, 8 + sb:9 + sb]), waits=[cmm, rc, gate_dst])
                    st["ctx_ev"][0] = cev
                    st["ctx_ev"][1] = cev
                    out_toks.append(cev)

                nms = {}
                nms[0] = s1F(0)
                for h in range(1, NH):
                    nms[h] = s1F(h)
                    s2F(h - 1, nms[h - 1])
                s2F(NH - 1, nms[NH - 1])
                st["s_ev"][0] = sev[0]
                st["s_ev"][1] = sev[1]
                st["sev2"] = sev[2]
                return out_toks

            def resid(tt, ctx_toks):
                xt = dma("sync", a_x1[:],
                         X1[base_t + 128 * tt:base_t + 128 * tt + 128, :],
                         waits=[st["resid_x1"]], sem="dr")
                rst = st["resid_st"]
                rst = list(rst) if isinstance(rst, tuple) else [rst]
                at_ = gop(lambda eng: eng.tensor_tensor(a_x2[:], a_x1[:],
                                                        a_ctx[:], Alu.add),
                          waits=[xt] + rst + ctx_toks)
                st["resid_x1"] = at_
                ttg = NT128S * seq + tt
                stx = dma("gpsimd",
                          X2[base_t + 128 * tt:base_t + 128 * tt + 128, :],
                          a_x2[:], sem="dg")
                sm = sop(lambda eng, ttg=ttg: eng.activation(
                    a_ctx[:], a_x2[:], Act.Identity, accum_out=m3[:, ttg:ttg + 1]),
                    waits=[at_, st["resid_sq"]])
                sq = sop(lambda eng, ttg=ttg: eng.activation(
                    a_ctx[:], a_x2[:], Act.Square, accum_out=ssq3[:, ttg:ttg + 1]))
                st["resid_sq"] = sq
                v1 = vop(lambda eng, ttg=ttg: eng.reduce_max(
                    mx3[:, ttg:ttg + 1], a_x2[:], AXX), waits=[at_])
                v2 = vop(lambda eng, ttg=ttg: eng.tensor_reduce(
                    mn3[:, ttg:ttg + 1], a_x2[:], AXX, Alu.min))
                st["resid_rm"] = v2
                st["resid_st"] = (stx, sm, sq, v2)
                st["resid_xc"] = sq
                return sq, stx

            free_tok = None
            laststx = None
            for tt in range(NT128S):
                if tt in (0, NT128S - 1):
                    toks = do_full(tt, a_ctx, free_tok)
                else:
                    toks = do_mid(tt, a_ctx, free_tok)
                free_tok, laststx = resid(tt, toks)
            return laststx

        attention(0, kv_a)
        s.full_barrier()
        kv_b = pass3b(1)
        s.full_barrier()
        attention(1, kv_b)
        vop(lambda eng: eng.tensor_scalar(
            m3[:], m3[:], 1.0 / float(H), None, Alu.mult),
            waits=[s.now("tG"), s.now("tS")])
        vdrain()
        vop(lambda eng: eng.tensor_tensor(xm3[:], m3[:], m3[:], Alu.mult))
        vop(lambda eng: eng.tensor_scalar(
            r3[:], ssq3[:], 1.0 / float(H), 1e-5, Alu.mult, Alu.add))
        vdrain()
        vop(lambda eng: eng.tensor_tensor(r3[:], r3[:], xm3[:], Alu.subtract))
        vdrain()
        vop(lambda eng: eng.reciprocal(r3[:], r3[:]))
        vdrain()
        r3_t = sop(lambda eng: eng.activation(r3[:], r3[:], Act.Sqrt),
                   waits=[s.now("tV")])
        vop(lambda eng: eng.tensor_tensor(mx3[:], mx3[:], m3[:], Alu.subtract))
        vop(lambda eng: eng.tensor_tensor(mn3[:], m3[:], mn3[:], Alu.subtract))
        vdrain()
        vop(lambda eng: eng.tensor_tensor(xm3[:], mx3[:], mn3[:], Alu.max))
        vdrain()
        vop(lambda eng: eng.tensor_tensor(xm3[:], xm3[:], r3[:], Alu.mult),
            waits=[r3_t])
        vdrain()
        vop(lambda eng: eng.reduce_max(amax[2][:], xm3[:], AXX))
        vdrain()

    if stage("all"):
        s.full_barrier()
        bc3 = allreduce_scalar(2, s.now("tV"), [(C_S3, "s"), (C_RS3, "rs")])
        vop(lambda eng: eng.tensor_scalar(
            rq3[:], r3[:], scal_b[:, C_RS3:C_RS3 + 1], None, Alu.mult), waits=bc3)
        vop(lambda eng: eng.tensor_scalar(
            gscale[:], swg[:], scal_b[:, C_S3:C_S3 + 1], None, Alu.mult), waits=bc3)
        vop(lambda eng: eng.tensor_scalar(
            uscale[:], swu[:], scal_b[:, C_S3:C_S3 + 1], None, Alu.mult), waits=bc3)
        for hc in range(HC):
            dma("sync", qwgT[:, hc, :], QWG[hc], sem="dw")
            dma("gpsimd", quT[:, hc, :], QWU[hc], sem="dw")
        for ic in range(IC):
            dma("scalar", qwdT[:, ic, :], QWD[ic], sem="dw")
        s.full_barrier()
        amax4_tok = pass2(X2, m3, rq3, 3)
        bc4 = allreduce_scalar(3, amax4_tok, [(C_RS4, "rs"), (C_S4H, "sh")])
        s.full_barrier()
        pass3a(X2, outd, C_S4H, C_RS4)
        s.op("gpsimd", lambda eng: None,
             waits=[s.now(x) for x in Sched.DMA_SEMS])
        s.op("sync", lambda eng: None,
             waits=[s.now(x) for x in Sched.DMA_SEMS])
    else:
        qs = [s.now(q) for q in Sched.DMA_SEMS]
        for e in ENGS:
            s.op(e, lambda eng: None,
                 waits=qs + [s.now("tV"), s.now("tS"), s.now("tP")])

    import contextlib
    es = contextlib.ExitStack()
    with es:
        sems = {}
        for name in ("qS", "qG", "qC", "qV", "dx", "dx0", "dx1", "dh0", "dh1",
                     "dh2", "dh3", "dw", "dk", "dq0", "dq1", "dr", "de", "dc", "dg",
                     "dg0", "dg1", "dg2", "dg3", "tP", "tV", "tS", "tG", "cc", "bar"):
            sems[name] = es.enter_context(nc.semaphore(name))
        s.bar_handle = sems["bar"]
        with nc.Block() as block:
            def mk(ops):
                def body(eng):
                    for fn, ws, inc in ops:
                        wmax = {}
                        for (sname, v) in ws:
                            wmax[sname] = max(wmax.get(sname, 0), v)
                        for idx, (sname, v) in enumerate(wmax.items()):
                            if idx and idx % 2 == 0:
                                eng.nop(nofuse=True)
                            eng.wait_ge(sems[sname], v)
                        if wmax and eng.engine == mybir.EngineType.PE:
                            eng.nop(nofuse=True)
                        inst = fn(eng)
                        if inc is not None:
                            sem, n = inc
                            inst.then_inc(sems[sem], n)
                return body
            block.sync(mk(s.prog["sync"]))
            block.vector(mk(s.prog["vector"]))
            block.scalar(mk(s.prog["scalar"]))
            block.tensor(mk(s.prog["tensor"]))
            block.gpsimd(mk(s.prog["gpsimd"]))
    return nc


def _shard_inputs(inputs):
    x = np.ascontiguousarray(
        np.asarray(inputs["L0_attn_hidden_states"], dtype=np.float32).reshape(-1, H))
    bq = np.asarray(inputs["L0_attn_bq"], dtype=np.float32)
    bk = np.asarray(inputs["L0_attn_bk"], dtype=np.float32)
    bv = np.asarray(inputs["L0_attn_bv"], dtype=np.float32)
    com = {
        "wq_t": np.ascontiguousarray(np.asarray(inputs["L0_attn_Wq"], dtype=np.float32).T),
        "wk_t": np.ascontiguousarray(np.asarray(inputs["L0_attn_Wk"], dtype=np.float32).T),
        "wv_t": np.ascontiguousarray(np.asarray(inputs["L0_attn_Wv"], dtype=np.float32).T),
        "bq2": np.ascontiguousarray(bq.reshape(8, 128).T),
        "bk2": np.ascontiguousarray(bk.reshape(8, 128).T),
        "bv_row": np.ascontiguousarray(bv.reshape(1, H)),
        "w_gate": np.ascontiguousarray(np.asarray(inputs["L0_ffn_w_gate"], dtype=np.float32)),
        "w_up": np.ascontiguousarray(np.asarray(inputs["L0_ffn_w_up"], dtype=np.float32)),
        "w_down": np.ascontiguousarray(np.asarray(inputs["L0_ffn_w_down"], dtype=np.float32)),
        "clip2": np.array([[float(np.asarray(inputs["L0_ffn_clip_x"]).reshape(-1)[0]),
                            float(np.asarray(inputs["L0_ffn_clip_down"]).reshape(-1)[0])]],
                          dtype=np.float32),
    }
    in_maps = []
    for c in range(NCORE):
        m = dict(com)
        m["x0"] = np.ascontiguousarray(x[c * T:(c + 1) * T])
        in_maps.append(m)
    return in_maps


def kernel(**inputs):
    in_maps = _shard_inputs(inputs)
    nc = build_nc(debug=False, upto="all")
    res = run_bass_kernel_spmd(nc, in_maps, core_ids=list(range(NCORE)))
    out = np.concatenate([np.asarray(r["out"], dtype=np.float32) for r in res.results],
                         axis=0)
    return out.reshape(16, 4096, 1024)


# version 09
# speedup vs baseline: 1.0468x; 1.0094x over previous
import numpy as np
import ml_dtypes
import concourse.bass as bass
import concourse.mybir as mybir
from concourse.bass_utils import run_bass_kernel_spmd

F32 = mybir.dt.float32
BF16 = mybir.dt.bfloat16
Alu = mybir.AluOpType
Act = mybir.ActivationFunctionType
AXX = mybir.AxisListType.X
MAGIC = float(1.5 * 2.0 ** 23)

T = 8192
H = 1024
II = 2816
IC = II // 128
HC = H // 128
SEQ = 4096
NH = 16
NCHUNK = SEQ // 128
NT128S = SEQ // 128
NCORE = 8

ENGS = ("sync", "vector", "scalar", "tensor", "gpsimd")
QSEM = {"sync": "qS", "gpsimd": "qG", "scalar": "qC", "vector": "qV"}
C_S1, C_RS1, C_S2, C_RS2, C_S2H, C_S3, C_RS3, C_S4, C_RS4, C_S4H = range(10)


class Sched:
    def __init__(self):
        self.prog = {e: [] for e in ENGS}
        self.cnt = {}
        self.bar_handle = None

    def op(self, eng, fn, waits=(), inc=None, n=1):
        ws = [w for w in waits if w is not None]
        tok = None
        if inc is not None:
            self.cnt[inc] = self.cnt.get(inc, 0) + n
            tok = (inc, self.cnt[inc])
        self.prog[eng].append((fn, ws, (inc, n) if inc else None))
        return tok

    def now(self, sem):
        return (sem, self.cnt.get(sem, 0))

    DMA_SEMS = ("qS", "qG", "qC", "qV", "dx", "dx0", "dx1", "dh0", "dh1",
                "dh2", "dh3", "dw", "dk", "dq0", "dq1", "dr", "de", "dc", "dg",
                "dg0", "dg1", "dg2", "dg3")

    def full_barrier(self):
        qs = [self.now(q) for q in self.DMA_SEMS]
        for e in ENGS:
            w = qs if e == "gpsimd" else []
            self.op(e, lambda eng: eng.nop(nofuse=True),
                    waits=w, inc="bar", n=1)
        val = self.cnt["bar"]
        for e in ENGS:
            self.op(e, lambda eng: None, waits=[("bar", val)])


def build_nc(debug=False, upto="all"):
    nc = bass.Bass("TRN2", target_bir_lowering=False)
    STAGES = ["pass2", "pass3a", "pass3b", "attn", "all"]
    SI = STAGES.index(upto)

    def stage(name):
        return SI >= STAGES.index(name)

    ext_dbg = "ExternalOutput" if debug else "Internal"

    x0d = nc.dram_tensor("x0", [T, H], F32, kind="ExternalInput").ap()
    outd = nc.dram_tensor("out", [T, H], F32, kind="ExternalOutput").ap()
    wqtd = nc.dram_tensor("wq_t", [H, H], F32, kind="ExternalInput").ap()
    wktd = nc.dram_tensor("wk_t", [H, H], F32, kind="ExternalInput").ap()
    wvtd = nc.dram_tensor("wv_t", [H, H], F32, kind="ExternalInput").ap()
    bq2d = nc.dram_tensor("bq2", [128, 8], F32, kind="ExternalInput").ap()
    bk2d = nc.dram_tensor("bk2", [128, 8], F32, kind="ExternalInput").ap()
    bvrd = nc.dram_tensor("bv_row", [1, H], F32, kind="ExternalInput").ap()
    wgd = nc.dram_tensor("w_gate", [II, H], F32, kind="ExternalInput").ap()
    wud = nc.dram_tensor("w_up", [II, H], F32, kind="ExternalInput").ap()
    wdd = nc.dram_tensor("w_down", [H, II], F32, kind="ExternalInput").ap()
    clipd = nc.dram_tensor("clip2", [1, 2], F32, kind="ExternalInput").ap()

    X1 = nc.dram_tensor("X1", [T, H], F32, kind=ext_dbg).ap()
    X2 = nc.dram_tensor("X2", [T, H], F32, kind=ext_dbg).ap()
    HP = nc.dram_tensor("HP", [IC, 128, T], F32, kind=ext_dbg).ap()
    QT = nc.dram_tensor("QTsp", [T // 128, 128, HC, 128], BF16, kind=ext_dbg).ap()
    QWG = nc.dram_tensor("QWG", [HC, 128, II], BF16).ap()
    QWU = nc.dram_tensor("QWU", [HC, 128, II], BF16).ap()
    QWD = nc.dram_tensor("QWD", [IC, 128, H], BF16).ap()
    WQB = nc.dram_tensor("WQB", [HC, 128, H], BF16).ap()
    WKB = nc.dram_tensor("WKB", [HC, 128, H], BF16).ap()
    WVB = nc.dram_tensor("WVB", [HC, 128, H], BF16).ap()
    SCAL = nc.dram_tensor("SCAL", [1, 16], F32).ap()
    RED = [nc.dram_tensor(f"RED{k}", [128, 1], F32).ap() for k in range(4)]
    CCI = [nc.dram_tensor(f"cci{k}", [1, 16], F32) for k in range(4)]
    CCO = [nc.dram_tensor(f"cco{k}", [1, 16], F32) for k in range(4)]
    if debug:
        KDd = nc.dram_tensor("KD", [128, HC, SEQ], BF16, kind="ExternalOutput").ap()
        VDd = nc.dram_tensor("VD", [128, NCHUNK, H], BF16, kind="ExternalOutput").ap()
        DBG1 = nc.dram_tensor("DBG1", [128, 256], F32, kind="ExternalOutput").ap()
        DBGM = nc.dram_tensor("DBGM", [1, 16], F32, kind="ExternalOutput").ap()
        DBGW = nc.dram_tensor("DBGW", [128, 2816], BF16, kind="ExternalOutput").ap()
        DBGQ = nc.dram_tensor("DBGQ", [128, HC, 512], BF16, kind="ExternalOutput").ap()
        X1C = nc.dram_tensor("X1C", [T, H], F32, kind="ExternalOutput").ap()
        QHD = nc.dram_tensor("QHD", [32, 2, 128, 256], BF16, kind="ExternalOutput").ap()

    identd = nc.inline_tensor(np.eye(128, dtype=ml_dtypes.bfloat16), name="identc")

    ARENA = 212832
    nc.alloc_sbuf_tensor("bigarena", [128, ARENA], mybir.dt.uint8)
    base = nc.lookup_mloc("bigarena").addr
    LIMIT = base + ARENA
    cur = [base]

    def at(name, shape, dtype, off=None):
        nbytes = int(np.prod(shape[1:])) * mybir.dt.size(dtype)
        if off is None:
            o = cur[0]
            cur[0] = o + ((nbytes + 31) // 32) * 32
        else:
            o = base + off
        assert o + nbytes <= LIMIT, f"SBUF overflow {name}: {o + nbytes - base}"
        return nc.alloc_sbuf_tensor_at(name, shape, dtype, offset=o)

    ident = at("ident", [128, 128], BF16)
    biasq = at("biasq", [128, 8], F32)
    biask = at("biask", [128, 8], F32)
    bvx = at("bvx", [128, H], F32)
    scal_b = at("scal_b", [128, 16], F32)
    m1 = at("m1", [128, 64], F32)
    r1 = at("r1", [128, 64], F32)
    rq1 = at("rq1", [128, 64], F32)
    m3 = at("m3", [128, 64], F32)
    r3 = at("r3", [128, 64], F32)
    rq3 = at("rq3", [128, 64], F32)
    swg = at("swg", [128, 22], F32)
    swu = at("swu", [128, 22], F32)
    gscale = at("gscale", [128, 22], F32)
    uscale = at("uscale", [128, 22], F32)
    amax = [at(f"amax{k}", [128, 1], F32) for k in range(4)]
    xm1 = at("xm1", [128, 64], F32)
    xm3 = at("xm3", [128, 64], F32)
    biasq8 = at("biasq8", [128, 8], F32)
    red_row = at("red_row", [1, 128], F32)
    misc = at("misc", [1, 16], F32)
    n30 = at("n30", [128, 1], F32)
    ssq3 = at("ssq3", [128, 64], F32)
    mx3 = at("mx3", [128, 64], F32)
    mn3 = at("mn3", [128, 64], F32)
    GLOB_END = cur[0] - base
    assert GLOB_END < 9500, GLOB_END

    WREG = ((GLOB_END + 63) // 64) * 64
    qwgT = at("qwgT", [128, HC, II], BF16, off=WREG)
    quT = at("quT", [128, HC, II], BF16, off=WREG + 45056)
    qwdT = at("qwdT", [128, IC, H], BF16, off=WREG + 90112)
    kT = at("kT", [128, HC, SEQ], BF16, off=WREG)
    Vt = at("Vt", [128, NCHUNK, H], BF16, off=WREG + 65536)
    WORK = WREG + 135168

    def wat(name, shape, dtype, off):
        return at(name, shape, dtype, off=WORK + off)

    wstage = [wat(f"wstage{i}", [128, II], F32, 11264 * i) for i in range(2)]
    qwnat = wat("qwnat", [128, II], BF16, 22528)
    qkvb16 = wat("qkvb16", [128, H], BF16, 28160)
    wsc = wat("wsc", [128, 8], F32, 30208)
    px = [wat("px0", [128, 4, H], F32, 30976)]
    ptmp = wat("ptmp", [128, H], F32, 47360)
    ptmp2 = wat("ptmp2", [128, H], F32, 51456)
    pcol = wat("pcol", [128, 4], F32, 55552)

    p2_x = [wat(f"p2x{i}", [128, 4, H], F32, 16384 * i) for i in range(2)]
    p2_qx = wat("p2qx", [128, 4, H], BF16, 32768)
    p2_qxT = wat("p2qxT", [128, HC, 512], BF16, 40960)
    p2_t = wat("p2t", [128, H], F32, 49152)
    p2_sg = [wat(f"p2sg{i}", [128, 512], F32, 53248 + 2048 * i) for i in range(2)]
    p2_hp = [wat(f"p2hp{i}", [128, 512], F32, 57344 + 2048 * i) for i in range(4)]
    p2_col = wat("p2col", [128, 24], F32, 65536)

    p3_hp = [wat(f"p3hp{i}", [128, 256], F32, 1024 * i) for i in range(4)]
    p3_qh = [wat(f"p3qh{i}", [128, 256], BF16, 4096 + 512 * i) for i in range(4)]
    p3_x0 = [wat("p3x0", [128, 2, H], F32, 6144)]
    p3_x1 = wat("p3x1", [128, 2, H], F32, 14336)

    b_wq = wat("bwq", [128, HC, H], BF16, 0)
    b_wk = wat("bwk", [128, HC, H], BF16, 16384)
    b_wv = wat("bwv", [128, HC, H], BF16, 32768)
    b_x1 = [wat("bx1", [128, 2, H], F32, 49152)]
    b_ln = wat("bln", [128, H], BF16, 57344)
    b_xT = wat("bxT", [128, HC, 256], BF16, 59392)
    b_col = wat("bcol", [128, 8], F32, 63488)
    b_pt = wat("bpt", [128, H], F32, 63520)
    b_qs = wat("bqs", [128, 2, HC, 128], BF16, 63520)

    a_kTe = wat("akTe", [128, HC, 128], BF16, 0)
    a_VtE = wat("aVtE", [128, H], BF16, 2048)
    a_qt = [wat(f"aqt{i}", [128, HC, 128], BF16, 4096 + 2048 * i) for i in range(2)]
    a_A = [wat(f"aA{i}", [128, 512], BF16, 8192 + 1024 * i) for i in range(4)]
    a_aT = [wat(f"aaT{i}", [128, 512], BF16, 12288 + 1024 * i) for i in range(2)]
    a_ctx = wat("actx", [128, H], F32, 14336)
    a_x1 = wat("ax1", [128, H], F32, 18432)
    a_x2 = wat("ax2", [128, H], F32, 22528)
    a_Sd = [wat(f"aS{i}", [128, SEQ], F32, 26624 + 16384 * i) for i in range(2)]
    a_AF = wat("aAF", [128, SEQ], BF16, 59392)
    a_xc = wat("axc", [128, H], F32, 59392)
    a_col = wat("acol", [128, 16], F32, 67584)
    g_col = wat("gcol", [128, 4], F32, 67648)

    pb = [nc.alloc_psum_tensor(f"pb{i}", [128, 512], F32) for i in range(8)]
    pbh = [p.ap().bitcast(BF16) for p in pb]

    s = Sched()

    def dma(q, out, in_, waits=(), sem=None, **kw):
        return s.op(q, lambda eng, o=out, i=in_, kw=kw: eng.dma_start(out=o, in_=i, **kw),
                    waits=waits, inc=(sem or QSEM[q]), n=16)

    def vop(fn, waits=()):
        return s.op("vector", fn, waits=waits, inc="tV")

    def gop(fn, waits=()):
        return s.op("gpsimd", fn, waits=waits, inc="tG")

    def gdrain():
        s.op("gpsimd", lambda eng: eng.drain())

    def sop(fn, waits=()):
        return s.op("scalar", fn, waits=waits, inc="tS")

    def top(fn, waits=()):
        return s.op("tensor", fn, waits=waits, inc="tP")

    def vdrain():
        s.op("vector", lambda eng: eng.drain())

    def sdrain():
        s.op("scalar", lambda eng: eng.drain())

    cast_tok = None
    spill_tok = None
    for wi, (src, dst) in enumerate(((wqtd, WQB), (wktd, WKB), (wvtd, WVB))):
        for hc in range(HC):
            b = hc % 2
            ldt = dma("sync", wstage[b][:, 0:H], src[128 * hc:128 * hc + 128, :],
                      waits=[cast_tok], sem="dw")
            cast_tok = vop(lambda eng, b=b: eng.tensor_copy(qkvb16[:], wstage[b][:, 0:H]),
                           waits=[ldt, spill_tok])
            spill_tok = dma("gpsimd", dst[hc], qkvb16[:], waits=[cast_tok], sem="dg")
    dma("sync", biasq[:], bq2d[:, :], sem="dk")
    dma("sync", biask[:], bk2d[:, :], sem="dk")
    bvx_ld = dma("sync", bvx[:], bvrd[0:1, :].broadcast_to((128, H)), sem="dk")
    dma("sync", ident[:], identd.ap(), sem="dk")
    kc_ld = dma("sync", misc[0:1, 0:2], clipd[0:1, :], sem="dk")
    id_ld = kc_ld
    clip_ld = kc_ld
    vop(lambda eng: eng.tensor_scalar(
        biasq8[:], biasq[:], 0.125, None, Alu.mult), waits=[kc_ld])
    sig_tok = sop(lambda eng: eng.activation(misc[0:1, 2:4], misc[0:1, 0:2],
                                             Act.Sigmoid), waits=[clip_ld])
    vop(lambda eng: eng.memset(n30[:], -1e30))

    def quant_weight_gu(srcd, dstT, sw_dst):
        q_tok = None
        ev_tok = None
        for ic in range(IC):
            b = ic % 2
            ldt = dma("sync", wstage[b][:, 0:H], srcd[128 * ic:128 * ic + 128, :],
                      waits=[q_tok], sem="dw")
            rm = vop(lambda eng, b=b: eng.reduce_max(wsc[:, 0:1], wstage[b][:, 0:H], AXX,
                                                     apply_absolute_value=True),
                     waits=[ldt])
            vdrain()
            vop(lambda eng, ic=ic, d=sw_dst: eng.tensor_scalar(
                d[:, ic:ic + 1], wsc[:, 0:1], 1.0 / 127.0, None, Alu.mult))
            vop(lambda eng: eng.reciprocal(wsc[:, 1:2], wsc[:, 0:1]))
            vdrain()
            vop(lambda eng, b=b: eng.tensor_scalar(
                wstage[b][:, 0:H], wstage[b][:, 0:H], wsc[:, 1:2], 127.0,
                Alu.mult, Alu.mult))
            q_tok = vop(lambda eng, b=b: eng.tensor_scalar(
                qwnat[:, 0:H], wstage[b][:, 0:H], MAGIC, MAGIC, Alu.add, Alu.subtract),
                waits=[ev_tok])
            for hcx in range(HC):
                pbk = 6 + (hcx % 2)
                tr = top(lambda eng, hcx=hcx, pbk=pbk: eng.transpose(
                    pbh[pbk][:, 0:128], qwnat[:, 128 * hcx:128 * hcx + 128], ident[:]),
                    waits=[q_tok, ev_tok, id_ld])
                ev_tok = vop(lambda eng, hcx=hcx, ic=ic, pbk=pbk, d=dstT: eng.tensor_copy(
                    d[:, hcx, 128 * ic:128 * ic + 128], pbh[pbk][:, 0:128]), waits=[tr])
        return ev_tok

    gu_ev = quant_weight_gu(wgd, qwgT, swg)
    gu_ev2 = quant_weight_gu(wud, quT, swu)

    def quant_weight_down():
        ev_tok = None
        q_tok = None
        for hcx in range(HC):
            b = hcx % 2
            ldt = dma("sync", wstage[b][:], wdd[128 * hcx:128 * hcx + 128, :],
                      waits=[q_tok], sem="dw")
            rm = vop(lambda eng, b=b: eng.reduce_max(wsc[:, 0:1], wstage[b][:], AXX,
                                                     apply_absolute_value=True),
                     waits=[ldt])
            vdrain()
            vop(lambda eng: eng.reciprocal(wsc[:, 1:2], wsc[:, 0:1]))
            vop(lambda eng: eng.tensor_scalar(
                wsc[:, 2:3], wsc[:, 0:1], 1.0 / 127.0, None, Alu.mult))
            vdrain()
            vop(lambda eng, b=b: eng.tensor_scalar(
                wstage[b][:], wstage[b][:], wsc[:, 1:2], 127.0, Alu.mult, Alu.mult))
            vop(lambda eng, b=b: eng.tensor_scalar(
                wstage[b][:], wstage[b][:], MAGIC, MAGIC, Alu.add, Alu.subtract))
            q_tok = vop(lambda eng, b=b: eng.tensor_scalar(
                qwnat[:], wstage[b][:], wsc[:, 2:3], None, Alu.mult),
                waits=[ev_tok])
            for ic in range(IC):
                pbk = 6 + (ic % 2)
                tr = top(lambda eng, ic=ic, pbk=pbk: eng.transpose(
                    pbh[pbk][:, 0:128], qwnat[:, 128 * ic:128 * ic + 128], ident[:]),
                    waits=[q_tok, ev_tok, id_ld])
                ev_tok = vop(lambda eng, ic=ic, hcx=hcx, pbk=pbk: eng.tensor_copy(
                    qwdT[:, ic, 128 * hcx:128 * hcx + 128], pbh[pbk][:, 0:128]),
                    waits=[tr])
        return ev_tok

    dn_ev = quant_weight_down()
    for hc in range(HC):
        dma("gpsimd", QWG[hc], qwgT[:, hc, :], waits=[gu_ev], sem="dg")
        dma("gpsimd", QWU[hc], quT[:, hc, :], waits=[gu_ev2], sem="dg")
    for ic in range(IC):
        dma("gpsimd", QWD[ic], qwdT[:, ic, :], waits=[dn_ev], sem="dg")

    ld_tok = [None, None]
    q_done = [None, None]
    sq_prev = None
    for t in range(16):
        b = 0
        ld_tok[b] = dma("sync", px[b][:],
                        x0d[512 * t:512 * t + 512, :].rearrange("(a p) h -> p a h", p=128),
                        waits=[q_done[b]], sem="dx")
        for sub in range(4):
            tt = 4 * t + sub
            vop(lambda eng, b=b, sub=sub: eng.tensor_reduce(
                pcol[:, 0:1], px[b][:, sub, :], AXX, Alu.add), waits=[ld_tok[b]])
            vdrain()
            vop(lambda eng, tt=tt: eng.tensor_scalar(
                m1[:, tt:tt + 1], pcol[:, 0:1], 1.0 / float(H), None, Alu.mult))
            vdrain()
            xc_t = vop(lambda eng, b=b, sub=sub, tt=tt: eng.tensor_scalar(
                ptmp[:], px[b][:, sub, :], m1[:, tt:tt + 1], None, Alu.subtract),
                waits=[sq_prev])
            sq_prev = sop(lambda eng: eng.activation(ptmp2[:], ptmp[:], Act.Square,
                                                     accum_out=pcol[:, 1:2]),
                          waits=[xc_t])
            vop(lambda eng: eng.tensor_scalar(
                pcol[:, 1:2], pcol[:, 1:2], 1.0 / float(H), 1e-5, Alu.mult, Alu.add),
                waits=[sq_prev])
            vdrain()
            vop(lambda eng: eng.reciprocal(pcol[:, 1:2], pcol[:, 1:2]))
            vdrain()
            r_t = sop(lambda eng, tt=tt: eng.activation(
                r1[:, tt:tt + 1], pcol[:, 1:2], Act.Sqrt), waits=[s.now("tV")])
            q_done[b] = vop(lambda eng, tt=tt: eng.reduce_max(
                xm1[:, tt:tt + 1], ptmp[:], AXX,
                apply_absolute_value=True), waits=[sq_prev])
    vdrain()
    vop(lambda eng: eng.tensor_tensor(xm1[:], xm1[:], r1[:], Alu.mult),
        waits=[s.now("tS")])
    vdrain()
    vop(lambda eng: eng.reduce_max(amax[0][:], xm1[:], AXX))
    vdrain()

    def allreduce_scalar(k, amax_tok, extra_cols):
        sig_col = 2 if k in (0, 2) else 3
        t1 = dma("gpsimd", RED[k][:, :], amax[k][:, 0:1], waits=[amax_tok], sem="dc")
        t2 = dma("gpsimd", red_row[0:1, :], RED[k].transpose([1, 0]), waits=[t1],
                 allow_slow_non_contiguous=True, sem="dc")
        mx = vop(lambda eng: eng.reduce_max(misc[0:1, 4:5], red_row[0:1, :], AXX),
                 waits=[t2])
        t3 = dma("gpsimd", CCI[k].ap()[0:1, 0:1], misc[0:1, 4:5], waits=[mx], sem="dc")
        cc_t = s.op("gpsimd", lambda eng, k=k: eng.collective_compute(
            "AllReduce", Alu.max, replica_groups=[list(range(NCORE))],
            ins=[CCI[k].ap().opt()], outs=[CCO[k].ap().opt()]),
            waits=[t3], inc="cc", n=1)
        t4 = dma("gpsimd", misc[0:1, 4:5], CCO[k].ap()[0:1, 0:1], waits=[cc_t], sem="dc")
        vop(lambda eng, sc=sig_col: eng.tensor_tensor(
            misc[0:1, 5:6], misc[0:1, 4:5], misc[0:1, sc:sc + 1], Alu.mult),
            waits=[t4, sig_tok])
        vdrain()
        vop(lambda eng: eng.tensor_scalar(
            misc[0:1, 6:7], misc[0:1, 5:6], 1.0 / 127.0, None, Alu.mult))
        vdrain()
        rsv = vop(lambda eng: eng.reciprocal(misc[0:1, 7:8], misc[0:1, 6:7]))
        shv = vop(lambda eng: eng.tensor_scalar(
            misc[0:1, 8:9], misc[0:1, 6:7], 0.5, None, Alu.mult))
        vdrain()
        toks = []
        for (col, tag) in extra_cols:
            srcc = {"s": 6, "rs": 7, "sh": 8}[tag]
            t5 = dma("gpsimd", SCAL[0:1, col:col + 1], misc[0:1, srcc:srcc + 1],
                     waits=[rsv, shv], sem="dc")
            toks.append(dma("gpsimd", scal_b[:, col:col + 1],
                            SCAL[0:1, col:col + 1].broadcast_to((128, 1)), waits=[t5],
                            sem="dc"))
        return toks

    bc1 = allreduce_scalar(0, s.now("tV"), [(C_S1, "s"), (C_RS1, "rs")])
    vop(lambda eng: eng.tensor_scalar(
        rq1[:], r1[:], scal_b[:, C_RS1:C_RS1 + 1], None, Alu.mult), waits=bc1)
    vop(lambda eng: eng.tensor_scalar(
        gscale[:], swg[:], scal_b[:, C_S1:C_S1 + 1], None, Alu.mult),
        waits=bc1 + [gu_ev])
    vop(lambda eng: eng.tensor_scalar(
        uscale[:], swu[:], scal_b[:, C_S1:C_S1 + 1], None, Alu.mult),
        waits=bc1 + [gu_ev2])
    s.full_barrier()

    def pass2(xsrc, mcache, rqcache, amax_idx):
        vop(lambda eng, k=amax_idx: eng.memset(amax[k][:], 0.0))
        ld_tok = [None, None]
        lastq = [None, None]
        silu_tok = [None, None]
        hp_tok = [None, None, None, None]
        hpdma_tok = [None, None, None, None]
        evq_tok = [None, None]
        tr_last = None
        amax_tok = None
        for t in range(16):
            b = t % 2
            ld_tok[b] = dma("sync", p2_x[b][:],
                            xsrc[512 * t:512 * t + 512, :].rearrange("(a p) h -> p a h",
                                                                     p=128),
                            waits=[lastq[b]], sem=f"dx{b}")
            for sub in range(4):
                tt = 4 * t + sub
                vop(lambda eng, b=b, sub=sub, tt=tt, mc=mcache, rc=rqcache:
                    eng.tensor_scalar(p2_t[:], p2_x[b][:, sub, :], mc[:, tt:tt + 1],
                                      rc[:, tt:tt + 1], Alu.subtract, Alu.mult),
                    waits=[ld_tok[b]] + ([tr_last] if sub == 0 else []))
                vop(lambda eng: eng.tensor_scalar(
                    p2_t[:], p2_t[:], MAGIC, MAGIC, Alu.add, Alu.subtract))
                lastq[b] = vop(lambda eng, sub=sub: eng.tensor_scalar(
                    p2_qx[:, sub, :], p2_t[:], 127.0, -128.0, Alu.min, Alu.max))
            for sub in range(4):
                for hc in range(HC):
                    j = (sub * HC + hc) % 2
                    pbk = 4 + j
                    tr = top(lambda eng, sub=sub, hc=hc, pbk=pbk: eng.transpose(
                        pbh[pbk][:, 0:128], p2_qx[:, sub, 128 * hc:128 * hc + 128],
                        ident[:]), waits=[lastq[b], evq_tok[j]])
                    evq_tok[j] = vop(lambda eng, sub=sub, hc=hc, pbk=pbk: eng.tensor_copy(
                        p2_qxT[:, hc, 128 * sub:128 * sub + 128], pbh[pbk][:, 0:128]),
                        waits=[tr])
            tr_ev = [evq_tok[0], evq_tok[1]]
            mm_g = None
            mm_u = None
            for ic in range(IC):
                pg = ic % 2
                pu = 2 + ic % 2
                for hc in range(HC):
                    mm_g = top(lambda eng, ic=ic, hc=hc, pg=pg: eng.matmul(
                        pb[pg][:], qwgT[:, hc, 128 * ic:128 * ic + 128],
                        p2_qxT[:, hc, :],
                        start=(hc == 0), stop=(hc == HC - 1)),
                        waits=(tr_ev + [silu_tok[ic % 2]]) if hc == 0 else [])
                for hc in range(HC):
                    mm_u = top(lambda eng, ic=ic, hc=hc, pu=pu: eng.matmul(
                        pb[pu][:], quT[:, hc, 128 * ic:128 * ic + 128],
                        p2_qxT[:, hc, :],
                        start=(hc == 0), stop=(hc == HC - 1)),
                        waits=[hp_tok[(ic + 2) % 4]] if hc == 0 else [])
                silu_tok[ic % 2] = sop(lambda eng, ic=ic, pg=pg: eng.activation(
                    p2_sg[ic % 2][:], pb[pg][:], Act.Silu, scale=gscale[:, ic:ic + 1]),
                    waits=[mm_g, hp_tok[(ic + 2) % 4]])
                hb = ic % 4
                hp_tok[hb] = vop(lambda eng, ic=ic, pu=pu, hb=hb:
                                 eng.scalar_tensor_tensor(
                                     p2_hp[hb][:], pb[pu][:], uscale[:, ic:ic + 1],
                                     p2_sg[ic % 2][:], Alu.mult, Alu.mult),
                                 waits=[mm_u, silu_tok[ic % 2], hpdma_tok[hb]])
                vop(lambda eng, hb=hb, ic=ic: eng.reduce_max(
                    p2_col[:, ic:ic + 1], p2_hp[hb][:], AXX,
                    apply_absolute_value=True))
                hpdma_tok[hb] = dma("gpsimd" if hb % 2 == 0 else "scalar",
                                    HP[ic, :, 512 * t:512 * t + 512],
                                    p2_hp[hb][:], waits=[hp_tok[hb]], sem=f"dg{hb}")
            vdrain()
            vop(lambda eng: eng.reduce_max(p2_col[:, 22:23], p2_col[:, 0:IC], AXX))
            vdrain()
            amax_tok = vop(lambda eng, k=amax_idx: eng.tensor_tensor(
                amax[k][:], amax[k][:], p2_col[:, 22:23], Alu.max))
            vdrain()
            tr_last = mm_g
        return amax_tok

    amax2_tok = pass2(x0d, m1, rq1, 1)
    bc2 = allreduce_scalar(1, amax2_tok, [(C_RS2, "rs"), (C_S2H, "sh")])
    s.full_barrier()
    if debug:
        dma("gpsimd", DBG1[:, 0:1], amax[0][:], allow_slow_non_contiguous=True, sem="dg")
        dma("gpsimd", DBG1[:, 1:2], amax[1][:], allow_slow_non_contiguous=True, sem="dg")
        dma("gpsimd", DBG1[:, 16:32], scal_b[:], allow_slow_non_contiguous=True, sem="dg")
        dma("gpsimd", DBG1[:, 32:96], m1[:], allow_slow_non_contiguous=True, sem="dg")
        dma("gpsimd", DBG1[:, 96:160], r1[:], allow_slow_non_contiguous=True, sem="dg")
        dma("gpsimd", DBG1[:, 160:224], rq1[:], allow_slow_non_contiguous=True, sem="dg")
        dma("gpsimd", DBG1[:, 224:246], gscale[:], allow_slow_non_contiguous=True, sem="dg")
        dma("gpsimd", DBGM[:, :], misc[:], sem="dg")
        dma("gpsimd", DBGW[:, :], qwgT[:, 0, :], sem="dg")
        dma("gpsimd", DBGQ[:, :, :], p2_qxT[:], sem="dg")
        s.full_barrier()

    if stage("pass3a"):
        def pass3a(xres, dstd, shcol, rscol):
            ld_tok = [None] * 4
            qh_tok = [None] * 4
            mmh = [None] * 4
            mm_tok = None
            ev_toks = [None] * 4
            stq = None
            last_store = None
            for u in range(32):
                x0ld = dma("sync", p3_x0[0][:],
                           xres[256 * u:256 * u + 256, :].rearrange(
                               "(a p) h -> p a h", p=128),
                           waits=[stq], sem="dx")
                for ic in range(IC):
                    hb = ic % 4
                    ld_tok[hb] = dma("sync" if ic % 2 == 0 else "scalar",
                                     p3_hp[hb][:],
                                     HP[ic, :, 256 * u:256 * u + 256],
                                     waits=[qh_tok[hb]], sem=f"dh{hb}")
                    vop(lambda eng, hb=hb, rs=rscol: eng.tensor_scalar(
                        p3_hp[hb][:], p3_hp[hb][:], scal_b[:, rs:rs + 1], MAGIC,
                        Alu.mult, Alu.add), waits=[ld_tok[hb]])
                    vop(lambda eng, hb=hb: eng.tensor_scalar(
                        p3_hp[hb][:], p3_hp[hb][:], MAGIC, 127.0,
                        Alu.subtract, Alu.min))
                    qh_tok[hb] = vop(lambda eng, hb=hb: eng.tensor_scalar(
                        p3_qh[hb][:], p3_hp[hb][:], -128.0, None, Alu.max),
                        waits=[mmh[hb]])
                    if debug and ic in (0, 1) and dstd is X1:
                        dma("gpsimd", QHD[u, ic], p3_qh[hb][:], waits=[qh_tok[hb]],
                            sem="dg")
                    for th in range(2):
                        for fh in range(2):
                            pbk = 2 * th + fh
                            mm_tok = top(lambda eng, ic=ic, th=th, fh=fh, hb=hb,
                                         pbk=pbk:
                                         eng.matmul(
                                             pb[pbk][:],
                                             p3_qh[hb][:, 128 * th:128 * th + 128],
                                             qwdT[:, ic, 512 * fh:512 * fh + 512],
                                             start=(ic == 0), stop=(ic == IC - 1)),
                                         waits=[qh_tok[hb]] +
                                         ([ev_toks[pbk]] if ic == 0 else []))
                    mmh[hb] = mm_tok
                for th in range(2):
                    for fh in range(2):
                        pbk = 2 * th + fh
                        ev_toks[pbk] = vop(
                            lambda eng, th=th, fh=fh, pbk=pbk, sh=shcol:
                            eng.scalar_tensor_tensor(
                                p3_x1[:, th, 512 * fh:512 * fh + 512],
                                pb[pbk][:], scal_b[:, sh:sh + 1],
                                p3_x0[0][:, th, 512 * fh:512 * fh + 512],
                                Alu.mult, Alu.add),
                            waits=[mm_tok, x0ld, stq])
                stq = dma("gpsimd",
                          dstd[256 * u:256 * u + 256, :].rearrange(
                              "(a p) h -> p a h", p=128),
                          p3_x1[:], waits=[ev_toks[3], ev_toks[0], ev_toks[1],
                                           ev_toks[2]], sem="dg")
                last_store = stq
            return last_store

        pass3a(x0d, X1, C_S2H, C_RS2)
        s.full_barrier()
        if debug:
            for t in range(16):
                dma("gpsimd", X1C[512 * t:512 * t + 512, :], X1[512 * t:512 * t + 512, :],
                    sem="dg")
            s.full_barrier()

    if stage("pass3b"):
        def pass3b(seq):
            base_t = SEQ * seq
            wtoks = []
            for hc in range(HC):
                wtoks.append(dma("sync", b_wq[:, hc, :], WQB[hc], sem="dw"))
                wtoks.append(dma("sync", b_wk[:, hc, :], WKB[hc], sem="dw"))
                wtoks.append(dma("sync", b_wv[:, hc, :], WVB[hc], sem="dw"))
            wq_ld = wtoks[-1]
            lastuse = None
            ev_xT = [None, None]
            qk_ev = [None, None]
            v_ev = [None, None]
            qs_dma = None
            sq_prev2 = None
            k_last = None
            v_last = None
            for u in range(16):
                ld_tok = dma("sync", b_x1[0][:],
                             X1[base_t + 256 * u: base_t + 256 * u + 256, :]
                             .rearrange("(a p) h -> p a h", p=128),
                             waits=[lastuse], sem="dx")
                for th in range(2):
                    vop(lambda eng, th=th: eng.tensor_reduce(
                        b_col[:, 0:1], b_x1[0][:, th, :], AXX, Alu.add),
                        waits=[ld_tok])
                    vdrain()
                    vop(lambda eng: eng.tensor_scalar(
                        b_col[:, 1:2], b_col[:, 0:1], 1.0 / float(H), None, Alu.mult))
                    vdrain()
                    xc_t = vop(lambda eng, th=th: eng.tensor_scalar(
                        b_pt[:], b_x1[0][:, th, :], b_col[:, 1:2], None, Alu.subtract),
                        waits=[sq_prev2, qs_dma])
                    sq_prev2 = sop(lambda eng: eng.activation(
                        b_ln[:], b_pt[:], Act.Square, accum_out=b_col[:, 2:3]),
                        waits=[xc_t, ev_xT[0], ev_xT[1]])
                    vop(lambda eng: eng.tensor_scalar(
                        b_col[:, 2:3], b_col[:, 2:3], 1.0 / float(H), 1e-5,
                        Alu.mult, Alu.add), waits=[sq_prev2])
                    vdrain()
                    vop(lambda eng: eng.reciprocal(b_col[:, 2:3], b_col[:, 2:3]))
                    vdrain()
                    r_t = sop(lambda eng: eng.activation(
                        b_col[:, 3:4], b_col[:, 2:3], Act.Sqrt), waits=[s.now("tV")])
                    lastuse = vop(lambda eng: eng.tensor_scalar(
                        b_ln[:], b_pt[:], b_col[:, 3:4], None, Alu.mult),
                        waits=[r_t])
                    for hc in range(HC):
                        j = hc % 2
                        pbk = 6 + j
                        tr = top(lambda eng, hc=hc, pbk=pbk: eng.transpose(
                            pbh[pbk][:, 0:128], b_ln[:, 128 * hc:128 * hc + 128],
                            ident[:]), waits=[lastuse, ev_xT[j]])
                        ev_xT[j] = vop(lambda eng, th=th, hc=hc, pbk=pbk:
                                       eng.tensor_copy(
                                           b_xT[:, hc, 128 * th:128 * th + 128],
                                           pbh[pbk][:, 0:128]), waits=[tr])
                exT = [ev_xT[0], ev_xT[1]]
                mm_qk = None
                for which in range(2):
                    wsrc = b_wq if which == 0 else b_wk
                    for fc in range(HC):
                        pbk = which
                        for hc in range(HC):
                            mm_qk = top(lambda eng, wsrc=wsrc, fc=fc, hc=hc, pbk=pbk:
                                        eng.matmul(
                                            pb[pbk][:, 0:256],
                                            wsrc[:, hc, 128 * fc:128 * fc + 128],
                                            b_xT[:, hc, :],
                                            start=(hc == 0), stop=(hc == HC - 1)),
                                        waits=(exT + [qk_ev[pbk], wq_ld])
                                        if hc == 0 else [])
                        if which == 0:
                            qk_ev[pbk] = sop(lambda eng, fc=fc, pbk=pbk: eng.activation(
                                b_qs[:, :, fc, :], pb[pbk][:, 0:256], Act.Identity,
                                bias=biasq8[:, fc:fc + 1], scale=0.125),
                                waits=[mm_qk] + ([qs_dma] if fc == 0 else []))
                        else:
                            qk_ev[pbk] = sop(lambda eng, fc=fc, pbk=pbk, u=u:
                                             eng.activation(
                                                 kT[:, fc, 256 * u:256 * u + 256],
                                                 pb[pbk][:, 0:256], Act.Identity,
                                                 bias=biask[:, fc:fc + 1]),
                                             waits=[mm_qk])
                            k_last = qk_ev[pbk]
                for j in range(2):
                    blk = 32 * seq + 2 * u + j
                    qs_dma = dma("gpsimd", QT[blk], b_qs[:, j], waits=[qk_ev[0]],
                                 sem="dg")
                mm_v = None
                for th in range(2):
                    for fh in range(2):
                        pbk = 2 + fh
                        for hc in range(HC):
                            mm_v = top(lambda eng, th=th, fh=fh, hc=hc, pbk=pbk:
                                       eng.matmul(
                                           pb[pbk][:],
                                           b_xT[:, hc, 128 * th:128 * th + 128],
                                           b_wv[:, hc, 512 * fh:512 * fh + 512],
                                           start=(hc == 0), stop=(hc == HC - 1)),
                                       waits=(exT + [v_ev[fh], wq_ld])
                                       if hc == 0 else [])
                        ch = 2 * u + th
                        v_ev[fh] = vop(lambda eng, ch=ch, fh=fh, pbk=pbk:
                                       eng.tensor_tensor(
                                           Vt[:, ch, 512 * fh:512 * fh + 512],
                                           pb[pbk][:], bvx[:, 512 * fh:512 * fh + 512],
                                           Alu.add), waits=[mm_v, bvx_ld])
                        v_last = v_ev[fh]
            return k_last, v_last

        kv_a = pass3b(0)
        if debug:
            for fc in range(HC):
                dma("gpsimd", KDd[:, fc, :], kT[:, fc, :],
                    waits=[kv_a[0], kv_a[1]], sem="dg")
            for ch in range(NCHUNK):
                dma("gpsimd", VDd[:, ch, :], Vt[:, ch, :],
                    waits=[kv_a[0], kv_a[1]], sem="dg")
        s.full_barrier()

    if stage("attn"):

        def attention(seq, kv):
            base_t = SEQ * seq
            e1 = dma("gpsimd", a_VtE[0:64, :], Vt[0:64, 0, :], waits=list(kv),
                     sem="de")
            e2 = dma("gpsimd", a_VtE[64:128, :], Vt[64:128, NCHUNK - 1, :],
                     waits=list(kv), sem="de")
            e3 = vop(lambda eng: eng.tensor_copy(a_kTe[:, :, 0:64], kT[:, :, 0:64]),
                     waits=list(kv))
            e4 = vop(lambda eng: eng.tensor_copy(a_kTe[:, :, 64:128],
                                                 kT[:, :, SEQ - 64:SEQ]))

            st = {
                "qld": [None, None],
                "qt_use": [None, None],
                "s_ev": [None, None, None, None],
                "tr_ev": [None, None],
                "A_use": [None, None, None, None],
                "ctx_ev": [None, None],
                "resid_x1": None, "resid_st": None, "resid_xc": None,
                "resid_sq": None, "resid_rm": None,
            }

            def load_qt(tt):
                b = tt % 2
                qld_t = dma("sync", a_qt[b][:], QT[32 * seq + tt],
                            waits=[st["qt_use"][b], st["qld"][b]], sem=f"dq{b}")
                st["qld"][b] = qld_t
                return b, qld_t

            def do_mid(tt, ctx_dst, gate_dst, pend):
                b, qld_t = load_qt(tt)
                vc = tt - 1
                segs = [("edge", None), ("win", vc), ("win", vc + 1), ("win", vc + 2)]
                mask_list = [(0, 64, 128, 192), (0, 64, 384, 512),
                             (64, 128, 128, 256), (64, 128, 448, 512)]
                SBANK = (0, 1, 6, 7)
                out_toks = []

                def stage1(j):
                    hp, hh = divmod(j, 2)
                    p0 = 64 * hh
                    sl = j % 4
                    pS = SBANK[sl]
                    top(lambda eng, p0=p0, hp=hp, pS=pS, b=b:
                        eng.matmul(pb[pS][:, 0:128],
                                   a_qt[b][p0:p0 + 64, hp, :],
                                   a_kTe[p0:p0 + 64, hp, :],
                                   start=True, stop=False),
                        waits=[qld_t, st["s_ev"][sl], kv[0], kv[1], e4])
                    k0 = 128 * vc
                    mmt = top(lambda eng, p0=p0, hp=hp, pS=pS, b=b, k0=k0:
                              eng.matmul(pb[pS][:, 128:512],
                                         a_qt[b][p0:p0 + 64, hp, :],
                                         kT[p0:p0 + 64, hp, k0:k0 + 384],
                                         start=False, stop=True,
                                         skip_group_check=True))
                    st["qt_use"][b] = mmt
                    mt = None
                    for (r0, r1_, c0, c1) in mask_list:
                        mt = vop(lambda eng, pS=pS, r0=r0, r1_=r1_, c0=c0, c1=c1:
                                 eng.memset(pb[pS][r0:r1_, c0:c1], -1e30),
                                 waits=[mmt])
                    nm = vop(lambda eng, pS=pS, sl=sl: eng.reduce_max(
                        a_col[:, sl:sl + 1], pb[pS][:], AXX, negate=True),
                        waits=[mt])
                    ex = sop(lambda eng, pS=pS, sl=sl: eng.activation(
                        a_A[sl][:], pb[pS][:], Act.Exp,
                        bias=a_col[:, sl:sl + 1],
                        accum_out=a_col[:, 4 + sl:5 + sl]),
                        waits=[nm, st["A_use"][sl]])
                    st["s_ev"][sl] = ex
                    return ex

                def stage2(j, ex):
                    hp, hh = divmod(j, 2)
                    h = 2 * hp + hh
                    sl = j % 4
                    rc = vop(lambda eng, sl=sl: eng.reciprocal(
                        a_col[:, 8 + sl:9 + sl], a_col[:, 4 + sl:5 + sl]),
                        waits=[ex, st["ctx_ev"][hh]])
                    cmm = None
                    for ci in range(4):
                        pT = 2 + (ci % 2)
                        tr = top(lambda eng, sl=sl, ci=ci, pT=pT: eng.transpose(
                            pbh[pT][:, 0:128],
                            a_A[sl][:, 128 * ci:128 * ci + 128], ident[:]),
                            waits=[ex, st["tr_ev"][ci % 2]])
                        st["A_use"][sl] = tr
                        if ci % 2 == 0:
                            tre = vop(lambda eng, hh=hh, ci=ci, pT=pT:
                                      eng.tensor_copy(
                                          a_aT[hh][:, 128 * ci:128 * ci + 128],
                                          pbh[pT][:, 0:128]), waits=[tr])
                        else:
                            tre = sop(lambda eng, hh=hh, ci=ci, pT=pT:
                                      eng.activation(
                                          a_aT[hh][:, 128 * ci:128 * ci + 128],
                                          pbh[pT][:, 0:128], Act.Identity),
                                      waits=[tr])
                        st["tr_ev"][ci % 2] = tre
                        styp, c = segs[ci]
                        pC = 4 + hh
                        extra = [st["ctx_ev"][hh]] if ci == 0 else []
                        if ci == 0 and hh == 1 and st.get("sev2"):
                            extra.append(st.pop("sev2"))
                        if styp == "edge":
                            cmm = top(lambda eng, hh=hh, ci=ci, h=h, pC=pC:
                                      eng.matmul(
                                          pb[pC][:, 0:64],
                                          a_aT[hh][:, 128 * ci:128 * ci + 128],
                                          a_VtE[:, 64 * h:64 * h + 64],
                                          start=(ci == 0), stop=(ci == 3)),
                                      waits=[tre, e2] + extra)
                        else:
                            cmm = top(lambda eng, hh=hh, ci=ci, h=h, c=c, pC=pC:
                                      eng.matmul(
                                          pb[pC][:, 0:64],
                                          a_aT[hh][:, 128 * ci:128 * ci + 128],
                                          Vt[:, c, 64 * h:64 * h + 64],
                                          start=(ci == 0), stop=(ci == 3)),
                                      waits=[tre] + extra)
                    cev = sop(lambda eng, h=h, sl=sl, pC=4 + hh, dst=ctx_dst:
                              eng.activation(dst[:, 64 * h:64 * h + 64],
                                             pb[pC][:, 0:64], Act.Identity,
                                             scale=a_col[:, 8 + sl:9 + sl]),
                              waits=[cmm, rc, gate_dst])
                    st["ctx_ev"][hh] = cev
                    out_toks.append(cev)

                exs = {}
                for j in range(4):
                    exs[j] = stage1(j)
                if pend is not None:
                    resid_stats(*pend)
                for j in range(4, 16):
                    stage2(j - 4, exs[j - 4])
                    exs[j] = stage1(j)
                for j in range(12, 16):
                    stage2(j, exs[j])
                return out_toks

            def do_full(tt, ctx_dst, gate_dst, pend):
                b, qld_t = load_qt(tt)
                if tt == 0:
                    mrow, mcol = (64, 128), (192, 4032)
                else:
                    mrow, mcol = (0, 64), (64, 3904)
                out_toks = []
                sev = [None, None, None]
                SB = (0, 1, 5)
                exF = [None, None]
                rcF = [None, None]
                trlast = [None]

                def s1F(h):
                    hp, hh = divmod(h, 2)
                    p0 = 64 * hh
                    sb = h % 2
                    aS = a_Sd[sb]
                    mmt = None
                    for kc in range(8):
                        pS = SB[kc % 3]
                        mmt = top(lambda eng, hp=hp, p0=p0, kc=kc, pS=pS, b=b:
                                  eng.matmul(
                                      pb[pS][:], a_qt[b][p0:p0 + 64, hp, :],
                                      kT[p0:p0 + 64, hp, 512 * kc:512 * kc + 512],
                                      start=True, stop=True),
                                  waits=[qld_t, sev[kc % 3],
                                         st["ctx_ev"][0], st["ctx_ev"][1],
                                         st["s_ev"][0], st["s_ev"][1]]
                                  if kc < 3 else [sev[kc % 3]])
                        sev[kc % 3] = vop(lambda eng, kc=kc, pS=pS, aS=aS:
                                          eng.tensor_copy(
                                              aS[:, 512 * kc:512 * kc + 512],
                                              pb[pS][:]),
                                          waits=[mmt] +
                                          ([exF[sb]] if kc == 0 else []))
                    st["qt_use"][b] = mmt
                    vop(lambda eng, aS=aS: eng.memset(
                        aS[mrow[0]:mrow[1], mcol[0]:mcol[1]], -1e30))
                    nm = vop(lambda eng, aS=aS, sb=sb: eng.reduce_max(
                        a_col[:, 0 + sb:1 + sb], aS[:], AXX, negate=True),
                        waits=[exF[sb]])
                    return nm

                def s2F(h, nm):
                    hp, hh = divmod(h, 2)
                    sb = h % 2
                    aS = a_Sd[sb]
                    ex = sop(lambda eng, aS=aS, sb=sb: eng.activation(
                        a_AF[:], aS[:], Act.Exp, bias=a_col[:, 0 + sb:1 + sb],
                        accum_out=a_col[:, 4 + sb:5 + sb]),
                        waits=[nm, trlast[0], rcF[sb], st["resid_rm"]])
                    exF[sb] = ex
                    rc = vop(lambda eng, sb=sb: eng.reciprocal(
                        a_col[:, 8 + sb:9 + sb], a_col[:, 4 + sb:5 + sb]),
                        waits=[ex, st["ctx_ev"][0]])
                    rcF[sb] = rc
                    cmm = None
                    for ci in range(NCHUNK):
                        j = ci % 2
                        pT = 2 + j
                        tr = top(lambda eng, ci=ci, pT=pT: eng.transpose(
                            pbh[pT][:, 0:128], a_AF[:, 128 * ci:128 * ci + 128],
                            ident[:]), waits=[ex, st["tr_ev"][j]])
                        trlast[0] = tr
                        if j == 0:
                            tre = vop(lambda eng, j=j, pT=pT: eng.tensor_copy(
                                a_aT[j][:, 0:128], pbh[pT][:, 0:128]), waits=[tr])
                        else:
                            tre = sop(lambda eng, j=j, pT=pT: eng.activation(
                                a_aT[j][:, 0:128], pbh[pT][:, 0:128],
                                Act.Identity), waits=[tr])
                        st["tr_ev"][j] = tre
                        cmm = top(lambda eng, ci=ci, h=h, j=j: eng.matmul(
                            pb[4][:, 0:64], a_aT[j][:, 0:128],
                            Vt[:, ci, 64 * h:64 * h + 64],
                            start=(ci == 0), stop=(ci == NCHUNK - 1)),
                            waits=[tre] + ([st["ctx_ev"][0]] if ci == 0 else []))
                    cev = sop(lambda eng, h=h, sb=sb, dst=ctx_dst: eng.activation(
                        dst[:, 64 * h:64 * h + 64], pb[4][:, 0:64], Act.Identity,
                        scale=a_col[:, 8 + sb:9 + sb]), waits=[cmm, rc, gate_dst])
                    st["ctx_ev"][0] = cev
                    st["ctx_ev"][1] = cev
                    out_toks.append(cev)

                nms = {}
                nms[0] = s1F(0)
                if pend is not None:
                    resid_stats(*pend)
                for h in range(1, NH):
                    nms[h] = s1F(h)
                    s2F(h - 1, nms[h - 1])
                s2F(NH - 1, nms[NH - 1])
                st["s_ev"][0] = sev[0]
                st["s_ev"][1] = sev[1]
                st["sev2"] = sev[2]
                return out_toks

            def resid_add(tt, ctx_toks):
                xt = dma("sync", a_x1[:],
                         X1[base_t + 128 * tt:base_t + 128 * tt + 128, :],
                         waits=[st["resid_x1"]], sem="dr")
                rst = st["resid_st"]
                rst = list(rst) if isinstance(rst, tuple) else [rst]
                at_ = gop(lambda eng: eng.tensor_tensor(a_x2[:], a_x1[:],
                                                        a_ctx[:], Alu.add),
                          waits=[xt] + rst + ctx_toks)
                st["resid_x1"] = at_
                stx = dma("gpsimd",
                          X2[base_t + 128 * tt:base_t + 128 * tt + 128, :],
                          a_x2[:], sem="dg")
                return at_, stx

            def resid_stats(tt, at_, stx):
                ttg = NT128S * seq + tt
                sm = sop(lambda eng, ttg=ttg: eng.activation(
                    a_ctx[:], a_x2[:], Act.Identity, accum_out=m3[:, ttg:ttg + 1]),
                    waits=[at_, st["resid_sq"]])
                sq = sop(lambda eng, ttg=ttg: eng.activation(
                    a_ctx[:], a_x2[:], Act.Square, accum_out=ssq3[:, ttg:ttg + 1]))
                st["resid_sq"] = sq
                v1 = vop(lambda eng, ttg=ttg: eng.reduce_max(
                    mx3[:, ttg:ttg + 1], a_x2[:], AXX), waits=[at_])
                v2 = vop(lambda eng, ttg=ttg: eng.tensor_reduce(
                    mn3[:, ttg:ttg + 1], a_x2[:], AXX, Alu.min))
                st["resid_rm"] = v2
                st["resid_st"] = (stx, sm, sq, v2)

            free_tok = None
            laststx = None
            pend = None
            for tt in range(NT128S):
                if tt in (0, NT128S - 1):
                    toks = do_full(tt, a_ctx, free_tok, pend)
                else:
                    toks = do_mid(tt, a_ctx, free_tok, pend)
                pend = None
                free_tok, laststx = resid_add(tt, toks)
                pend = (tt, free_tok, laststx)
            resid_stats(*pend)
            return laststx

        attention(0, kv_a)
        s.full_barrier()
        kv_b = pass3b(1)
        s.full_barrier()
        attention(1, kv_b)
        vop(lambda eng: eng.tensor_scalar(
            m3[:], m3[:], 1.0 / float(H), None, Alu.mult),
            waits=[s.now("tG"), s.now("tS")])
        vdrain()
        vop(lambda eng: eng.tensor_tensor(xm3[:], m3[:], m3[:], Alu.mult))
        vop(lambda eng: eng.tensor_scalar(
            r3[:], ssq3[:], 1.0 / float(H), 1e-5, Alu.mult, Alu.add))
        vdrain()
        vop(lambda eng: eng.tensor_tensor(r3[:], r3[:], xm3[:], Alu.subtract))
        vdrain()
        vop(lambda eng: eng.reciprocal(r3[:], r3[:]))
        vdrain()
        r3_t = sop(lambda eng: eng.activation(r3[:], r3[:], Act.Sqrt),
                   waits=[s.now("tV")])
        vop(lambda eng: eng.tensor_tensor(mx3[:], mx3[:], m3[:], Alu.subtract))
        vop(lambda eng: eng.tensor_tensor(mn3[:], m3[:], mn3[:], Alu.subtract))
        vdrain()
        vop(lambda eng: eng.tensor_tensor(xm3[:], mx3[:], mn3[:], Alu.max))
        vdrain()
        vop(lambda eng: eng.tensor_tensor(xm3[:], xm3[:], r3[:], Alu.mult),
            waits=[r3_t])
        vdrain()
        vop(lambda eng: eng.reduce_max(amax[2][:], xm3[:], AXX))
        vdrain()

    if stage("all"):
        s.full_barrier()
        bc3 = allreduce_scalar(2, s.now("tV"), [(C_S3, "s"), (C_RS3, "rs")])
        vop(lambda eng: eng.tensor_scalar(
            rq3[:], r3[:], scal_b[:, C_RS3:C_RS3 + 1], None, Alu.mult), waits=bc3)
        vop(lambda eng: eng.tensor_scalar(
            gscale[:], swg[:], scal_b[:, C_S3:C_S3 + 1], None, Alu.mult), waits=bc3)
        vop(lambda eng: eng.tensor_scalar(
            uscale[:], swu[:], scal_b[:, C_S3:C_S3 + 1], None, Alu.mult), waits=bc3)
        for hc in range(HC):
            dma("sync", qwgT[:, hc, :], QWG[hc], sem="dw")
            dma("gpsimd", quT[:, hc, :], QWU[hc], sem="dw")
        for ic in range(IC):
            dma("scalar", qwdT[:, ic, :], QWD[ic], sem="dw")
        s.full_barrier()
        amax4_tok = pass2(X2, m3, rq3, 3)
        bc4 = allreduce_scalar(3, amax4_tok, [(C_RS4, "rs"), (C_S4H, "sh")])
        s.full_barrier()
        pass3a(X2, outd, C_S4H, C_RS4)
        s.op("gpsimd", lambda eng: None,
             waits=[s.now(x) for x in Sched.DMA_SEMS])
        s.op("sync", lambda eng: None,
             waits=[s.now(x) for x in Sched.DMA_SEMS])
    else:
        qs = [s.now(q) for q in Sched.DMA_SEMS]
        for e in ENGS:
            s.op(e, lambda eng: None,
                 waits=qs + [s.now("tV"), s.now("tS"), s.now("tP")])

    import contextlib
    es = contextlib.ExitStack()
    with es:
        sems = {}
        for name in ("qS", "qG", "qC", "qV", "dx", "dx0", "dx1", "dh0", "dh1",
                     "dh2", "dh3", "dw", "dk", "dq0", "dq1", "dr", "de", "dc", "dg",
                     "dg0", "dg1", "dg2", "dg3", "tP", "tV", "tS", "tG", "cc", "bar"):
            sems[name] = es.enter_context(nc.semaphore(name))
        s.bar_handle = sems["bar"]
        with nc.Block() as block:
            def mk(ops):
                def body(eng):
                    for fn, ws, inc in ops:
                        wmax = {}
                        for (sname, v) in ws:
                            wmax[sname] = max(wmax.get(sname, 0), v)
                        for idx, (sname, v) in enumerate(wmax.items()):
                            if idx and idx % 2 == 0:
                                eng.nop(nofuse=True)
                            eng.wait_ge(sems[sname], v)
                        if wmax and eng.engine == mybir.EngineType.PE:
                            eng.nop(nofuse=True)
                        inst = fn(eng)
                        if inc is not None:
                            sem, n = inc
                            inst.then_inc(sems[sem], n)
                return body
            block.sync(mk(s.prog["sync"]))
            block.vector(mk(s.prog["vector"]))
            block.scalar(mk(s.prog["scalar"]))
            block.tensor(mk(s.prog["tensor"]))
            block.gpsimd(mk(s.prog["gpsimd"]))
    return nc


def _shard_inputs(inputs):
    x = np.ascontiguousarray(
        np.asarray(inputs["L0_attn_hidden_states"], dtype=np.float32).reshape(-1, H))
    bq = np.asarray(inputs["L0_attn_bq"], dtype=np.float32)
    bk = np.asarray(inputs["L0_attn_bk"], dtype=np.float32)
    bv = np.asarray(inputs["L0_attn_bv"], dtype=np.float32)
    com = {
        "wq_t": np.ascontiguousarray(np.asarray(inputs["L0_attn_Wq"], dtype=np.float32).T),
        "wk_t": np.ascontiguousarray(np.asarray(inputs["L0_attn_Wk"], dtype=np.float32).T),
        "wv_t": np.ascontiguousarray(np.asarray(inputs["L0_attn_Wv"], dtype=np.float32).T),
        "bq2": np.ascontiguousarray(bq.reshape(8, 128).T),
        "bk2": np.ascontiguousarray(bk.reshape(8, 128).T),
        "bv_row": np.ascontiguousarray(bv.reshape(1, H)),
        "w_gate": np.ascontiguousarray(np.asarray(inputs["L0_ffn_w_gate"], dtype=np.float32)),
        "w_up": np.ascontiguousarray(np.asarray(inputs["L0_ffn_w_up"], dtype=np.float32)),
        "w_down": np.ascontiguousarray(np.asarray(inputs["L0_ffn_w_down"], dtype=np.float32)),
        "clip2": np.array([[float(np.asarray(inputs["L0_ffn_clip_x"]).reshape(-1)[0]),
                            float(np.asarray(inputs["L0_ffn_clip_down"]).reshape(-1)[0])]],
                          dtype=np.float32),
    }
    in_maps = []
    for c in range(NCORE):
        m = dict(com)
        m["x0"] = np.ascontiguousarray(x[c * T:(c + 1) * T])
        in_maps.append(m)
    return in_maps


def kernel(**inputs):
    in_maps = _shard_inputs(inputs)
    nc = build_nc(debug=False, upto="all")
    res = run_bass_kernel_spmd(nc, in_maps, core_ids=list(range(NCORE)))
    out = np.concatenate([np.asarray(r["out"], dtype=np.float32) for r in res.results],
                         axis=0)
    return out.reshape(16, 4096, 1024)


# version 10
# speedup vs baseline: 1.0784x; 1.0301x over previous
import numpy as np
import ml_dtypes
import concourse.bass as bass
import concourse.mybir as mybir
from concourse.bass_utils import run_bass_kernel_spmd

F32 = mybir.dt.float32
BF16 = mybir.dt.bfloat16
Alu = mybir.AluOpType
Act = mybir.ActivationFunctionType
AXX = mybir.AxisListType.X
MAGIC = float(1.5 * 2.0 ** 23)

T = 8192
H = 1024
II = 2816
IC = II // 128
HC = H // 128
SEQ = 4096
NH = 16
NCHUNK = SEQ // 128
NT128S = SEQ // 128
NCORE = 8

ENGS = ("sync", "vector", "scalar", "tensor", "gpsimd")
QSEM = {"sync": "qS", "gpsimd": "qG", "scalar": "qC", "vector": "qV"}
C_S1, C_RS1, C_S2, C_RS2, C_S2H, C_S3, C_RS3, C_S4, C_RS4, C_S4H = range(10)


class Sched:
    def __init__(self):
        self.prog = {e: [] for e in ENGS}
        self.cnt = {}
        self.bar_handle = None

    def op(self, eng, fn, waits=(), inc=None, n=1):
        ws = [w for w in waits if w is not None]
        tok = None
        if inc is not None:
            self.cnt[inc] = self.cnt.get(inc, 0) + n
            tok = (inc, self.cnt[inc])
        self.prog[eng].append((fn, ws, (inc, n) if inc else None))
        return tok

    def now(self, sem):
        return (sem, self.cnt.get(sem, 0))

    DMA_SEMS = ("qS", "qG", "qC", "qV", "dx", "dx0", "dx1", "dh0", "dh1",
                "dh2", "dh3", "dw", "dk", "dq0", "dq1", "dr", "de", "dc", "dg",
                "dg0", "dg1", "dg2", "dg3")

    def full_barrier(self):
        qs = [self.now(q) for q in self.DMA_SEMS]
        for e in ENGS:
            w = qs if e == "gpsimd" else []
            self.op(e, lambda eng: eng.nop(nofuse=True),
                    waits=w, inc="bar", n=1)
        val = self.cnt["bar"]
        for e in ENGS:
            self.op(e, lambda eng: None, waits=[("bar", val)])


def build_nc(debug=False, upto="all"):
    nc = bass.Bass("TRN2", target_bir_lowering=False)
    STAGES = ["pass2", "pass3a", "pass3b", "attn", "all"]
    SI = STAGES.index(upto)

    def stage(name):
        return SI >= STAGES.index(name)

    ext_dbg = "ExternalOutput" if debug else "Internal"

    x0d = nc.dram_tensor("x0", [T, H], F32, kind="ExternalInput").ap()
    outd = nc.dram_tensor("out", [T, H], F32, kind="ExternalOutput").ap()
    wqtd = nc.dram_tensor("wq_t", [H, H], F32, kind="ExternalInput").ap()
    wktd = nc.dram_tensor("wk_t", [H, H], F32, kind="ExternalInput").ap()
    wvtd = nc.dram_tensor("wv_t", [H, H], F32, kind="ExternalInput").ap()
    bq2d = nc.dram_tensor("bq2", [128, 8], F32, kind="ExternalInput").ap()
    bk2d = nc.dram_tensor("bk2", [128, 8], F32, kind="ExternalInput").ap()
    bvrd = nc.dram_tensor("bv_row", [1, H], F32, kind="ExternalInput").ap()
    wgd = nc.dram_tensor("w_gate", [II, H], F32, kind="ExternalInput").ap()
    wud = nc.dram_tensor("w_up", [II, H], F32, kind="ExternalInput").ap()
    wdd = nc.dram_tensor("w_down", [H, II], F32, kind="ExternalInput").ap()
    clipd = nc.dram_tensor("clip2", [1, 2], F32, kind="ExternalInput").ap()

    X1 = nc.dram_tensor("X1", [T, H], F32, kind=ext_dbg).ap()
    X2 = nc.dram_tensor("X2", [T, H], F32, kind=ext_dbg).ap()
    HP = nc.dram_tensor("HP", [IC, 128, T], F32, kind=ext_dbg).ap()
    QT = nc.dram_tensor("QTsp", [T // 128, 128, HC, 128], BF16, kind=ext_dbg).ap()
    QWG = nc.dram_tensor("QWG", [HC, 128, II], BF16).ap()
    QWU = nc.dram_tensor("QWU", [HC, 128, II], BF16).ap()
    QWD = nc.dram_tensor("QWD", [IC, 128, H], BF16).ap()
    WQB = nc.dram_tensor("WQB", [HC, 128, H], BF16).ap()
    WKB = nc.dram_tensor("WKB", [HC, 128, H], BF16).ap()
    WVB = nc.dram_tensor("WVB", [HC, 128, H], BF16).ap()
    SCAL = nc.dram_tensor("SCAL", [1, 16], F32).ap()
    RED = [nc.dram_tensor(f"RED{k}", [128, 1], F32).ap() for k in range(4)]
    CCI = [nc.dram_tensor(f"cci{k}", [1, 16], F32) for k in range(4)]
    CCO = [nc.dram_tensor(f"cco{k}", [1, 16], F32) for k in range(4)]
    if debug:
        KDd = nc.dram_tensor("KD", [128, HC, SEQ], BF16, kind="ExternalOutput").ap()
        VDd = nc.dram_tensor("VD", [128, NCHUNK, H], BF16, kind="ExternalOutput").ap()
        DBG1 = nc.dram_tensor("DBG1", [128, 256], F32, kind="ExternalOutput").ap()
        DBGM = nc.dram_tensor("DBGM", [1, 16], F32, kind="ExternalOutput").ap()
        DBGW = nc.dram_tensor("DBGW", [128, 2816], BF16, kind="ExternalOutput").ap()
        DBGQ = nc.dram_tensor("DBGQ", [128, HC, 512], BF16, kind="ExternalOutput").ap()
        X1C = nc.dram_tensor("X1C", [T, H], F32, kind="ExternalOutput").ap()
        QHD = nc.dram_tensor("QHD", [32, 2, 128, 256], BF16, kind="ExternalOutput").ap()

    identd = nc.inline_tensor(np.eye(128, dtype=ml_dtypes.bfloat16), name="identc")

    ARENA = 212832
    nc.alloc_sbuf_tensor("bigarena", [128, ARENA], mybir.dt.uint8)
    base = nc.lookup_mloc("bigarena").addr
    LIMIT = base + ARENA
    cur = [base]

    def at(name, shape, dtype, off=None):
        nbytes = int(np.prod(shape[1:])) * mybir.dt.size(dtype)
        if off is None:
            o = cur[0]
            cur[0] = o + ((nbytes + 31) // 32) * 32
        else:
            o = base + off
        assert o + nbytes <= LIMIT, f"SBUF overflow {name}: {o + nbytes - base}"
        return nc.alloc_sbuf_tensor_at(name, shape, dtype, offset=o)

    ident = at("ident", [128, 128], BF16)
    biasq = at("biasq", [128, 8], F32)
    biask = at("biask", [128, 8], F32)
    bvx = at("bvx", [128, H], F32)
    scal_b = at("scal_b", [128, 16], F32)
    m1 = at("m1", [128, 64], F32)
    r1 = at("r1", [128, 64], F32)
    rq1 = at("rq1", [128, 64], F32)
    m3 = at("m3", [128, 64], F32)
    r3 = at("r3", [128, 64], F32)
    rq3 = at("rq3", [128, 64], F32)
    swg = at("swg", [128, 22], F32)
    swu = at("swu", [128, 22], F32)
    gscale = at("gscale", [128, 22], F32)
    uscale = at("uscale", [128, 22], F32)
    amax = [at(f"amax{k}", [128, 1], F32) for k in range(4)]
    xm1 = at("xm1", [128, 64], F32)
    xm3 = at("xm3", [128, 64], F32)
    biasq8 = at("biasq8", [128, 8], F32)
    red_row = at("red_row", [1, 128], F32)
    misc = at("misc", [1, 16], F32)
    n30 = at("n30", [128, 1], F32)
    ssq3 = at("ssq3", [128, 64], F32)
    mx3 = at("mx3", [128, 64], F32)
    mn3 = at("mn3", [128, 64], F32)
    GLOB_END = cur[0] - base
    assert GLOB_END < 9500, GLOB_END

    WREG = ((GLOB_END + 63) // 64) * 64
    qwgT = at("qwgT", [128, HC, II], BF16, off=WREG)
    quT = at("quT", [128, HC, II], BF16, off=WREG + 45056)
    qwdT = at("qwdT", [128, IC, H], BF16, off=WREG + 90112)
    kT = at("kT", [128, HC, SEQ], BF16, off=WREG)
    Vt = at("Vt", [128, NCHUNK, H], BF16, off=WREG + 65536)
    WORK = WREG + 135168

    def wat(name, shape, dtype, off):
        return at(name, shape, dtype, off=WORK + off)

    wstage = [wat(f"wstage{i}", [128, II], F32, 11264 * i) for i in range(2)]
    qwnat = wat("qwnat", [128, II], BF16, 22528)
    qkvb16 = wat("qkvb16", [128, H], BF16, 28160)
    wsc = wat("wsc", [128, 8], F32, 30208)
    px = [wat("px0", [128, 4, H], F32, 30976)]
    ptmp = wat("ptmp", [128, H], F32, 47360)
    ptmp2 = wat("ptmp2", [128, H], F32, 51456)
    pcol = wat("pcol", [128, 4], F32, 55552)

    p2_x = [wat(f"p2x{i}", [128, 4, H], F32, 16384 * i) for i in range(2)]
    p2_qx = wat("p2qx", [128, 4, H], BF16, 32768)
    p2_qxT = wat("p2qxT", [128, HC, 512], BF16, 40960)
    p2_t = wat("p2t", [128, H], F32, 49152)
    p2_sg = [wat(f"p2sg{i}", [128, 512], F32, 53248 + 2048 * i) for i in range(2)]
    p2_hp = [wat(f"p2hp{i}", [128, 512], F32, 57344 + 2048 * i) for i in range(4)]
    p2_col = wat("p2col", [128, 24], F32, 65536)

    p3_hp = [wat(f"p3hp{i}", [128, 256], F32, 1024 * i) for i in range(4)]
    p3_qh = [wat(f"p3qh{i}", [128, 256], BF16, 4096 + 512 * i) for i in range(4)]
    p3_x0 = [wat("p3x0", [128, 2, H], F32, 6144)]
    p3_x1 = wat("p3x1", [128, 2, H], F32, 14336)

    b_wq = wat("bwq", [128, HC, H], BF16, 0)
    b_wk = wat("bwk", [128, HC, H], BF16, 16384)
    b_wv = wat("bwv", [128, HC, H], BF16, 32768)
    b_x1 = [wat("bx1", [128, 2, H], F32, 49152)]
    b_ln = wat("bln", [128, H], BF16, 57344)
    b_xT = wat("bxT", [128, HC, 256], BF16, 59392)
    b_col = wat("bcol", [128, 8], F32, 63488)
    b_pt = wat("bpt", [128, H], F32, 63520)
    b_qs = wat("bqs", [128, 2, HC, 128], BF16, 63520)

    a_kTe = wat("akTe", [128, HC, 128], BF16, 0)
    a_VtE = wat("aVtE", [128, H], BF16, 2048)
    a_qt = [wat(f"aqt{i}", [128, HC, 128], BF16, 4096 + 2048 * i) for i in range(2)]
    a_A = [wat(f"aA{i}", [128, 512], BF16, 8192 + 1024 * i) for i in range(4)]
    a_aT = [wat(f"aaT{i}", [128, 512], BF16, 12288 + 1024 * i) for i in range(2)]
    a_ctx = wat("actx", [128, H], F32, 14336)
    a_x1 = wat("ax1", [128, H], F32, 18432)
    a_x2 = wat("ax2", [128, H], F32, 22528)
    a_Sd = [wat(f"aS{i}", [128, SEQ], F32, 26624 + 16384 * i) for i in range(2)]
    a_AF = wat("aAF", [128, SEQ], BF16, 59392)
    a_xc = wat("axc", [128, H], F32, 59392)
    a_col = wat("acol", [128, 16], F32, 67584)
    g_col = wat("gcol", [128, 4], F32, 67648)

    pb = [nc.alloc_psum_tensor(f"pb{i}", [128, 512], F32) for i in range(8)]
    pbh = [p.ap().bitcast(BF16) for p in pb]

    s = Sched()

    def dma(q, out, in_, waits=(), sem=None, **kw):
        return s.op(q, lambda eng, o=out, i=in_, kw=kw: eng.dma_start(out=o, in_=i, **kw),
                    waits=waits, inc=(sem or QSEM[q]), n=16)

    def vop(fn, waits=()):
        return s.op("vector", fn, waits=waits, inc="tV")

    def gop(fn, waits=()):
        return s.op("gpsimd", fn, waits=waits, inc="tG")

    def gdrain():
        s.op("gpsimd", lambda eng: eng.drain())

    def sop(fn, waits=()):
        return s.op("scalar", fn, waits=waits, inc="tS")

    def top(fn, waits=()):
        return s.op("tensor", fn, waits=waits, inc="tP")

    def vdrain():
        s.op("vector", lambda eng: eng.drain())

    def sdrain():
        s.op("scalar", lambda eng: eng.drain())

    cast_tok = None
    spill_tok = None
    for wi, (src, dst) in enumerate(((wqtd, WQB), (wktd, WKB), (wvtd, WVB))):
        for hc in range(HC):
            b = hc % 2
            ldt = dma("sync", wstage[b][:, 0:H], src[128 * hc:128 * hc + 128, :],
                      waits=[cast_tok], sem="dw")
            cast_tok = vop(lambda eng, b=b: eng.tensor_copy(qkvb16[:], wstage[b][:, 0:H]),
                           waits=[ldt, spill_tok])
            spill_tok = dma("gpsimd", dst[hc], qkvb16[:], waits=[cast_tok], sem="dg")
    dma("sync", biasq[:], bq2d[:, :], sem="dk")
    dma("sync", biask[:], bk2d[:, :], sem="dk")
    bvx_ld = dma("sync", bvx[:], bvrd[0:1, :].broadcast_to((128, H)), sem="dk")
    dma("sync", ident[:], identd.ap(), sem="dk")
    kc_ld = dma("sync", misc[0:1, 0:2], clipd[0:1, :], sem="dk")
    id_ld = kc_ld
    clip_ld = kc_ld
    vop(lambda eng: eng.tensor_scalar(
        biasq8[:], biasq[:], 0.125, None, Alu.mult), waits=[kc_ld])
    sig_tok = sop(lambda eng: eng.activation(misc[0:1, 2:4], misc[0:1, 0:2],
                                             Act.Sigmoid), waits=[clip_ld])
    vop(lambda eng: eng.memset(n30[:], -1e30))

    def quant_weight_gu(srcd, dstT, sw_dst):
        q_tok = None
        ev_tok = None
        for ic in range(IC):
            b = ic % 2
            ldt = dma("sync", wstage[b][:, 0:H], srcd[128 * ic:128 * ic + 128, :],
                      waits=[q_tok], sem="dw")
            rm = vop(lambda eng, b=b: eng.reduce_max(wsc[:, 0:1], wstage[b][:, 0:H], AXX,
                                                     apply_absolute_value=True),
                     waits=[ldt])
            vdrain()
            vop(lambda eng, ic=ic, d=sw_dst: eng.tensor_scalar(
                d[:, ic:ic + 1], wsc[:, 0:1], 1.0 / 127.0, None, Alu.mult))
            vop(lambda eng: eng.reciprocal(wsc[:, 1:2], wsc[:, 0:1]))
            vdrain()
            vop(lambda eng, b=b: eng.tensor_scalar(
                wstage[b][:, 0:H], wstage[b][:, 0:H], wsc[:, 1:2], 127.0,
                Alu.mult, Alu.mult))
            q_tok = vop(lambda eng, b=b: eng.tensor_scalar(
                qwnat[:, 0:H], wstage[b][:, 0:H], MAGIC, MAGIC, Alu.add, Alu.subtract),
                waits=[ev_tok])
            for hcx in range(HC):
                pbk = 6 + (hcx % 2)
                tr = top(lambda eng, hcx=hcx, pbk=pbk: eng.transpose(
                    pbh[pbk][:, 0:128], qwnat[:, 128 * hcx:128 * hcx + 128], ident[:]),
                    waits=[q_tok, ev_tok, id_ld])
                ev_tok = vop(lambda eng, hcx=hcx, ic=ic, pbk=pbk, d=dstT: eng.tensor_copy(
                    d[:, hcx, 128 * ic:128 * ic + 128], pbh[pbk][:, 0:128]), waits=[tr])
        return ev_tok

    gu_ev = quant_weight_gu(wgd, qwgT, swg)
    gu_ev2 = quant_weight_gu(wud, quT, swu)

    def quant_weight_down():
        ev_tok = None
        q_tok = None
        for hcx in range(HC):
            b = hcx % 2
            ldt = dma("sync", wstage[b][:], wdd[128 * hcx:128 * hcx + 128, :],
                      waits=[q_tok], sem="dw")
            rm = vop(lambda eng, b=b: eng.reduce_max(wsc[:, 0:1], wstage[b][:], AXX,
                                                     apply_absolute_value=True),
                     waits=[ldt])
            vdrain()
            vop(lambda eng: eng.reciprocal(wsc[:, 1:2], wsc[:, 0:1]))
            vop(lambda eng: eng.tensor_scalar(
                wsc[:, 2:3], wsc[:, 0:1], 1.0 / 127.0, None, Alu.mult))
            vdrain()
            vop(lambda eng, b=b: eng.tensor_scalar(
                wstage[b][:], wstage[b][:], wsc[:, 1:2], 127.0, Alu.mult, Alu.mult))
            vop(lambda eng, b=b: eng.tensor_scalar(
                wstage[b][:], wstage[b][:], MAGIC, MAGIC, Alu.add, Alu.subtract))
            q_tok = vop(lambda eng, b=b: eng.tensor_scalar(
                qwnat[:], wstage[b][:], wsc[:, 2:3], None, Alu.mult),
                waits=[ev_tok])
            for ic in range(IC):
                pbk = 6 + (ic % 2)
                tr = top(lambda eng, ic=ic, pbk=pbk: eng.transpose(
                    pbh[pbk][:, 0:128], qwnat[:, 128 * ic:128 * ic + 128], ident[:]),
                    waits=[q_tok, ev_tok, id_ld])
                ev_tok = vop(lambda eng, ic=ic, hcx=hcx, pbk=pbk: eng.tensor_copy(
                    qwdT[:, ic, 128 * hcx:128 * hcx + 128], pbh[pbk][:, 0:128]),
                    waits=[tr])
        return ev_tok

    dn_ev = quant_weight_down()
    for hc in range(HC):
        dma("gpsimd", QWG[hc], qwgT[:, hc, :], waits=[gu_ev], sem="dg")
        dma("gpsimd", QWU[hc], quT[:, hc, :], waits=[gu_ev2], sem="dg")
    for ic in range(IC):
        dma("gpsimd", QWD[ic], qwdT[:, ic, :], waits=[dn_ev], sem="dg")

    ld_tok = [None, None]
    q_done = [None, None]
    sq_prev = None
    for t in range(16):
        b = 0
        ld_tok[b] = dma("sync", px[b][:],
                        x0d[512 * t:512 * t + 512, :].rearrange("(a p) h -> p a h", p=128),
                        waits=[q_done[b]], sem="dx")
        for sub in range(4):
            tt = 4 * t + sub
            vop(lambda eng, b=b, sub=sub: eng.tensor_reduce(
                pcol[:, 0:1], px[b][:, sub, :], AXX, Alu.add), waits=[ld_tok[b]])
            vdrain()
            vop(lambda eng, tt=tt: eng.tensor_scalar(
                m1[:, tt:tt + 1], pcol[:, 0:1], 1.0 / float(H), None, Alu.mult))
            vdrain()
            xc_t = vop(lambda eng, b=b, sub=sub, tt=tt: eng.tensor_scalar(
                ptmp[:], px[b][:, sub, :], m1[:, tt:tt + 1], None, Alu.subtract),
                waits=[sq_prev])
            sq_prev = sop(lambda eng: eng.activation(ptmp2[:], ptmp[:], Act.Square,
                                                     accum_out=pcol[:, 1:2]),
                          waits=[xc_t])
            vop(lambda eng: eng.tensor_scalar(
                pcol[:, 1:2], pcol[:, 1:2], 1.0 / float(H), 1e-5, Alu.mult, Alu.add),
                waits=[sq_prev])
            vdrain()
            vop(lambda eng: eng.reciprocal(pcol[:, 1:2], pcol[:, 1:2]))
            vdrain()
            r_t = sop(lambda eng, tt=tt: eng.activation(
                r1[:, tt:tt + 1], pcol[:, 1:2], Act.Sqrt), waits=[s.now("tV")])
            q_done[b] = vop(lambda eng, tt=tt: eng.reduce_max(
                xm1[:, tt:tt + 1], ptmp[:], AXX,
                apply_absolute_value=True), waits=[sq_prev])
    vdrain()
    vop(lambda eng: eng.tensor_tensor(xm1[:], xm1[:], r1[:], Alu.mult),
        waits=[s.now("tS")])
    vdrain()
    vop(lambda eng: eng.reduce_max(amax[0][:], xm1[:], AXX))
    vdrain()

    def allreduce_scalar(k, amax_tok, extra_cols):
        sig_col = 2 if k in (0, 2) else 3
        t1 = dma("gpsimd", RED[k][:, :], amax[k][:, 0:1], waits=[amax_tok], sem="dc")
        t2 = dma("gpsimd", red_row[0:1, :], RED[k].transpose([1, 0]), waits=[t1],
                 allow_slow_non_contiguous=True, sem="dc")
        mx = vop(lambda eng: eng.reduce_max(misc[0:1, 4:5], red_row[0:1, :], AXX),
                 waits=[t2])
        t3 = dma("gpsimd", CCI[k].ap()[0:1, 0:1], misc[0:1, 4:5], waits=[mx], sem="dc")
        cc_t = s.op("gpsimd", lambda eng, k=k: eng.collective_compute(
            "AllReduce", Alu.max, replica_groups=[list(range(NCORE))],
            ins=[CCI[k].ap().opt()], outs=[CCO[k].ap().opt()]),
            waits=[t3], inc="cc", n=1)
        t4 = dma("gpsimd", misc[0:1, 4:5], CCO[k].ap()[0:1, 0:1], waits=[cc_t], sem="dc")
        vop(lambda eng, sc=sig_col: eng.tensor_tensor(
            misc[0:1, 5:6], misc[0:1, 4:5], misc[0:1, sc:sc + 1], Alu.mult),
            waits=[t4, sig_tok])
        vdrain()
        vop(lambda eng: eng.tensor_scalar(
            misc[0:1, 6:7], misc[0:1, 5:6], 1.0 / 127.0, None, Alu.mult))
        vdrain()
        rsv = vop(lambda eng: eng.reciprocal(misc[0:1, 7:8], misc[0:1, 6:7]))
        shv = vop(lambda eng: eng.tensor_scalar(
            misc[0:1, 8:9], misc[0:1, 6:7], 0.5, None, Alu.mult))
        vdrain()
        toks = []
        for (col, tag) in extra_cols:
            srcc = {"s": 6, "rs": 7, "sh": 8}[tag]
            t5 = dma("gpsimd", SCAL[0:1, col:col + 1], misc[0:1, srcc:srcc + 1],
                     waits=[rsv, shv], sem="dc")
            toks.append(dma("gpsimd", scal_b[:, col:col + 1],
                            SCAL[0:1, col:col + 1].broadcast_to((128, 1)), waits=[t5],
                            sem="dc"))
        return toks

    bc1 = allreduce_scalar(0, s.now("tV"), [(C_S1, "s"), (C_RS1, "rs")])
    vop(lambda eng: eng.tensor_scalar(
        rq1[:], r1[:], scal_b[:, C_RS1:C_RS1 + 1], None, Alu.mult), waits=bc1)
    vop(lambda eng: eng.tensor_scalar(
        gscale[:], swg[:], scal_b[:, C_S1:C_S1 + 1], None, Alu.mult),
        waits=bc1 + [gu_ev])
    vop(lambda eng: eng.tensor_scalar(
        uscale[:], swu[:], scal_b[:, C_S1:C_S1 + 1], None, Alu.mult),
        waits=bc1 + [gu_ev2])
    s.full_barrier()

    def pass2(xsrc, mcache, rqcache, amax_idx):
        vop(lambda eng, k=amax_idx: eng.memset(amax[k][:], 0.0))
        ld_tok = [None, None]
        lastq = [None, None]
        silu_tok = [None, None]
        hp_tok = [None, None, None, None]
        hpdma_tok = [None, None, None, None]
        evq_tok = [None, None]
        tr_last = None
        amax_tok = None
        for t in range(16):
            b = t % 2
            ld_tok[b] = dma("sync", p2_x[b][:],
                            xsrc[512 * t:512 * t + 512, :].rearrange("(a p) h -> p a h",
                                                                     p=128),
                            waits=[lastq[b]], sem=f"dx{b}")
            for sub in range(4):
                tt = 4 * t + sub
                vop(lambda eng, b=b, sub=sub, tt=tt, mc=mcache, rc=rqcache:
                    eng.tensor_scalar(p2_t[:], p2_x[b][:, sub, :], mc[:, tt:tt + 1],
                                      rc[:, tt:tt + 1], Alu.subtract, Alu.mult),
                    waits=[ld_tok[b]] + ([tr_last] if sub == 0 else []))
                vop(lambda eng: eng.tensor_scalar(
                    p2_t[:], p2_t[:], MAGIC, MAGIC, Alu.add, Alu.subtract))
                lastq[b] = vop(lambda eng, sub=sub: eng.tensor_scalar(
                    p2_qx[:, sub, :], p2_t[:], 127.0, -128.0, Alu.min, Alu.max))
            for sub in range(4):
                for hc in range(HC):
                    j = (sub * HC + hc) % 2
                    pbk = 4 + j
                    tr = top(lambda eng, sub=sub, hc=hc, pbk=pbk: eng.transpose(
                        pbh[pbk][:, 0:128], p2_qx[:, sub, 128 * hc:128 * hc + 128],
                        ident[:]), waits=[lastq[b], evq_tok[j]])
                    evq_tok[j] = vop(lambda eng, sub=sub, hc=hc, pbk=pbk: eng.tensor_copy(
                        p2_qxT[:, hc, 128 * sub:128 * sub + 128], pbh[pbk][:, 0:128]),
                        waits=[tr])
            tr_ev = [evq_tok[0], evq_tok[1]]
            mm_g = None
            mm_u = None
            for ic in range(IC):
                pg = ic % 2
                pu = 2 + ic % 2
                for hc in range(HC):
                    mm_g = top(lambda eng, ic=ic, hc=hc, pg=pg: eng.matmul(
                        pb[pg][:], qwgT[:, hc, 128 * ic:128 * ic + 128],
                        p2_qxT[:, hc, :],
                        start=(hc == 0), stop=(hc == HC - 1)),
                        waits=((tr_ev if ic == 0 else []) + [silu_tok[ic % 2]])
                        if hc == 0 else [])
                for hc in range(HC):
                    mm_u = top(lambda eng, ic=ic, hc=hc, pu=pu: eng.matmul(
                        pb[pu][:], quT[:, hc, 128 * ic:128 * ic + 128],
                        p2_qxT[:, hc, :],
                        start=(hc == 0), stop=(hc == HC - 1)),
                        waits=[hp_tok[(ic + 2) % 4]] if hc == 0 else [])
                silu_tok[ic % 2] = sop(lambda eng, ic=ic, pg=pg: eng.activation(
                    p2_sg[ic % 2][:], pb[pg][:], Act.Silu, scale=gscale[:, ic:ic + 1]),
                    waits=[mm_g, hp_tok[(ic + 2) % 4]])
                hb = ic % 4
                hp_tok[hb] = vop(lambda eng, ic=ic, pu=pu, hb=hb:
                                 eng.scalar_tensor_tensor(
                                     p2_hp[hb][:], pb[pu][:], uscale[:, ic:ic + 1],
                                     p2_sg[ic % 2][:], Alu.mult, Alu.mult),
                                 waits=[mm_u, silu_tok[ic % 2], hpdma_tok[hb]])
                vop(lambda eng, hb=hb, ic=ic: eng.reduce_max(
                    p2_col[:, ic:ic + 1], p2_hp[hb][:], AXX,
                    apply_absolute_value=True))
                hpdma_tok[hb] = dma("gpsimd" if hb % 2 == 0 else "scalar",
                                    HP[ic, :, 512 * t:512 * t + 512],
                                    p2_hp[hb][:], waits=[hp_tok[hb]], sem=f"dg{hb}")
            vdrain()
            vop(lambda eng: eng.reduce_max(p2_col[:, 22:23], p2_col[:, 0:IC], AXX))
            vdrain()
            amax_tok = vop(lambda eng, k=amax_idx: eng.tensor_tensor(
                amax[k][:], amax[k][:], p2_col[:, 22:23], Alu.max))
            vdrain()
            tr_last = mm_g
        return amax_tok

    amax2_tok = pass2(x0d, m1, rq1, 1)
    bc2 = allreduce_scalar(1, amax2_tok, [(C_RS2, "rs"), (C_S2H, "sh")])
    s.full_barrier()
    if debug:
        dma("gpsimd", DBG1[:, 0:1], amax[0][:], allow_slow_non_contiguous=True, sem="dg")
        dma("gpsimd", DBG1[:, 1:2], amax[1][:], allow_slow_non_contiguous=True, sem="dg")
        dma("gpsimd", DBG1[:, 16:32], scal_b[:], allow_slow_non_contiguous=True, sem="dg")
        dma("gpsimd", DBG1[:, 32:96], m1[:], allow_slow_non_contiguous=True, sem="dg")
        dma("gpsimd", DBG1[:, 96:160], r1[:], allow_slow_non_contiguous=True, sem="dg")
        dma("gpsimd", DBG1[:, 160:224], rq1[:], allow_slow_non_contiguous=True, sem="dg")
        dma("gpsimd", DBG1[:, 224:246], gscale[:], allow_slow_non_contiguous=True, sem="dg")
        dma("gpsimd", DBGM[:, :], misc[:], sem="dg")
        dma("gpsimd", DBGW[:, :], qwgT[:, 0, :], sem="dg")
        dma("gpsimd", DBGQ[:, :, :], p2_qxT[:], sem="dg")
        s.full_barrier()

    if stage("pass3a"):
        def pass3a(xres, dstd, shcol, rscol):
            ld_tok = [None] * 4
            qh_tok = [None] * 4
            mmh = [None] * 4
            mm_tok = None
            ev_toks = [None] * 4
            stq = None
            last_store = None
            for u in range(32):
                x0ld = dma("sync", p3_x0[0][:],
                           xres[256 * u:256 * u + 256, :].rearrange(
                               "(a p) h -> p a h", p=128),
                           waits=[stq], sem="dx")
                for ic in range(IC):
                    hb = ic % 4
                    ld_tok[hb] = dma("sync" if ic % 2 == 0 else "scalar",
                                     p3_hp[hb][:],
                                     HP[ic, :, 256 * u:256 * u + 256],
                                     waits=[qh_tok[hb]], sem=f"dh{hb}")
                    vop(lambda eng, hb=hb, rs=rscol: eng.tensor_scalar(
                        p3_hp[hb][:], p3_hp[hb][:], scal_b[:, rs:rs + 1], MAGIC,
                        Alu.mult, Alu.add), waits=[ld_tok[hb]])
                    vop(lambda eng, hb=hb: eng.tensor_scalar(
                        p3_hp[hb][:], p3_hp[hb][:], MAGIC, 127.0,
                        Alu.subtract, Alu.min))
                    qh_tok[hb] = vop(lambda eng, hb=hb: eng.tensor_scalar(
                        p3_qh[hb][:], p3_hp[hb][:], -128.0, None, Alu.max),
                        waits=[mmh[hb]])
                    if debug and ic in (0, 1) and dstd is X1:
                        dma("gpsimd", QHD[u, ic], p3_qh[hb][:], waits=[qh_tok[hb]],
                            sem="dg")
                    for th in range(2):
                        for fh in range(2):
                            pbk = 2 * th + fh
                            mm_tok = top(lambda eng, ic=ic, th=th, fh=fh, hb=hb,
                                         pbk=pbk:
                                         eng.matmul(
                                             pb[pbk][:],
                                             p3_qh[hb][:, 128 * th:128 * th + 128],
                                             qwdT[:, ic, 512 * fh:512 * fh + 512],
                                             start=(ic == 0), stop=(ic == IC - 1)),
                                         waits=[qh_tok[hb]] +
                                         ([ev_toks[pbk]] if ic == 0 else []))
                    mmh[hb] = mm_tok
                for th in range(2):
                    for fh in range(2):
                        pbk = 2 * th + fh
                        ev_toks[pbk] = vop(
                            lambda eng, th=th, fh=fh, pbk=pbk, sh=shcol:
                            eng.scalar_tensor_tensor(
                                p3_x1[:, th, 512 * fh:512 * fh + 512],
                                pb[pbk][:], scal_b[:, sh:sh + 1],
                                p3_x0[0][:, th, 512 * fh:512 * fh + 512],
                                Alu.mult, Alu.add),
                            waits=[mm_tok, x0ld, stq])
                stq = dma("gpsimd",
                          dstd[256 * u:256 * u + 256, :].rearrange(
                              "(a p) h -> p a h", p=128),
                          p3_x1[:], waits=[ev_toks[3], ev_toks[0], ev_toks[1],
                                           ev_toks[2]], sem="dg")
                last_store = stq
            return last_store

        pass3a(x0d, X1, C_S2H, C_RS2)
        s.full_barrier()
        if debug:
            for t in range(16):
                dma("gpsimd", X1C[512 * t:512 * t + 512, :], X1[512 * t:512 * t + 512, :],
                    sem="dg")
            s.full_barrier()

    if stage("pass3b"):
        def pass3b(seq):
            base_t = SEQ * seq
            wtoks = []
            for hc in range(HC):
                wtoks.append(dma("sync", b_wq[:, hc, :], WQB[hc], sem="dw"))
                wtoks.append(dma("sync", b_wk[:, hc, :], WKB[hc], sem="dw"))
                wtoks.append(dma("sync", b_wv[:, hc, :], WVB[hc], sem="dw"))
            wq_ld = wtoks[-1]
            lastuse = None
            ev_xT = [None, None]
            qk_ev = [None, None]
            v_ev = [None, None]
            qs_dma = None
            sq_prev2 = None
            k_last = None
            v_last = None
            for u in range(16):
                ld_tok = dma("sync", b_x1[0][:],
                             X1[base_t + 256 * u: base_t + 256 * u + 256, :]
                             .rearrange("(a p) h -> p a h", p=128),
                             waits=[lastuse], sem="dx")
                for th in range(2):
                    vop(lambda eng, th=th: eng.tensor_reduce(
                        b_col[:, 0:1], b_x1[0][:, th, :], AXX, Alu.add),
                        waits=[ld_tok])
                    vdrain()
                    vop(lambda eng: eng.tensor_scalar(
                        b_col[:, 1:2], b_col[:, 0:1], 1.0 / float(H), None, Alu.mult))
                    vdrain()
                    xc_t = vop(lambda eng, th=th: eng.tensor_scalar(
                        b_pt[:], b_x1[0][:, th, :], b_col[:, 1:2], None, Alu.subtract),
                        waits=[sq_prev2, qs_dma])
                    sq_prev2 = sop(lambda eng: eng.activation(
                        b_ln[:], b_pt[:], Act.Square, accum_out=b_col[:, 2:3]),
                        waits=[xc_t, ev_xT[0], ev_xT[1]])
                    vop(lambda eng: eng.tensor_scalar(
                        b_col[:, 2:3], b_col[:, 2:3], 1.0 / float(H), 1e-5,
                        Alu.mult, Alu.add), waits=[sq_prev2])
                    vdrain()
                    vop(lambda eng: eng.reciprocal(b_col[:, 2:3], b_col[:, 2:3]))
                    vdrain()
                    r_t = sop(lambda eng: eng.activation(
                        b_col[:, 3:4], b_col[:, 2:3], Act.Sqrt), waits=[s.now("tV")])
                    lastuse = vop(lambda eng: eng.tensor_scalar(
                        b_ln[:], b_pt[:], b_col[:, 3:4], None, Alu.mult),
                        waits=[r_t])
                    for hc in range(HC):
                        j = hc % 2
                        pbk = 6 + j
                        tr = top(lambda eng, hc=hc, pbk=pbk: eng.transpose(
                            pbh[pbk][:, 0:128], b_ln[:, 128 * hc:128 * hc + 128],
                            ident[:]), waits=[lastuse, ev_xT[j]])
                        ev_xT[j] = vop(lambda eng, th=th, hc=hc, pbk=pbk:
                                       eng.tensor_copy(
                                           b_xT[:, hc, 128 * th:128 * th + 128],
                                           pbh[pbk][:, 0:128]), waits=[tr])
                exT = [ev_xT[0], ev_xT[1]]
                mm_qk = None
                for which in range(2):
                    wsrc = b_wq if which == 0 else b_wk
                    for fc in range(HC):
                        pbk = which
                        for hc in range(HC):
                            mm_qk = top(lambda eng, wsrc=wsrc, fc=fc, hc=hc, pbk=pbk:
                                        eng.matmul(
                                            pb[pbk][:, 0:256],
                                            wsrc[:, hc, 128 * fc:128 * fc + 128],
                                            b_xT[:, hc, :],
                                            start=(hc == 0), stop=(hc == HC - 1)),
                                        waits=((exT + [wq_ld] if fc == 0 else [])
                                               + [qk_ev[pbk]])
                                        if hc == 0 else [])
                        if which == 0:
                            qk_ev[pbk] = sop(lambda eng, fc=fc, pbk=pbk: eng.activation(
                                b_qs[:, :, fc, :], pb[pbk][:, 0:256], Act.Identity,
                                bias=biasq8[:, fc:fc + 1], scale=0.125),
                                waits=[mm_qk] + ([qs_dma] if fc == 0 else []))
                        else:
                            qk_ev[pbk] = sop(lambda eng, fc=fc, pbk=pbk, u=u:
                                             eng.activation(
                                                 kT[:, fc, 256 * u:256 * u + 256],
                                                 pb[pbk][:, 0:256], Act.Identity,
                                                 bias=biask[:, fc:fc + 1]),
                                             waits=[mm_qk])
                            k_last = qk_ev[pbk]
                for j in range(2):
                    blk = 32 * seq + 2 * u + j
                    qs_dma = dma("gpsimd", QT[blk], b_qs[:, j], waits=[qk_ev[0]],
                                 sem="dg")
                mm_v = None
                for th in range(2):
                    for fh in range(2):
                        pbk = 2 + fh
                        for hc in range(HC):
                            mm_v = top(lambda eng, th=th, fh=fh, hc=hc, pbk=pbk:
                                       eng.matmul(
                                           pb[pbk][:],
                                           b_xT[:, hc, 128 * th:128 * th + 128],
                                           b_wv[:, hc, 512 * fh:512 * fh + 512],
                                           start=(hc == 0), stop=(hc == HC - 1)),
                                       waits=(exT + [v_ev[fh], wq_ld])
                                       if hc == 0 else [])
                        ch = 2 * u + th
                        v_ev[fh] = vop(lambda eng, ch=ch, fh=fh, pbk=pbk:
                                       eng.tensor_tensor(
                                           Vt[:, ch, 512 * fh:512 * fh + 512],
                                           pb[pbk][:], bvx[:, 512 * fh:512 * fh + 512],
                                           Alu.add), waits=[mm_v, bvx_ld])
                        v_last = v_ev[fh]
            return k_last, v_last

        kv_a = pass3b(0)
        if debug:
            for fc in range(HC):
                dma("gpsimd", KDd[:, fc, :], kT[:, fc, :],
                    waits=[kv_a[0], kv_a[1]], sem="dg")
            for ch in range(NCHUNK):
                dma("gpsimd", VDd[:, ch, :], Vt[:, ch, :],
                    waits=[kv_a[0], kv_a[1]], sem="dg")
        s.full_barrier()

    if stage("attn"):

        def attention(seq, kv):
            base_t = SEQ * seq
            e1 = dma("gpsimd", a_VtE[0:64, :], Vt[0:64, 0, :], waits=list(kv),
                     sem="de")
            e2 = dma("gpsimd", a_VtE[64:128, :], Vt[64:128, NCHUNK - 1, :],
                     waits=list(kv), sem="de")
            e3 = vop(lambda eng: eng.tensor_copy(a_kTe[:, :, 0:64], kT[:, :, 0:64]),
                     waits=list(kv))
            e4 = vop(lambda eng: eng.tensor_copy(a_kTe[:, :, 64:128],
                                                 kT[:, :, SEQ - 64:SEQ]))

            st = {
                "qld": [None, None],
                "qt_use": [None, None],
                "s_ev": [None, None, None, None],
                "tr_ev": [None, None],
                "A_use": [None, None, None, None],
                "ctx_ev": [None, None],
                "resid_x1": None, "resid_st": None, "resid_xc": None,
                "resid_sq": None, "resid_rm": None,
            }

            def load_qt(tt):
                b = tt % 2
                qld_t = dma("sync", a_qt[b][:], QT[32 * seq + tt],
                            waits=[st["qt_use"][b], st["qld"][b]], sem=f"dq{b}")
                st["qld"][b] = qld_t
                return b, qld_t

            def do_mid(tt, ctx_dst, gate_dst, pend):
                b, qld_t = load_qt(tt)
                vc = tt - 1
                segs = [("edge", None), ("win", vc), ("win", vc + 1), ("win", vc + 2)]
                mask_list = [(0, 64, 128, 192), (0, 64, 384, 512),
                             (64, 128, 128, 256), (64, 128, 448, 512)]
                SBANK = (0, 1, 6, 7)
                out_toks = []

                def stage1(j):
                    hp, hh = divmod(j, 2)
                    p0 = 64 * hh
                    sl = j % 4
                    pS = SBANK[sl]
                    w1 = [qld_t, st["s_ev"][sl]] if j < 4 else [st["s_ev"][sl]]
                    if j == 0:
                        w1 += [kv[0], kv[1], e4]
                    top(lambda eng, p0=p0, hp=hp, pS=pS, b=b:
                        eng.matmul(pb[pS][:, 0:128],
                                   a_qt[b][p0:p0 + 64, hp, :],
                                   a_kTe[p0:p0 + 64, hp, :],
                                   start=True, stop=False),
                        waits=w1)
                    k0 = 128 * vc
                    mmt = top(lambda eng, p0=p0, hp=hp, pS=pS, b=b, k0=k0:
                              eng.matmul(pb[pS][:, 128:512],
                                         a_qt[b][p0:p0 + 64, hp, :],
                                         kT[p0:p0 + 64, hp, k0:k0 + 384],
                                         start=False, stop=True,
                                         skip_group_check=True))
                    st["qt_use"][b] = mmt
                    mt = None
                    for (r0, r1_, c0, c1) in mask_list:
                        mt = vop(lambda eng, pS=pS, r0=r0, r1_=r1_, c0=c0, c1=c1:
                                 eng.memset(pb[pS][r0:r1_, c0:c1], -1e30),
                                 waits=[mmt])
                    nm = vop(lambda eng, pS=pS, sl=sl: eng.reduce_max(
                        a_col[:, sl:sl + 1], pb[pS][:], AXX, negate=True),
                        waits=[mt])
                    ex = sop(lambda eng, pS=pS, sl=sl: eng.activation(
                        a_A[sl][:], pb[pS][:], Act.Exp,
                        bias=a_col[:, sl:sl + 1],
                        accum_out=a_col[:, 4 + sl:5 + sl]),
                        waits=[nm, st["A_use"][sl]])
                    st["s_ev"][sl] = ex
                    return ex

                def stage2(j, ex):
                    hp, hh = divmod(j, 2)
                    h = 2 * hp + hh
                    sl = j % 4
                    rc = vop(lambda eng, sl=sl: eng.reciprocal(
                        a_col[:, 8 + sl:9 + sl], a_col[:, 4 + sl:5 + sl]),
                        waits=[ex, st["ctx_ev"][hh]])
                    cmm = None
                    for ci in range(4):
                        pT = 2 + (ci % 2)
                        tr = top(lambda eng, sl=sl, ci=ci, pT=pT: eng.transpose(
                            pbh[pT][:, 0:128],
                            a_A[sl][:, 128 * ci:128 * ci + 128], ident[:]),
                            waits=[ex, st["tr_ev"][ci % 2]])
                        st["A_use"][sl] = tr
                        if ci % 2 == 0:
                            tre = vop(lambda eng, hh=hh, ci=ci, pT=pT:
                                      eng.tensor_copy(
                                          a_aT[hh][:, 128 * ci:128 * ci + 128],
                                          pbh[pT][:, 0:128]), waits=[tr])
                        else:
                            tre = sop(lambda eng, hh=hh, ci=ci, pT=pT:
                                      eng.activation(
                                          a_aT[hh][:, 128 * ci:128 * ci + 128],
                                          pbh[pT][:, 0:128], Act.Identity),
                                      waits=[tr])
                        st["tr_ev"][ci % 2] = tre
                        styp, c = segs[ci]
                        pC = 4 + hh
                        extra = [st["ctx_ev"][hh]] if ci == 0 else []
                        if ci == 0 and hh == 1 and st.get("sev2"):
                            extra.append(st.pop("sev2"))
                        if styp == "edge":
                            cmm = top(lambda eng, hh=hh, ci=ci, h=h, pC=pC:
                                      eng.matmul(
                                          pb[pC][:, 0:64],
                                          a_aT[hh][:, 128 * ci:128 * ci + 128],
                                          a_VtE[:, 64 * h:64 * h + 64],
                                          start=(ci == 0), stop=(ci == 3)),
                                      waits=[tre, e2] + extra)
                        else:
                            cmm = top(lambda eng, hh=hh, ci=ci, h=h, c=c, pC=pC:
                                      eng.matmul(
                                          pb[pC][:, 0:64],
                                          a_aT[hh][:, 128 * ci:128 * ci + 128],
                                          Vt[:, c, 64 * h:64 * h + 64],
                                          start=(ci == 0), stop=(ci == 3)),
                                      waits=[tre] + extra)
                    cev = sop(lambda eng, h=h, sl=sl, pC=4 + hh, dst=ctx_dst:
                              eng.activation(dst[:, 64 * h:64 * h + 64],
                                             pb[pC][:, 0:64], Act.Identity,
                                             scale=a_col[:, 8 + sl:9 + sl]),
                              waits=[cmm, rc, gate_dst])
                    st["ctx_ev"][hh] = cev
                    out_toks.append(cev)

                exs = {}
                for j in range(4):
                    exs[j] = stage1(j)
                if pend is not None:
                    resid_stats(*pend)
                for j in range(4, 16):
                    stage2(j - 4, exs[j - 4])
                    exs[j] = stage1(j)
                for j in range(12, 16):
                    stage2(j, exs[j])
                return out_toks

            def do_full(tt, ctx_dst, gate_dst, pend):
                b, qld_t = load_qt(tt)
                if tt == 0:
                    mrow, mcol = (64, 128), (192, 4032)
                else:
                    mrow, mcol = (0, 64), (64, 3904)
                out_toks = []
                sev = [None, None, None]
                SB = (0, 1, 5)
                exF = [None, None]
                rcF = [None, None]
                trlast = [None]

                def s1F(h):
                    hp, hh = divmod(h, 2)
                    p0 = 64 * hh
                    sb = h % 2
                    aS = a_Sd[sb]
                    mmt = None
                    for kc in range(8):
                        pS = SB[kc % 3]
                        if kc < 3:
                            wf = [sev[kc % 3]]
                            if h == 0:
                                wf += [qld_t, st["ctx_ev"][0], st["ctx_ev"][1],
                                       st["s_ev"][0], st["s_ev"][1], kv[0], kv[1]]
                        else:
                            wf = [sev[kc % 3]]
                        mmt = top(lambda eng, hp=hp, p0=p0, kc=kc, pS=pS, b=b:
                                  eng.matmul(
                                      pb[pS][:], a_qt[b][p0:p0 + 64, hp, :],
                                      kT[p0:p0 + 64, hp, 512 * kc:512 * kc + 512],
                                      start=True, stop=True),
                                  waits=wf)
                        sev[kc % 3] = vop(lambda eng, kc=kc, pS=pS, aS=aS:
                                          eng.tensor_copy(
                                              aS[:, 512 * kc:512 * kc + 512],
                                              pb[pS][:]),
                                          waits=[mmt] +
                                          ([exF[sb]] if kc == 0 else []))
                    st["qt_use"][b] = mmt
                    vop(lambda eng, aS=aS: eng.memset(
                        aS[mrow[0]:mrow[1], mcol[0]:mcol[1]], -1e30))
                    nm = vop(lambda eng, aS=aS, sb=sb: eng.reduce_max(
                        a_col[:, 0 + sb:1 + sb], aS[:], AXX, negate=True),
                        waits=[exF[sb]])
                    return nm

                def s2F(h, nm):
                    hp, hh = divmod(h, 2)
                    sb = h % 2
                    aS = a_Sd[sb]
                    ex = sop(lambda eng, aS=aS, sb=sb: eng.activation(
                        a_AF[:], aS[:], Act.Exp, bias=a_col[:, 0 + sb:1 + sb],
                        accum_out=a_col[:, 4 + sb:5 + sb]),
                        waits=[nm, trlast[0], rcF[sb], st["resid_rm"]])
                    exF[sb] = ex
                    rc = vop(lambda eng, sb=sb: eng.reciprocal(
                        a_col[:, 8 + sb:9 + sb], a_col[:, 4 + sb:5 + sb]),
                        waits=[ex, st["ctx_ev"][0]])
                    rcF[sb] = rc
                    cmm = None
                    for ci in range(NCHUNK):
                        j = ci % 2
                        pT = 2 + j
                        tr = top(lambda eng, ci=ci, pT=pT: eng.transpose(
                            pbh[pT][:, 0:128], a_AF[:, 128 * ci:128 * ci + 128],
                            ident[:]), waits=[ex, st["tr_ev"][j]])
                        trlast[0] = tr
                        if j == 0:
                            tre = vop(lambda eng, j=j, pT=pT: eng.tensor_copy(
                                a_aT[j][:, 0:128], pbh[pT][:, 0:128]), waits=[tr])
                        else:
                            tre = sop(lambda eng, j=j, pT=pT: eng.activation(
                                a_aT[j][:, 0:128], pbh[pT][:, 0:128],
                                Act.Identity), waits=[tr])
                        st["tr_ev"][j] = tre
                        cmm = top(lambda eng, ci=ci, h=h, j=j: eng.matmul(
                            pb[4][:, 0:64], a_aT[j][:, 0:128],
                            Vt[:, ci, 64 * h:64 * h + 64],
                            start=(ci == 0), stop=(ci == NCHUNK - 1)),
                            waits=[tre] + ([st["ctx_ev"][0]] if ci == 0 else []))
                    cev = sop(lambda eng, h=h, sb=sb, dst=ctx_dst: eng.activation(
                        dst[:, 64 * h:64 * h + 64], pb[4][:, 0:64], Act.Identity,
                        scale=a_col[:, 8 + sb:9 + sb]), waits=[cmm, rc, gate_dst])
                    st["ctx_ev"][0] = cev
                    st["ctx_ev"][1] = cev
                    out_toks.append(cev)

                nms = {}
                nms[0] = s1F(0)
                if pend is not None:
                    resid_stats(*pend)
                for h in range(1, NH):
                    nms[h] = s1F(h)
                    s2F(h - 1, nms[h - 1])
                s2F(NH - 1, nms[NH - 1])
                st["s_ev"][0] = sev[0]
                st["s_ev"][1] = sev[1]
                st["sev2"] = sev[2]
                return out_toks

            def resid_add(tt, ctx_toks):
                xt = dma("sync", a_x1[:],
                         X1[base_t + 128 * tt:base_t + 128 * tt + 128, :],
                         waits=[st["resid_x1"]], sem="dr")
                rst = st["resid_st"]
                rst = list(rst) if isinstance(rst, tuple) else [rst]
                at_ = gop(lambda eng: eng.tensor_tensor(a_x2[:], a_x1[:],
                                                        a_ctx[:], Alu.add),
                          waits=[xt] + rst + ctx_toks)
                st["resid_x1"] = at_
                stx = dma("gpsimd",
                          X2[base_t + 128 * tt:base_t + 128 * tt + 128, :],
                          a_x2[:], sem="dg")
                return at_, stx

            def resid_stats(tt, at_, stx):
                ttg = NT128S * seq + tt
                sm = sop(lambda eng, ttg=ttg: eng.activation(
                    a_ctx[:], a_x2[:], Act.Identity, accum_out=m3[:, ttg:ttg + 1]),
                    waits=[at_, st["resid_sq"]])
                sq = sop(lambda eng, ttg=ttg: eng.activation(
                    a_ctx[:], a_x2[:], Act.Square, accum_out=ssq3[:, ttg:ttg + 1]))
                st["resid_sq"] = sq
                v1 = vop(lambda eng, ttg=ttg: eng.reduce_max(
                    mx3[:, ttg:ttg + 1], a_x2[:], AXX), waits=[at_])
                v2 = vop(lambda eng, ttg=ttg: eng.tensor_reduce(
                    mn3[:, ttg:ttg + 1], a_x2[:], AXX, Alu.min))
                st["resid_rm"] = v2
                st["resid_st"] = (stx, sm, sq, v2)

            free_tok = None
            laststx = None
            pend = None
            for tt in range(NT128S):
                if tt in (0, NT128S - 1):
                    toks = do_full(tt, a_ctx, free_tok, pend)
                else:
                    toks = do_mid(tt, a_ctx, free_tok, pend)
                pend = None
                free_tok, laststx = resid_add(tt, toks)
                pend = (tt, free_tok, laststx)
            resid_stats(*pend)
            return laststx

        attention(0, kv_a)
        s.full_barrier()
        kv_b = pass3b(1)
        s.full_barrier()
        attention(1, kv_b)
        vop(lambda eng: eng.tensor_scalar(
            m3[:], m3[:], 1.0 / float(H), None, Alu.mult),
            waits=[s.now("tG"), s.now("tS")])
        vdrain()
        vop(lambda eng: eng.tensor_tensor(xm3[:], m3[:], m3[:], Alu.mult))
        vop(lambda eng: eng.tensor_scalar(
            r3[:], ssq3[:], 1.0 / float(H), 1e-5, Alu.mult, Alu.add))
        vdrain()
        vop(lambda eng: eng.tensor_tensor(r3[:], r3[:], xm3[:], Alu.subtract))
        vdrain()
        vop(lambda eng: eng.reciprocal(r3[:], r3[:]))
        vdrain()
        r3_t = sop(lambda eng: eng.activation(r3[:], r3[:], Act.Sqrt),
                   waits=[s.now("tV")])
        vop(lambda eng: eng.tensor_tensor(mx3[:], mx3[:], m3[:], Alu.subtract))
        vop(lambda eng: eng.tensor_tensor(mn3[:], m3[:], mn3[:], Alu.subtract))
        vdrain()
        vop(lambda eng: eng.tensor_tensor(xm3[:], mx3[:], mn3[:], Alu.max))
        vdrain()
        vop(lambda eng: eng.tensor_tensor(xm3[:], xm3[:], r3[:], Alu.mult),
            waits=[r3_t])
        vdrain()
        vop(lambda eng: eng.reduce_max(amax[2][:], xm3[:], AXX))
        vdrain()

    if stage("all"):
        s.full_barrier()
        bc3 = allreduce_scalar(2, s.now("tV"), [(C_S3, "s"), (C_RS3, "rs")])
        vop(lambda eng: eng.tensor_scalar(
            rq3[:], r3[:], scal_b[:, C_RS3:C_RS3 + 1], None, Alu.mult), waits=bc3)
        vop(lambda eng: eng.tensor_scalar(
            gscale[:], swg[:], scal_b[:, C_S3:C_S3 + 1], None, Alu.mult), waits=bc3)
        vop(lambda eng: eng.tensor_scalar(
            uscale[:], swu[:], scal_b[:, C_S3:C_S3 + 1], None, Alu.mult), waits=bc3)
        for hc in range(HC):
            dma("sync", qwgT[:, hc, :], QWG[hc], sem="dw")
            dma("gpsimd", quT[:, hc, :], QWU[hc], sem="dw")
        for ic in range(IC):
            dma("scalar", qwdT[:, ic, :], QWD[ic], sem="dw")
        s.full_barrier()
        amax4_tok = pass2(X2, m3, rq3, 3)
        bc4 = allreduce_scalar(3, amax4_tok, [(C_RS4, "rs"), (C_S4H, "sh")])
        s.full_barrier()
        pass3a(X2, outd, C_S4H, C_RS4)
        s.op("gpsimd", lambda eng: None,
             waits=[s.now(x) for x in Sched.DMA_SEMS])
        s.op("sync", lambda eng: None,
             waits=[s.now(x) for x in Sched.DMA_SEMS])
    else:
        qs = [s.now(q) for q in Sched.DMA_SEMS]
        for e in ENGS:
            s.op(e, lambda eng: None,
                 waits=qs + [s.now("tV"), s.now("tS"), s.now("tP")])

    import contextlib
    es = contextlib.ExitStack()
    with es:
        sems = {}
        for name in ("qS", "qG", "qC", "qV", "dx", "dx0", "dx1", "dh0", "dh1",
                     "dh2", "dh3", "dw", "dk", "dq0", "dq1", "dr", "de", "dc", "dg",
                     "dg0", "dg1", "dg2", "dg3", "tP", "tV", "tS", "tG", "cc", "bar"):
            sems[name] = es.enter_context(nc.semaphore(name))
        s.bar_handle = sems["bar"]
        with nc.Block() as block:
            def mk(ops):
                def body(eng):
                    for fn, ws, inc in ops:
                        wmax = {}
                        for (sname, v) in ws:
                            wmax[sname] = max(wmax.get(sname, 0), v)
                        for idx, (sname, v) in enumerate(wmax.items()):
                            if idx and idx % 2 == 0:
                                eng.nop(nofuse=True)
                            eng.wait_ge(sems[sname], v)
                        if wmax and eng.engine == mybir.EngineType.PE:
                            eng.nop(nofuse=True)
                        inst = fn(eng)
                        if inc is not None:
                            sem, n = inc
                            inst.then_inc(sems[sem], n)
                return body
            block.sync(mk(s.prog["sync"]))
            block.vector(mk(s.prog["vector"]))
            block.scalar(mk(s.prog["scalar"]))
            block.tensor(mk(s.prog["tensor"]))
            block.gpsimd(mk(s.prog["gpsimd"]))
    return nc


def _shard_inputs(inputs):
    x = np.ascontiguousarray(
        np.asarray(inputs["L0_attn_hidden_states"], dtype=np.float32).reshape(-1, H))
    bq = np.asarray(inputs["L0_attn_bq"], dtype=np.float32)
    bk = np.asarray(inputs["L0_attn_bk"], dtype=np.float32)
    bv = np.asarray(inputs["L0_attn_bv"], dtype=np.float32)
    com = {
        "wq_t": np.ascontiguousarray(np.asarray(inputs["L0_attn_Wq"], dtype=np.float32).T),
        "wk_t": np.ascontiguousarray(np.asarray(inputs["L0_attn_Wk"], dtype=np.float32).T),
        "wv_t": np.ascontiguousarray(np.asarray(inputs["L0_attn_Wv"], dtype=np.float32).T),
        "bq2": np.ascontiguousarray(bq.reshape(8, 128).T),
        "bk2": np.ascontiguousarray(bk.reshape(8, 128).T),
        "bv_row": np.ascontiguousarray(bv.reshape(1, H)),
        "w_gate": np.ascontiguousarray(np.asarray(inputs["L0_ffn_w_gate"], dtype=np.float32)),
        "w_up": np.ascontiguousarray(np.asarray(inputs["L0_ffn_w_up"], dtype=np.float32)),
        "w_down": np.ascontiguousarray(np.asarray(inputs["L0_ffn_w_down"], dtype=np.float32)),
        "clip2": np.array([[float(np.asarray(inputs["L0_ffn_clip_x"]).reshape(-1)[0]),
                            float(np.asarray(inputs["L0_ffn_clip_down"]).reshape(-1)[0])]],
                          dtype=np.float32),
    }
    in_maps = []
    for c in range(NCORE):
        m = dict(com)
        m["x0"] = np.ascontiguousarray(x[c * T:(c + 1) * T])
        in_maps.append(m)
    return in_maps


def kernel(**inputs):
    in_maps = _shard_inputs(inputs)
    nc = build_nc(debug=False, upto="all")
    res = run_bass_kernel_spmd(nc, in_maps, core_ids=list(range(NCORE)))
    out = np.concatenate([np.asarray(r["out"], dtype=np.float32) for r in res.results],
                         axis=0)
    return out.reshape(16, 4096, 1024)


# version 11
# speedup vs baseline: 1.1038x; 1.0236x over previous
import numpy as np
import ml_dtypes
import concourse.bass as bass
import concourse.mybir as mybir
from concourse.bass_utils import run_bass_kernel_spmd

F32 = mybir.dt.float32
BF16 = mybir.dt.bfloat16
Alu = mybir.AluOpType
Act = mybir.ActivationFunctionType
AXX = mybir.AxisListType.X
MAGIC = float(1.5 * 2.0 ** 23)

T = 8192
H = 1024
II = 2816
IC = II // 128
HC = H // 128
SEQ = 4096
NH = 16
NCHUNK = SEQ // 128
NT128S = SEQ // 128
NCORE = 8

ENGS = ("sync", "vector", "scalar", "tensor", "gpsimd")
QSEM = {"sync": "qS", "gpsimd": "qG", "scalar": "qC", "vector": "qV"}
C_S1, C_RS1, C_S2, C_RS2, C_S2H, C_S3, C_RS3, C_S4, C_RS4, C_S4H = range(10)


class Sched:
    def __init__(self):
        self.prog = {e: [] for e in ENGS}
        self.cnt = {}
        self.bar_handle = None

    def op(self, eng, fn, waits=(), inc=None, n=1):
        ws = [w for w in waits if w is not None]
        tok = None
        if inc is not None:
            self.cnt[inc] = self.cnt.get(inc, 0) + n
            tok = (inc, self.cnt[inc])
        self.prog[eng].append((fn, ws, (inc, n) if inc else None))
        return tok

    def now(self, sem):
        return (sem, self.cnt.get(sem, 0))

    DMA_SEMS = ("qS", "qG", "qC", "qV", "dx", "dx0", "dx1", "dh0", "dh1",
                "dh2", "dh3", "dw", "dk", "dq0", "dq1", "dr", "de", "dc", "dg",
                "dg0", "dg1", "dg2", "dg3")

    def full_barrier(self):
        qs = [self.now(q) for q in self.DMA_SEMS]
        for e in ENGS:
            w = qs if e == "gpsimd" else []
            self.op(e, lambda eng: eng.nop(nofuse=True),
                    waits=w, inc="bar", n=1)
        val = self.cnt["bar"]
        for e in ENGS:
            self.op(e, lambda eng: None, waits=[("bar", val)])


def build_nc(debug=False, upto="all"):
    nc = bass.Bass("TRN2", target_bir_lowering=False)
    STAGES = ["pass2", "pass3a", "pass3b", "attn", "all"]
    SI = STAGES.index(upto)

    def stage(name):
        return SI >= STAGES.index(name)

    ext_dbg = "ExternalOutput" if debug else "Internal"

    x0d = nc.dram_tensor("x0", [T, H], F32, kind="ExternalInput").ap()
    outd = nc.dram_tensor("out", [T, H], F32, kind="ExternalOutput").ap()
    wqtd = nc.dram_tensor("wq_t", [H, H], F32, kind="ExternalInput").ap()
    wktd = nc.dram_tensor("wk_t", [H, H], F32, kind="ExternalInput").ap()
    wvtd = nc.dram_tensor("wv_t", [H, H], F32, kind="ExternalInput").ap()
    bq2d = nc.dram_tensor("bq2", [128, 8], F32, kind="ExternalInput").ap()
    bk2d = nc.dram_tensor("bk2", [128, 8], F32, kind="ExternalInput").ap()
    bvrd = nc.dram_tensor("bv_row", [1, H], F32, kind="ExternalInput").ap()
    wgd = nc.dram_tensor("w_gate", [II, H], F32, kind="ExternalInput").ap()
    wud = nc.dram_tensor("w_up", [II, H], F32, kind="ExternalInput").ap()
    wdd = nc.dram_tensor("w_down", [H, II], F32, kind="ExternalInput").ap()
    clipd = nc.dram_tensor("clip2", [1, 2], F32, kind="ExternalInput").ap()

    X1 = nc.dram_tensor("X1", [T, H], F32, kind=ext_dbg).ap()
    X2 = nc.dram_tensor("X2", [T, H], F32, kind=ext_dbg).ap()
    HP = nc.dram_tensor("HP", [IC, 128, T], F32, kind=ext_dbg).ap()
    QT = nc.dram_tensor("QTsp", [T // 128, 128, HC, 128], BF16, kind=ext_dbg).ap()
    QWG = nc.dram_tensor("QWG", [HC, 128, II], BF16).ap()
    QWU = nc.dram_tensor("QWU", [HC, 128, II], BF16).ap()
    QWD = nc.dram_tensor("QWD", [IC, 128, H], BF16).ap()
    WQB = nc.dram_tensor("WQB", [HC, 128, H], BF16).ap()
    WKB = nc.dram_tensor("WKB", [HC, 128, H], BF16).ap()
    WVB = nc.dram_tensor("WVB", [HC, 128, H], BF16).ap()
    SCAL = nc.dram_tensor("SCAL", [1, 16], F32).ap()
    RED = [nc.dram_tensor(f"RED{k}", [128, 1], F32).ap() for k in range(4)]
    CCI = [nc.dram_tensor(f"cci{k}", [1, 16], F32) for k in range(4)]
    CCO = [nc.dram_tensor(f"cco{k}", [1, 16], F32) for k in range(4)]
    if debug:
        KDd = nc.dram_tensor("KD", [128, HC, SEQ], BF16, kind="ExternalOutput").ap()
        VDd = nc.dram_tensor("VD", [128, NCHUNK, H], BF16, kind="ExternalOutput").ap()
        DBG1 = nc.dram_tensor("DBG1", [128, 256], F32, kind="ExternalOutput").ap()
        DBGM = nc.dram_tensor("DBGM", [1, 16], F32, kind="ExternalOutput").ap()
        DBGW = nc.dram_tensor("DBGW", [128, 2816], BF16, kind="ExternalOutput").ap()
        DBGQ = nc.dram_tensor("DBGQ", [128, HC, 512], BF16, kind="ExternalOutput").ap()
        X1C = nc.dram_tensor("X1C", [T, H], F32, kind="ExternalOutput").ap()
        QHD = nc.dram_tensor("QHD", [32, 2, 128, 256], BF16, kind="ExternalOutput").ap()

    identd = nc.inline_tensor(np.eye(128, dtype=ml_dtypes.bfloat16), name="identc")

    ARENA = 212832
    nc.alloc_sbuf_tensor("bigarena", [128, ARENA], mybir.dt.uint8)
    base = nc.lookup_mloc("bigarena").addr
    LIMIT = base + ARENA
    cur = [base]

    def at(name, shape, dtype, off=None):
        nbytes = int(np.prod(shape[1:])) * mybir.dt.size(dtype)
        if off is None:
            o = cur[0]
            cur[0] = o + ((nbytes + 31) // 32) * 32
        else:
            o = base + off
        assert o + nbytes <= LIMIT, f"SBUF overflow {name}: {o + nbytes - base}"
        return nc.alloc_sbuf_tensor_at(name, shape, dtype, offset=o)

    ident = at("ident", [128, 128], BF16)
    biasq = at("biasq", [128, 8], F32)
    biask = at("biask", [128, 8], F32)
    bvx = at("bvx", [128, H], F32)
    scal_b = at("scal_b", [128, 16], F32)
    m1 = at("m1", [128, 64], F32)
    r1 = at("r1", [128, 64], F32)
    rq1 = at("rq1", [128, 64], F32)
    m3 = at("m3", [128, 64], F32)
    r3 = at("r3", [128, 64], F32)
    rq3 = at("rq3", [128, 64], F32)
    swg = at("swg", [128, 22], F32)
    swu = at("swu", [128, 22], F32)
    gscale = at("gscale", [128, 22], F32)
    uscale = at("uscale", [128, 22], F32)
    amax = [at(f"amax{k}", [128, 1], F32) for k in range(4)]
    xm1 = at("xm1", [128, 64], F32)
    xm3 = at("xm3", [128, 64], F32)
    biasq8 = at("biasq8", [128, 8], F32)
    red_row = at("red_row", [1, 128], F32)
    misc = at("misc", [1, 16], F32)
    n30 = at("n30", [128, 1], F32)
    ssq3 = at("ssq3", [128, 64], F32)
    mx3 = at("mx3", [128, 64], F32)
    mn3 = at("mn3", [128, 64], F32)
    GLOB_END = cur[0] - base
    assert GLOB_END < 9500, GLOB_END

    WREG = ((GLOB_END + 63) // 64) * 64
    qwgT = at("qwgT", [128, HC, II], BF16, off=WREG)
    quT = at("quT", [128, HC, II], BF16, off=WREG + 45056)
    qwdT = at("qwdT", [128, IC, H], BF16, off=WREG + 90112)
    kT = at("kT", [128, HC, SEQ], BF16, off=WREG)
    Vt = at("Vt", [128, NCHUNK, H], BF16, off=WREG + 65536)
    WORK = WREG + 135168

    def wat(name, shape, dtype, off):
        return at(name, shape, dtype, off=WORK + off)

    wstage = [wat(f"wstage{i}", [128, II], F32, 11264 * i) for i in range(2)]
    qwnat = wat("qwnat", [128, II], BF16, 22528)
    qkvb16 = wat("qkvb16", [128, H], BF16, 28160)
    wsc = wat("wsc", [128, 8], F32, 30208)
    px = [wat("px0", [128, 4, H], F32, 30976)]
    ptmp = wat("ptmp", [128, H], F32, 47360)
    ptmp2 = wat("ptmp2", [128, H], F32, 51456)
    pcol = wat("pcol", [128, 4], F32, 55552)

    p2_x = [wat(f"p2x{i}", [128, 4, H], F32, 16384 * i) for i in range(2)]
    p2_qx = wat("p2qx", [128, 4, H], BF16, 32768)
    p2_qxT = wat("p2qxT", [128, HC, 512], BF16, 40960)
    p2_t = wat("p2t", [128, H], F32, 49152)
    p2_sg = [wat(f"p2sg{i}", [128, 512], F32, 53248 + 2048 * i) for i in range(2)]
    p2_hp = [wat(f"p2hp{i}", [128, 512], F32, 57344 + 2048 * i) for i in range(4)]
    p2_col = wat("p2col", [128, 24], F32, 65536)

    p3_hp = [wat(f"p3hp{i}", [128, 256], F32, 1024 * i) for i in range(4)]
    p3_qh = [wat(f"p3qh{i}", [128, 256], BF16, 4096 + 512 * i) for i in range(4)]
    p3_x0 = [wat("p3x0", [128, 2, H], F32, 6144)]
    p3_x1 = wat("p3x1", [128, 2, H], F32, 14336)

    b_wq = wat("bwq", [128, HC, H], BF16, 0)
    b_wk = wat("bwk", [128, HC, H], BF16, 16384)
    b_wv = wat("bwv", [128, HC, H], BF16, 32768)
    b_x1 = [wat("bx1", [128, 2, H], F32, 49152)]
    b_ln = wat("bln", [128, H], BF16, 57344)
    b_xT = wat("bxT", [128, HC, 256], BF16, 59392)
    b_col = wat("bcol", [128, 8], F32, 63488)
    b_pt = wat("bpt", [128, H], F32, 63520)
    b_qs = wat("bqs", [128, 2, HC, 128], BF16, 63520)

    a_kTe = wat("akTe", [128, HC, 128], BF16, 0)
    a_VtE = wat("aVtE", [128, H], BF16, 2048)
    a_qt = [wat(f"aqt{i}", [128, HC, 128], BF16, 4096 + 2048 * i) for i in range(2)]
    a_A = [wat(f"aA{i}", [128, 512], BF16, 8192 + 1024 * i) for i in range(4)]
    a_aT = [wat(f"aaT{i}", [128, 512], BF16, 12288 + 1024 * i) for i in range(2)]
    a_ctx = wat("actx", [128, H], F32, 14336)
    a_x1 = wat("ax1", [128, H], F32, 18432)
    a_x2 = wat("ax2", [128, H], F32, 22528)
    a_Sd = [wat(f"aS{i}", [128, SEQ], F32, 26624 + 16384 * i) for i in range(2)]
    a_AF = wat("aAF", [128, SEQ], BF16, 59392)
    a_xc = wat("axc", [128, H], F32, 59392)
    a_col = wat("acol", [128, 16], F32, 67584)
    g_col = wat("gcol", [128, 4], F32, 67648)

    pb = [nc.alloc_psum_tensor(f"pb{i}", [128, 512], F32) for i in range(8)]
    pbh = [p.ap().bitcast(BF16) for p in pb]

    s = Sched()

    def dma(q, out, in_, waits=(), sem=None, **kw):
        return s.op(q, lambda eng, o=out, i=in_, kw=kw: eng.dma_start(out=o, in_=i, **kw),
                    waits=waits, inc=(sem or QSEM[q]), n=16)

    def vop(fn, waits=()):
        return s.op("vector", fn, waits=waits, inc="tV")

    def gop(fn, waits=()):
        return s.op("gpsimd", fn, waits=waits, inc="tG")

    def gdrain():
        s.op("gpsimd", lambda eng: eng.drain())

    def sop(fn, waits=()):
        return s.op("scalar", fn, waits=waits, inc="tS")

    def top(fn, waits=()):
        return s.op("tensor", fn, waits=waits, inc="tP")

    def vdrain():
        s.op("vector", lambda eng: eng.drain())

    def sdrain():
        s.op("scalar", lambda eng: eng.drain())

    cast_tok = None
    spill_tok = None
    for wi, (src, dst) in enumerate(((wqtd, WQB), (wktd, WKB), (wvtd, WVB))):
        for hc in range(HC):
            b = hc % 2
            ldt = dma("sync", wstage[b][:, 0:H], src[128 * hc:128 * hc + 128, :],
                      waits=[cast_tok], sem="dw")
            cast_tok = vop(lambda eng, b=b: eng.tensor_copy(qkvb16[:], wstage[b][:, 0:H]),
                           waits=[ldt, spill_tok])
            spill_tok = dma("gpsimd", dst[hc], qkvb16[:], waits=[cast_tok], sem="dg")
    dma("sync", biasq[:], bq2d[:, :], sem="dk")
    dma("sync", biask[:], bk2d[:, :], sem="dk")
    bvx_ld = dma("sync", bvx[:], bvrd[0:1, :].broadcast_to((128, H)), sem="dk")
    dma("sync", ident[:], identd.ap(), sem="dk")
    kc_ld = dma("sync", misc[0:1, 0:2], clipd[0:1, :], sem="dk")
    id_ld = kc_ld
    clip_ld = kc_ld
    vop(lambda eng: eng.tensor_scalar(
        biasq8[:], biasq[:], 0.125, None, Alu.mult), waits=[kc_ld])
    sig_tok = sop(lambda eng: eng.activation(misc[0:1, 2:4], misc[0:1, 0:2],
                                             Act.Sigmoid), waits=[clip_ld])
    vop(lambda eng: eng.memset(n30[:], -1e30))

    def quant_weight_gu(srcd, dstT, sw_dst):
        q_tok = None
        ev_tok = None
        for ic in range(IC):
            b = ic % 2
            ldt = dma("sync", wstage[b][:, 0:H], srcd[128 * ic:128 * ic + 128, :],
                      waits=[q_tok], sem="dw")
            rm = vop(lambda eng, b=b: eng.reduce_max(wsc[:, 0:1], wstage[b][:, 0:H], AXX,
                                                     apply_absolute_value=True),
                     waits=[ldt])
            vdrain()
            vop(lambda eng, ic=ic, d=sw_dst: eng.tensor_scalar(
                d[:, ic:ic + 1], wsc[:, 0:1], 1.0 / 127.0, None, Alu.mult))
            vop(lambda eng: eng.reciprocal(wsc[:, 1:2], wsc[:, 0:1]))
            vdrain()
            vop(lambda eng, b=b: eng.tensor_scalar(
                wstage[b][:, 0:H], wstage[b][:, 0:H], wsc[:, 1:2], 127.0,
                Alu.mult, Alu.mult))
            q_tok = vop(lambda eng, b=b: eng.tensor_scalar(
                qwnat[:, 0:H], wstage[b][:, 0:H], MAGIC, MAGIC, Alu.add, Alu.subtract),
                waits=[ev_tok])
            for hcx in range(HC):
                pbk = 6 + (hcx % 2)
                tr = top(lambda eng, hcx=hcx, pbk=pbk: eng.transpose(
                    pbh[pbk][:, 0:128], qwnat[:, 128 * hcx:128 * hcx + 128], ident[:]),
                    waits=[q_tok, ev_tok, id_ld])
                ev_tok = vop(lambda eng, hcx=hcx, ic=ic, pbk=pbk, d=dstT: eng.tensor_copy(
                    d[:, hcx, 128 * ic:128 * ic + 128], pbh[pbk][:, 0:128]), waits=[tr])
        return ev_tok

    gu_ev = quant_weight_gu(wgd, qwgT, swg)
    gu_ev2 = quant_weight_gu(wud, quT, swu)

    def quant_weight_down():
        ev_tok = None
        q_tok = None
        for hcx in range(HC):
            b = hcx % 2
            ldt = dma("sync", wstage[b][:], wdd[128 * hcx:128 * hcx + 128, :],
                      waits=[q_tok], sem="dw")
            rm = vop(lambda eng, b=b: eng.reduce_max(wsc[:, 0:1], wstage[b][:], AXX,
                                                     apply_absolute_value=True),
                     waits=[ldt])
            vdrain()
            vop(lambda eng: eng.reciprocal(wsc[:, 1:2], wsc[:, 0:1]))
            vop(lambda eng: eng.tensor_scalar(
                wsc[:, 2:3], wsc[:, 0:1], 1.0 / 127.0, None, Alu.mult))
            vdrain()
            vop(lambda eng, b=b: eng.tensor_scalar(
                wstage[b][:], wstage[b][:], wsc[:, 1:2], 127.0, Alu.mult, Alu.mult))
            vop(lambda eng, b=b: eng.tensor_scalar(
                wstage[b][:], wstage[b][:], MAGIC, MAGIC, Alu.add, Alu.subtract))
            q_tok = vop(lambda eng, b=b: eng.tensor_scalar(
                qwnat[:], wstage[b][:], wsc[:, 2:3], None, Alu.mult),
                waits=[ev_tok])
            for ic in range(IC):
                pbk = 6 + (ic % 2)
                tr = top(lambda eng, ic=ic, pbk=pbk: eng.transpose(
                    pbh[pbk][:, 0:128], qwnat[:, 128 * ic:128 * ic + 128], ident[:]),
                    waits=[q_tok, ev_tok, id_ld])
                ev_tok = vop(lambda eng, ic=ic, hcx=hcx, pbk=pbk: eng.tensor_copy(
                    qwdT[:, ic, 128 * hcx:128 * hcx + 128], pbh[pbk][:, 0:128]),
                    waits=[tr])
        return ev_tok

    dn_ev = quant_weight_down()
    for hc in range(HC):
        dma("gpsimd", QWG[hc], qwgT[:, hc, :], waits=[gu_ev], sem="dg")
        dma("gpsimd", QWU[hc], quT[:, hc, :], waits=[gu_ev2], sem="dg")
    for ic in range(IC):
        dma("gpsimd", QWD[ic], qwdT[:, ic, :], waits=[dn_ev], sem="dg")

    ld_tok = [None, None]
    q_done = [None, None]
    sq_prev = None
    for t in range(16):
        b = 0
        ld_tok[b] = dma("sync", px[b][:],
                        x0d[512 * t:512 * t + 512, :].rearrange("(a p) h -> p a h", p=128),
                        waits=[q_done[b]], sem="dx")
        for sub in range(4):
            tt = 4 * t + sub
            vop(lambda eng, b=b, sub=sub: eng.tensor_reduce(
                pcol[:, 0:1], px[b][:, sub, :], AXX, Alu.add), waits=[ld_tok[b]])
            vdrain()
            vop(lambda eng, tt=tt: eng.tensor_scalar(
                m1[:, tt:tt + 1], pcol[:, 0:1], 1.0 / float(H), None, Alu.mult))
            vdrain()
            xc_t = vop(lambda eng, b=b, sub=sub, tt=tt: eng.tensor_scalar(
                ptmp[:], px[b][:, sub, :], m1[:, tt:tt + 1], None, Alu.subtract),
                waits=[sq_prev])
            sq_prev = sop(lambda eng: eng.activation(ptmp2[:], ptmp[:], Act.Square,
                                                     accum_out=pcol[:, 1:2]),
                          waits=[xc_t])
            vop(lambda eng: eng.tensor_scalar(
                pcol[:, 1:2], pcol[:, 1:2], 1.0 / float(H), 1e-5, Alu.mult, Alu.add),
                waits=[sq_prev])
            vdrain()
            vop(lambda eng: eng.reciprocal(pcol[:, 1:2], pcol[:, 1:2]))
            vdrain()
            r_t = sop(lambda eng, tt=tt: eng.activation(
                r1[:, tt:tt + 1], pcol[:, 1:2], Act.Sqrt), waits=[s.now("tV")])
            q_done[b] = vop(lambda eng, tt=tt: eng.reduce_max(
                xm1[:, tt:tt + 1], ptmp[:], AXX,
                apply_absolute_value=True), waits=[sq_prev])
    vdrain()
    vop(lambda eng: eng.tensor_tensor(xm1[:], xm1[:], r1[:], Alu.mult),
        waits=[s.now("tS")])
    vdrain()
    vop(lambda eng: eng.reduce_max(amax[0][:], xm1[:], AXX))
    vdrain()

    def allreduce_scalar(k, amax_tok, extra_cols):
        sig_col = 2 if k in (0, 2) else 3
        t1 = dma("gpsimd", RED[k][:, :], amax[k][:, 0:1], waits=[amax_tok], sem="dc")
        t2 = dma("gpsimd", red_row[0:1, :], RED[k].transpose([1, 0]), waits=[t1],
                 allow_slow_non_contiguous=True, sem="dc")
        mx = vop(lambda eng: eng.reduce_max(misc[0:1, 4:5], red_row[0:1, :], AXX),
                 waits=[t2])
        t3 = dma("gpsimd", CCI[k].ap()[0:1, 0:1], misc[0:1, 4:5], waits=[mx], sem="dc")
        cc_t = s.op("gpsimd", lambda eng, k=k: eng.collective_compute(
            "AllReduce", Alu.max, replica_groups=[list(range(NCORE))],
            ins=[CCI[k].ap().opt()], outs=[CCO[k].ap().opt()]),
            waits=[t3], inc="cc", n=1)
        t4 = dma("gpsimd", misc[0:1, 4:5], CCO[k].ap()[0:1, 0:1], waits=[cc_t], sem="dc")
        vop(lambda eng, sc=sig_col: eng.tensor_tensor(
            misc[0:1, 5:6], misc[0:1, 4:5], misc[0:1, sc:sc + 1], Alu.mult),
            waits=[t4, sig_tok])
        vdrain()
        vop(lambda eng: eng.tensor_scalar(
            misc[0:1, 6:7], misc[0:1, 5:6], 1.0 / 127.0, None, Alu.mult))
        vdrain()
        rsv = vop(lambda eng: eng.reciprocal(misc[0:1, 7:8], misc[0:1, 6:7]))
        shv = vop(lambda eng: eng.tensor_scalar(
            misc[0:1, 8:9], misc[0:1, 6:7], 0.5, None, Alu.mult))
        vdrain()
        toks = []
        for (col, tag) in extra_cols:
            srcc = {"s": 6, "rs": 7, "sh": 8}[tag]
            t5 = dma("gpsimd", SCAL[0:1, col:col + 1], misc[0:1, srcc:srcc + 1],
                     waits=[rsv, shv], sem="dc")
            toks.append(dma("gpsimd", scal_b[:, col:col + 1],
                            SCAL[0:1, col:col + 1].broadcast_to((128, 1)), waits=[t5],
                            sem="dc"))
        return toks

    bc1 = allreduce_scalar(0, s.now("tV"), [(C_S1, "s"), (C_RS1, "rs")])
    vop(lambda eng: eng.tensor_scalar(
        rq1[:], r1[:], scal_b[:, C_RS1:C_RS1 + 1], None, Alu.mult), waits=bc1)
    vop(lambda eng: eng.tensor_scalar(
        gscale[:], swg[:], scal_b[:, C_S1:C_S1 + 1], None, Alu.mult),
        waits=bc1 + [gu_ev])
    vop(lambda eng: eng.tensor_scalar(
        uscale[:], swu[:], scal_b[:, C_S1:C_S1 + 1], None, Alu.mult),
        waits=bc1 + [gu_ev2])
    s.full_barrier()

    def pass2(xsrc, mcache, rqcache, amax_idx):
        vop(lambda eng, k=amax_idx: eng.memset(amax[k][:], 0.0))
        ld_tok = [None, None]
        lastq = [None, None]
        silu_tok = [None, None]
        hp_tok = [None, None, None, None]
        hpdma_tok = [None, None, None, None]
        evq_tok = [None, None]
        tr_last = None
        amax_tok = None
        for t in range(16):
            b = t % 2
            ld_tok[b] = dma("sync", p2_x[b][:],
                            xsrc[512 * t:512 * t + 512, :].rearrange("(a p) h -> p a h",
                                                                     p=128),
                            waits=[lastq[b]], sem=f"dx{b}")
            for sub in range(4):
                tt = 4 * t + sub
                vop(lambda eng, b=b, sub=sub, tt=tt, mc=mcache, rc=rqcache:
                    eng.tensor_scalar(p2_t[:], p2_x[b][:, sub, :], mc[:, tt:tt + 1],
                                      rc[:, tt:tt + 1], Alu.subtract, Alu.mult),
                    waits=[ld_tok[b]] + ([tr_last] if sub == 0 else []))
                vop(lambda eng: eng.tensor_scalar(
                    p2_t[:], p2_t[:], MAGIC, MAGIC, Alu.add, Alu.subtract))
                lastq[b] = vop(lambda eng, sub=sub: eng.tensor_scalar(
                    p2_qx[:, sub, :], p2_t[:], 127.0, -128.0, Alu.min, Alu.max))
            for sub in range(4):
                for hc in range(HC):
                    j = (sub * HC + hc) % 2
                    pbk = 4 + j
                    tr = top(lambda eng, sub=sub, hc=hc, pbk=pbk: eng.transpose(
                        pbh[pbk][:, 0:128], p2_qx[:, sub, 128 * hc:128 * hc + 128],
                        ident[:]), waits=[lastq[b], evq_tok[j]])
                    evq_tok[j] = vop(lambda eng, sub=sub, hc=hc, pbk=pbk: eng.tensor_copy(
                        p2_qxT[:, hc, 128 * sub:128 * sub + 128], pbh[pbk][:, 0:128]),
                        waits=[tr])
            tr_ev = [evq_tok[0], evq_tok[1]]
            mm_g = None
            mm_u = None
            for ic in range(IC):
                pg = ic % 2
                pu = 2 + ic % 2
                for hc in range(HC):
                    mm_g = top(lambda eng, ic=ic, hc=hc, pg=pg: eng.matmul(
                        pb[pg][:], qwgT[:, hc, 128 * ic:128 * ic + 128],
                        p2_qxT[:, hc, :],
                        start=(hc == 0), stop=(hc == HC - 1)),
                        waits=((tr_ev if ic == 0 else []) + [silu_tok[ic % 2]])
                        if hc == 0 else [])
                for hc in range(HC):
                    mm_u = top(lambda eng, ic=ic, hc=hc, pu=pu: eng.matmul(
                        pb[pu][:], quT[:, hc, 128 * ic:128 * ic + 128],
                        p2_qxT[:, hc, :],
                        start=(hc == 0), stop=(hc == HC - 1)),
                        waits=[hp_tok[(ic + 2) % 4]] if hc == 0 else [])
                silu_tok[ic % 2] = sop(lambda eng, ic=ic, pg=pg: eng.activation(
                    p2_sg[ic % 2][:], pb[pg][:], Act.Silu, scale=gscale[:, ic:ic + 1]),
                    waits=[mm_g, hp_tok[(ic + 2) % 4]])
                hb = ic % 4
                hp_tok[hb] = vop(lambda eng, ic=ic, pu=pu, hb=hb:
                                 eng.scalar_tensor_tensor(
                                     p2_hp[hb][:], pb[pu][:], uscale[:, ic:ic + 1],
                                     p2_sg[ic % 2][:], Alu.mult, Alu.mult),
                                 waits=[mm_u, silu_tok[ic % 2], hpdma_tok[hb]])
                vop(lambda eng, hb=hb, ic=ic: eng.reduce_max(
                    p2_col[:, ic:ic + 1], p2_hp[hb][:], AXX,
                    apply_absolute_value=True))
                hpdma_tok[hb] = dma("gpsimd" if hb % 2 == 0 else "scalar",
                                    HP[ic, :, 512 * t:512 * t + 512],
                                    p2_hp[hb][:], waits=[hp_tok[hb]], sem=f"dg{hb}")
            vdrain()
            vop(lambda eng: eng.reduce_max(p2_col[:, 22:23], p2_col[:, 0:IC], AXX))
            vdrain()
            amax_tok = vop(lambda eng, k=amax_idx: eng.tensor_tensor(
                amax[k][:], amax[k][:], p2_col[:, 22:23], Alu.max))
            vdrain()
            tr_last = mm_g
        return amax_tok

    amax2_tok = pass2(x0d, m1, rq1, 1)
    bc2 = allreduce_scalar(1, amax2_tok, [(C_RS2, "rs"), (C_S2H, "sh")])
    s.full_barrier()
    if debug:
        dma("gpsimd", DBG1[:, 0:1], amax[0][:], allow_slow_non_contiguous=True, sem="dg")
        dma("gpsimd", DBG1[:, 1:2], amax[1][:], allow_slow_non_contiguous=True, sem="dg")
        dma("gpsimd", DBG1[:, 16:32], scal_b[:], allow_slow_non_contiguous=True, sem="dg")
        dma("gpsimd", DBG1[:, 32:96], m1[:], allow_slow_non_contiguous=True, sem="dg")
        dma("gpsimd", DBG1[:, 96:160], r1[:], allow_slow_non_contiguous=True, sem="dg")
        dma("gpsimd", DBG1[:, 160:224], rq1[:], allow_slow_non_contiguous=True, sem="dg")
        dma("gpsimd", DBG1[:, 224:246], gscale[:], allow_slow_non_contiguous=True, sem="dg")
        dma("gpsimd", DBGM[:, :], misc[:], sem="dg")
        dma("gpsimd", DBGW[:, :], qwgT[:, 0, :], sem="dg")
        dma("gpsimd", DBGQ[:, :, :], p2_qxT[:], sem="dg")
        s.full_barrier()

    if stage("pass3a"):
        def pass3a(xres, dstd, shcol, rscol):
            ld_tok = [None] * 4
            qh_tok = [None] * 4
            mmh = [None] * 4
            mm_tok = None
            ev_toks = [None] * 4
            stq = None
            last_store = None
            for u in range(32):
                x0ld = dma("sync", p3_x0[0][:],
                           xres[256 * u:256 * u + 256, :].rearrange(
                               "(a p) h -> p a h", p=128),
                           waits=[stq], sem="dx")
                for ic in range(IC):
                    hb = ic % 4
                    ld_tok[hb] = dma("sync" if ic % 2 == 0 else "scalar",
                                     p3_hp[hb][:],
                                     HP[ic, :, 256 * u:256 * u + 256],
                                     waits=[qh_tok[hb]], sem=f"dh{hb}")
                    vop(lambda eng, hb=hb, rs=rscol: eng.tensor_scalar(
                        p3_hp[hb][:], p3_hp[hb][:], scal_b[:, rs:rs + 1], MAGIC,
                        Alu.mult, Alu.add), waits=[ld_tok[hb]])
                    vop(lambda eng, hb=hb: eng.tensor_scalar(
                        p3_hp[hb][:], p3_hp[hb][:], MAGIC, 127.0,
                        Alu.subtract, Alu.min))
                    qh_tok[hb] = vop(lambda eng, hb=hb: eng.tensor_scalar(
                        p3_qh[hb][:], p3_hp[hb][:], -128.0, None, Alu.max),
                        waits=[mmh[hb]])
                    if debug and ic in (0, 1) and dstd is X1:
                        dma("gpsimd", QHD[u, ic], p3_qh[hb][:], waits=[qh_tok[hb]],
                            sem="dg")
                    for th in range(2):
                        for fh in range(2):
                            pbk = 2 * th + fh
                            mm_tok = top(lambda eng, ic=ic, th=th, fh=fh, hb=hb,
                                         pbk=pbk:
                                         eng.matmul(
                                             pb[pbk][:],
                                             p3_qh[hb][:, 128 * th:128 * th + 128],
                                             qwdT[:, ic, 512 * fh:512 * fh + 512],
                                             start=(ic == 0), stop=(ic == IC - 1)),
                                         waits=[qh_tok[hb]] +
                                         ([ev_toks[pbk]] if ic == 0 else []))
                    mmh[hb] = mm_tok
                for th in range(2):
                    for fh in range(2):
                        pbk = 2 * th + fh
                        ev_toks[pbk] = vop(
                            lambda eng, th=th, fh=fh, pbk=pbk, sh=shcol:
                            eng.scalar_tensor_tensor(
                                p3_x1[:, th, 512 * fh:512 * fh + 512],
                                pb[pbk][:], scal_b[:, sh:sh + 1],
                                p3_x0[0][:, th, 512 * fh:512 * fh + 512],
                                Alu.mult, Alu.add),
                            waits=[mm_tok, x0ld, stq])
                stq = dma("gpsimd",
                          dstd[256 * u:256 * u + 256, :].rearrange(
                              "(a p) h -> p a h", p=128),
                          p3_x1[:], waits=[ev_toks[3], ev_toks[0], ev_toks[1],
                                           ev_toks[2]], sem="dg")
                last_store = stq
            return last_store

        pass3a(x0d, X1, C_S2H, C_RS2)
        s.full_barrier()
        if debug:
            for t in range(16):
                dma("gpsimd", X1C[512 * t:512 * t + 512, :], X1[512 * t:512 * t + 512, :],
                    sem="dg")
            s.full_barrier()

    if stage("pass3b"):
        def pass3b(seq):
            base_t = SEQ * seq
            wtoks = []
            for hc in range(HC):
                wtoks.append(dma("sync", b_wq[:, hc, :], WQB[hc], sem="dw"))
                wtoks.append(dma("sync", b_wk[:, hc, :], WKB[hc], sem="dw"))
                wtoks.append(dma("sync", b_wv[:, hc, :], WVB[hc], sem="dw"))
            wq_ld = wtoks[-1]
            lastuse = None
            ev_xT = [None, None]
            qk_ev = [None, None]
            v_ev = [None, None]
            qs_dma = None
            sq_prev2 = None
            k_last = None
            v_last = None
            for u in range(16):
                ld_tok = dma("sync", b_x1[0][:],
                             X1[base_t + 256 * u: base_t + 256 * u + 256, :]
                             .rearrange("(a p) h -> p a h", p=128),
                             waits=[lastuse], sem="dx")
                for th in range(2):
                    vop(lambda eng, th=th: eng.tensor_reduce(
                        b_col[:, 0:1], b_x1[0][:, th, :], AXX, Alu.add),
                        waits=[ld_tok])
                    vdrain()
                    vop(lambda eng: eng.tensor_scalar(
                        b_col[:, 1:2], b_col[:, 0:1], 1.0 / float(H), None, Alu.mult))
                    vdrain()
                    xc_t = vop(lambda eng, th=th: eng.tensor_scalar(
                        b_pt[:], b_x1[0][:, th, :], b_col[:, 1:2], None, Alu.subtract),
                        waits=[sq_prev2, qs_dma])
                    sq_prev2 = sop(lambda eng: eng.activation(
                        b_ln[:], b_pt[:], Act.Square, accum_out=b_col[:, 2:3]),
                        waits=[xc_t, ev_xT[0], ev_xT[1]])
                    vop(lambda eng: eng.tensor_scalar(
                        b_col[:, 2:3], b_col[:, 2:3], 1.0 / float(H), 1e-5,
                        Alu.mult, Alu.add), waits=[sq_prev2])
                    vdrain()
                    vop(lambda eng: eng.reciprocal(b_col[:, 2:3], b_col[:, 2:3]))
                    vdrain()
                    r_t = sop(lambda eng: eng.activation(
                        b_col[:, 3:4], b_col[:, 2:3], Act.Sqrt), waits=[s.now("tV")])
                    lastuse = vop(lambda eng: eng.tensor_scalar(
                        b_ln[:], b_pt[:], b_col[:, 3:4], None, Alu.mult),
                        waits=[r_t])
                    for hc in range(HC):
                        j = hc % 2
                        pbk = 6 + j
                        tr = top(lambda eng, hc=hc, pbk=pbk: eng.transpose(
                            pbh[pbk][:, 0:128], b_ln[:, 128 * hc:128 * hc + 128],
                            ident[:]), waits=[lastuse, ev_xT[j]])
                        ev_xT[j] = vop(lambda eng, th=th, hc=hc, pbk=pbk:
                                       eng.tensor_copy(
                                           b_xT[:, hc, 128 * th:128 * th + 128],
                                           pbh[pbk][:, 0:128]), waits=[tr])
                exT = [ev_xT[0], ev_xT[1]]
                mm_qk = None
                for which in range(2):
                    wsrc = b_wq if which == 0 else b_wk
                    for fc in range(HC):
                        pbk = which
                        for hc in range(HC):
                            mm_qk = top(lambda eng, wsrc=wsrc, fc=fc, hc=hc, pbk=pbk:
                                        eng.matmul(
                                            pb[pbk][:, 0:256],
                                            wsrc[:, hc, 128 * fc:128 * fc + 128],
                                            b_xT[:, hc, :],
                                            start=(hc == 0), stop=(hc == HC - 1)),
                                        waits=((exT + [wq_ld] if fc == 0 else [])
                                               + [qk_ev[pbk]])
                                        if hc == 0 else [])
                        if which == 0:
                            qk_ev[pbk] = sop(lambda eng, fc=fc, pbk=pbk: eng.activation(
                                b_qs[:, :, fc, :], pb[pbk][:, 0:256], Act.Identity,
                                bias=biasq8[:, fc:fc + 1], scale=0.125),
                                waits=[mm_qk] + ([qs_dma] if fc == 0 else []))
                        else:
                            qk_ev[pbk] = sop(lambda eng, fc=fc, pbk=pbk, u=u:
                                             eng.activation(
                                                 kT[:, fc, 256 * u:256 * u + 256],
                                                 pb[pbk][:, 0:256], Act.Identity,
                                                 bias=biask[:, fc:fc + 1]),
                                             waits=[mm_qk])
                            k_last = qk_ev[pbk]
                for j in range(2):
                    blk = 32 * seq + 2 * u + j
                    qs_dma = dma("gpsimd", QT[blk], b_qs[:, j], waits=[qk_ev[0]],
                                 sem="dg")
                mm_v = None
                for th in range(2):
                    for fh in range(2):
                        pbk = 2 + fh
                        for hc in range(HC):
                            mm_v = top(lambda eng, th=th, fh=fh, hc=hc, pbk=pbk:
                                       eng.matmul(
                                           pb[pbk][:],
                                           b_xT[:, hc, 128 * th:128 * th + 128],
                                           b_wv[:, hc, 512 * fh:512 * fh + 512],
                                           start=(hc == 0), stop=(hc == HC - 1)),
                                       waits=(exT + [v_ev[fh], wq_ld])
                                       if hc == 0 else [])
                        ch = 2 * u + th
                        v_ev[fh] = vop(lambda eng, ch=ch, fh=fh, pbk=pbk:
                                       eng.tensor_tensor(
                                           Vt[:, ch, 512 * fh:512 * fh + 512],
                                           pb[pbk][:], bvx[:, 512 * fh:512 * fh + 512],
                                           Alu.add), waits=[mm_v, bvx_ld])
                        v_last = v_ev[fh]
            return k_last, v_last

        kv_a = pass3b(0)
        if debug:
            for fc in range(HC):
                dma("gpsimd", KDd[:, fc, :], kT[:, fc, :],
                    waits=[kv_a[0], kv_a[1]], sem="dg")
            for ch in range(NCHUNK):
                dma("gpsimd", VDd[:, ch, :], Vt[:, ch, :],
                    waits=[kv_a[0], kv_a[1]], sem="dg")
        s.full_barrier()

    if stage("attn"):

        def attention(seq, kv):
            base_t = SEQ * seq
            e1 = dma("gpsimd", a_VtE[0:64, :], Vt[0:64, 0, :], waits=list(kv),
                     sem="de")
            e2 = dma("gpsimd", a_VtE[64:128, :], Vt[64:128, NCHUNK - 1, :],
                     waits=list(kv), sem="de")
            e3 = vop(lambda eng: eng.tensor_copy(a_kTe[:, :, 0:64], kT[:, :, 0:64]),
                     waits=list(kv))
            e4 = vop(lambda eng: eng.tensor_copy(a_kTe[:, :, 64:128],
                                                 kT[:, :, SEQ - 64:SEQ]))

            st = {
                "qld": [None, None],
                "qt_use": [None, None],
                "s_ev": [None, None, None, None],
                "tr_ev": [None, None],
                "A_use": [None, None, None, None],
                "ctx_ev": [None, None],
                "resid_x1": None, "resid_st": None, "resid_xc": None,
                "resid_sq": None, "resid_rm": None,
            }

            def load_qt(tt):
                b = tt % 2
                qld_t = dma("sync", a_qt[b][:], QT[32 * seq + tt],
                            waits=[st["qt_use"][b], st["qld"][b]], sem=f"dq{b}")
                st["qld"][b] = qld_t
                return b, qld_t

            def do_mid(tt, ctx_dst, gate_dst, pend):
                b, qld_t = load_qt(tt)
                vc = tt - 1
                segs = [("edge", None), ("win", vc), ("win", vc + 1), ("win", vc + 2)]
                mask_list = [(0, 64, 128, 192), (0, 64, 384, 512),
                             (64, 128, 128, 256), (64, 128, 448, 512)]
                SBANK = (0, 1, 6, 7)
                out_toks = []

                def stage1(j):
                    hp, hh = divmod(j, 2)
                    p0 = 64 * hh
                    sl = j % 4
                    pS = SBANK[sl]
                    w1 = [qld_t, st["s_ev"][sl]] if j < 4 else [st["s_ev"][sl]]
                    if j == 0:
                        w1 += [kv[0], kv[1], e4]
                    top(lambda eng, p0=p0, hp=hp, pS=pS, b=b:
                        eng.matmul(pb[pS][:, 0:128],
                                   a_qt[b][p0:p0 + 64, hp, :],
                                   a_kTe[p0:p0 + 64, hp, :],
                                   start=True, stop=False),
                        waits=w1)
                    k0 = 128 * vc
                    mmt = top(lambda eng, p0=p0, hp=hp, pS=pS, b=b, k0=k0:
                              eng.matmul(pb[pS][:, 128:512],
                                         a_qt[b][p0:p0 + 64, hp, :],
                                         kT[p0:p0 + 64, hp, k0:k0 + 384],
                                         start=False, stop=True,
                                         skip_group_check=True))
                    st["qt_use"][b] = mmt
                    for mi, (r0, r1_, c0, c1) in enumerate(mask_list):
                        vop(lambda eng, pS=pS, r0=r0, r1_=r1_, c0=c0, c1=c1:
                            eng.memset(pb[pS][r0:r1_, c0:c1], -1e30),
                            waits=[mmt] if mi == 0 else [])
                    nm = vop(lambda eng, pS=pS, sl=sl: eng.reduce_max(
                        a_col[:, sl:sl + 1], pb[pS][:], AXX, negate=True))
                    ex = sop(lambda eng, pS=pS, sl=sl: eng.activation(
                        a_A[sl][:], pb[pS][:], Act.Exp,
                        bias=a_col[:, sl:sl + 1],
                        accum_out=a_col[:, 4 + sl:5 + sl]),
                        waits=[nm, st["A_use"][sl]])
                    st["s_ev"][sl] = ex
                    return ex

                def stage2(j, ex):
                    hp, hh = divmod(j, 2)
                    h = 2 * hp + hh
                    sl = j % 4
                    rc = vop(lambda eng, sl=sl: eng.reciprocal(
                        a_col[:, 8 + sl:9 + sl], a_col[:, 4 + sl:5 + sl]),
                        waits=[ex, st["ctx_ev"][hh]])
                    cmm = None
                    for ci in range(4):
                        pT = 2 + (ci % 2)
                        tr = top(lambda eng, sl=sl, ci=ci, pT=pT: eng.transpose(
                            pbh[pT][:, 0:128],
                            a_A[sl][:, 128 * ci:128 * ci + 128], ident[:]),
                            waits=[ex, st["tr_ev"][ci % 2]])
                        st["A_use"][sl] = tr
                        if ci % 2 == 0:
                            tre = vop(lambda eng, hh=hh, ci=ci, pT=pT:
                                      eng.tensor_copy(
                                          a_aT[hh][:, 128 * ci:128 * ci + 128],
                                          pbh[pT][:, 0:128]), waits=[tr])
                        else:
                            tre = sop(lambda eng, hh=hh, ci=ci, pT=pT:
                                      eng.activation(
                                          a_aT[hh][:, 128 * ci:128 * ci + 128],
                                          pbh[pT][:, 0:128], Act.Identity),
                                      waits=[tr])
                        st["tr_ev"][ci % 2] = tre
                        styp, c = segs[ci]
                        pC = 4 + hh
                        extra = [st["ctx_ev"][hh]] if ci == 0 else []
                        if ci == 0 and hh == 1 and st.get("sev2"):
                            extra.append(st.pop("sev2"))
                        if styp == "edge":
                            cmm = top(lambda eng, hh=hh, ci=ci, h=h, pC=pC:
                                      eng.matmul(
                                          pb[pC][:, 0:64],
                                          a_aT[hh][:, 128 * ci:128 * ci + 128],
                                          a_VtE[:, 64 * h:64 * h + 64],
                                          start=(ci == 0), stop=(ci == 3)),
                                      waits=[tre, e2] + extra)
                        else:
                            cmm = top(lambda eng, hh=hh, ci=ci, h=h, c=c, pC=pC:
                                      eng.matmul(
                                          pb[pC][:, 0:64],
                                          a_aT[hh][:, 128 * ci:128 * ci + 128],
                                          Vt[:, c, 64 * h:64 * h + 64],
                                          start=(ci == 0), stop=(ci == 3)),
                                      waits=[tre] + extra)
                    cev = sop(lambda eng, h=h, sl=sl, pC=4 + hh, dst=ctx_dst:
                              eng.activation(dst[:, 64 * h:64 * h + 64],
                                             pb[pC][:, 0:64], Act.Identity,
                                             scale=a_col[:, 8 + sl:9 + sl]),
                              waits=[cmm, rc, gate_dst])
                    st["ctx_ev"][hh] = cev
                    out_toks.append(cev)

                exs = {}
                for j in range(4):
                    exs[j] = stage1(j)
                if pend is not None:
                    resid_stats(*pend)
                for j in range(4, 16):
                    stage2(j - 4, exs[j - 4])
                    exs[j] = stage1(j)
                for j in range(12, 16):
                    stage2(j, exs[j])
                return out_toks

            def do_full(tt, ctx_dst, gate_dst, pend):
                b, qld_t = load_qt(tt)
                if tt == 0:
                    mrow, mcol = (64, 128), (192, 4032)
                else:
                    mrow, mcol = (0, 64), (64, 3904)
                out_toks = []
                sev = [None, None, None]
                SB = (0, 1, 5)
                exF = [None, None]
                rcF = [None, None]
                trlast = [None]

                def s1F(h):
                    hp, hh = divmod(h, 2)
                    p0 = 64 * hh
                    sb = h % 2
                    aS = a_Sd[sb]
                    mmt = None
                    for kc in range(8):
                        pS = SB[kc % 3]
                        if kc < 3:
                            wf = [sev[kc % 3]]
                            if h == 0:
                                wf += [qld_t, st["ctx_ev"][0], st["ctx_ev"][1],
                                       st["s_ev"][0], st["s_ev"][1], kv[0], kv[1]]
                        else:
                            wf = [sev[kc % 3]]
                        mmt = top(lambda eng, hp=hp, p0=p0, kc=kc, pS=pS, b=b:
                                  eng.matmul(
                                      pb[pS][:], a_qt[b][p0:p0 + 64, hp, :],
                                      kT[p0:p0 + 64, hp, 512 * kc:512 * kc + 512],
                                      start=True, stop=True),
                                  waits=wf)
                        sev[kc % 3] = vop(lambda eng, kc=kc, pS=pS, aS=aS:
                                          eng.tensor_copy(
                                              aS[:, 512 * kc:512 * kc + 512],
                                              pb[pS][:]),
                                          waits=[mmt] +
                                          ([exF[sb]] if kc == 0 else []))
                    st["qt_use"][b] = mmt
                    vop(lambda eng, aS=aS: eng.memset(
                        aS[mrow[0]:mrow[1], mcol[0]:mcol[1]], -1e30))
                    nm = vop(lambda eng, aS=aS, sb=sb: eng.reduce_max(
                        a_col[:, 0 + sb:1 + sb], aS[:], AXX, negate=True),
                        waits=[exF[sb]])
                    return nm

                def s2F(h, nm):
                    hp, hh = divmod(h, 2)
                    sb = h % 2
                    aS = a_Sd[sb]
                    ex = sop(lambda eng, aS=aS, sb=sb: eng.activation(
                        a_AF[:], aS[:], Act.Exp, bias=a_col[:, 0 + sb:1 + sb],
                        accum_out=a_col[:, 4 + sb:5 + sb]),
                        waits=[nm, trlast[0], rcF[sb], st["resid_rm"]])
                    exF[sb] = ex
                    rc = vop(lambda eng, sb=sb: eng.reciprocal(
                        a_col[:, 8 + sb:9 + sb], a_col[:, 4 + sb:5 + sb]),
                        waits=[ex, st["ctx_ev"][0]])
                    rcF[sb] = rc
                    cmm = None
                    for ci in range(NCHUNK):
                        j = ci % 2
                        pT = 2 + j
                        tr = top(lambda eng, ci=ci, pT=pT: eng.transpose(
                            pbh[pT][:, 0:128], a_AF[:, 128 * ci:128 * ci + 128],
                            ident[:]), waits=[ex, st["tr_ev"][j]])
                        trlast[0] = tr
                        if j == 0:
                            tre = vop(lambda eng, j=j, pT=pT: eng.tensor_copy(
                                a_aT[j][:, 0:128], pbh[pT][:, 0:128]), waits=[tr])
                        else:
                            tre = sop(lambda eng, j=j, pT=pT: eng.activation(
                                a_aT[j][:, 0:128], pbh[pT][:, 0:128],
                                Act.Identity), waits=[tr])
                        st["tr_ev"][j] = tre
                        cmm = top(lambda eng, ci=ci, h=h, j=j: eng.matmul(
                            pb[4][:, 0:64], a_aT[j][:, 0:128],
                            Vt[:, ci, 64 * h:64 * h + 64],
                            start=(ci == 0), stop=(ci == NCHUNK - 1)),
                            waits=[tre] + ([st["ctx_ev"][0]] if ci == 0 else []))
                    cev = sop(lambda eng, h=h, sb=sb, dst=ctx_dst: eng.activation(
                        dst[:, 64 * h:64 * h + 64], pb[4][:, 0:64], Act.Identity,
                        scale=a_col[:, 8 + sb:9 + sb]), waits=[cmm, rc, gate_dst])
                    st["ctx_ev"][0] = cev
                    st["ctx_ev"][1] = cev
                    out_toks.append(cev)

                nms = {}
                nms[0] = s1F(0)
                if pend is not None:
                    resid_stats(*pend)
                for h in range(1, NH):
                    nms[h] = s1F(h)
                    s2F(h - 1, nms[h - 1])
                s2F(NH - 1, nms[NH - 1])
                st["s_ev"][0] = sev[0]
                st["s_ev"][1] = sev[1]
                st["sev2"] = sev[2]
                return out_toks

            def resid_add(tt, ctx_toks):
                xt = dma("sync", a_x1[:],
                         X1[base_t + 128 * tt:base_t + 128 * tt + 128, :],
                         waits=[st["resid_x1"]], sem="dr")
                rst = st["resid_st"]
                rst = list(rst) if isinstance(rst, tuple) else [rst]
                at_ = gop(lambda eng: eng.tensor_tensor(a_x2[:], a_x1[:],
                                                        a_ctx[:], Alu.add),
                          waits=[xt] + rst + ctx_toks)
                st["resid_x1"] = at_
                stx = dma("gpsimd",
                          X2[base_t + 128 * tt:base_t + 128 * tt + 128, :],
                          a_x2[:], sem="dg")
                return at_, stx

            def resid_stats(tt, at_, stx):
                ttg = NT128S * seq + tt
                sm = sop(lambda eng, ttg=ttg: eng.activation(
                    a_ctx[:], a_x2[:], Act.Identity, accum_out=m3[:, ttg:ttg + 1]),
                    waits=[at_, st["resid_sq"]])
                sq = sop(lambda eng, ttg=ttg: eng.activation(
                    a_ctx[:], a_x2[:], Act.Square, accum_out=ssq3[:, ttg:ttg + 1]))
                st["resid_sq"] = sq
                v1 = vop(lambda eng, ttg=ttg: eng.reduce_max(
                    mx3[:, ttg:ttg + 1], a_x2[:], AXX), waits=[at_])
                v2 = vop(lambda eng, ttg=ttg: eng.tensor_reduce(
                    mn3[:, ttg:ttg + 1], a_x2[:], AXX, Alu.min))
                st["resid_rm"] = v2
                st["resid_st"] = (stx, sm, sq, v2)

            free_tok = None
            laststx = None
            pend = None
            for tt in range(NT128S):
                if tt in (0, NT128S - 1):
                    toks = do_full(tt, a_ctx, free_tok, pend)
                else:
                    toks = do_mid(tt, a_ctx, free_tok, pend)
                pend = None
                free_tok, laststx = resid_add(tt, toks)
                pend = (tt, free_tok, laststx)
            resid_stats(*pend)
            return laststx

        attention(0, kv_a)
        s.full_barrier()
        kv_b = pass3b(1)
        s.full_barrier()
        attention(1, kv_b)
        vop(lambda eng: eng.tensor_scalar(
            m3[:], m3[:], 1.0 / float(H), None, Alu.mult),
            waits=[s.now("tG"), s.now("tS")])
        vdrain()
        vop(lambda eng: eng.tensor_tensor(xm3[:], m3[:], m3[:], Alu.mult))
        vop(lambda eng: eng.tensor_scalar(
            r3[:], ssq3[:], 1.0 / float(H), 1e-5, Alu.mult, Alu.add))
        vdrain()
        vop(lambda eng: eng.tensor_tensor(r3[:], r3[:], xm3[:], Alu.subtract))
        vdrain()
        vop(lambda eng: eng.reciprocal(r3[:], r3[:]))
        vdrain()
        r3_t = sop(lambda eng: eng.activation(r3[:], r3[:], Act.Sqrt),
                   waits=[s.now("tV")])
        vop(lambda eng: eng.tensor_tensor(mx3[:], mx3[:], m3[:], Alu.subtract))
        vop(lambda eng: eng.tensor_tensor(mn3[:], m3[:], mn3[:], Alu.subtract))
        vdrain()
        vop(lambda eng: eng.tensor_tensor(xm3[:], mx3[:], mn3[:], Alu.max))
        vdrain()
        vop(lambda eng: eng.tensor_tensor(xm3[:], xm3[:], r3[:], Alu.mult),
            waits=[r3_t])
        vdrain()
        vop(lambda eng: eng.reduce_max(amax[2][:], xm3[:], AXX))
        vdrain()

    if stage("all"):
        s.full_barrier()
        bc3 = allreduce_scalar(2, s.now("tV"), [(C_S3, "s"), (C_RS3, "rs")])
        vop(lambda eng: eng.tensor_scalar(
            rq3[:], r3[:], scal_b[:, C_RS3:C_RS3 + 1], None, Alu.mult), waits=bc3)
        vop(lambda eng: eng.tensor_scalar(
            gscale[:], swg[:], scal_b[:, C_S3:C_S3 + 1], None, Alu.mult), waits=bc3)
        vop(lambda eng: eng.tensor_scalar(
            uscale[:], swu[:], scal_b[:, C_S3:C_S3 + 1], None, Alu.mult), waits=bc3)
        for hc in range(HC):
            dma("sync", qwgT[:, hc, :], QWG[hc], sem="dw")
            dma("gpsimd", quT[:, hc, :], QWU[hc], sem="dw")
        for ic in range(IC):
            dma("scalar", qwdT[:, ic, :], QWD[ic], sem="dw")
        s.full_barrier()
        amax4_tok = pass2(X2, m3, rq3, 3)
        bc4 = allreduce_scalar(3, amax4_tok, [(C_RS4, "rs"), (C_S4H, "sh")])
        s.full_barrier()
        pass3a(X2, outd, C_S4H, C_RS4)
        s.op("gpsimd", lambda eng: None,
             waits=[s.now(x) for x in Sched.DMA_SEMS])
        s.op("sync", lambda eng: None,
             waits=[s.now(x) for x in Sched.DMA_SEMS])
    else:
        qs = [s.now(q) for q in Sched.DMA_SEMS]
        for e in ENGS:
            s.op(e, lambda eng: None,
                 waits=qs + [s.now("tV"), s.now("tS"), s.now("tP")])

    import contextlib
    es = contextlib.ExitStack()
    with es:
        sems = {}
        for name in ("qS", "qG", "qC", "qV", "dx", "dx0", "dx1", "dh0", "dh1",
                     "dh2", "dh3", "dw", "dk", "dq0", "dq1", "dr", "de", "dc", "dg",
                     "dg0", "dg1", "dg2", "dg3", "tP", "tV", "tS", "tG", "cc", "bar"):
            sems[name] = es.enter_context(nc.semaphore(name))
        s.bar_handle = sems["bar"]
        with nc.Block() as block:
            def mk(ops):
                def body(eng):
                    for fn, ws, inc in ops:
                        wmax = {}
                        for (sname, v) in ws:
                            wmax[sname] = max(wmax.get(sname, 0), v)
                        for idx, (sname, v) in enumerate(wmax.items()):
                            if idx and idx % 2 == 0:
                                eng.nop(nofuse=True)
                            eng.wait_ge(sems[sname], v)
                        if wmax and eng.engine == mybir.EngineType.PE:
                            eng.nop(nofuse=True)
                        inst = fn(eng)
                        if inc is not None:
                            sem, n = inc
                            inst.then_inc(sems[sem], n)
                return body
            block.sync(mk(s.prog["sync"]))
            block.vector(mk(s.prog["vector"]))
            block.scalar(mk(s.prog["scalar"]))
            block.tensor(mk(s.prog["tensor"]))
            block.gpsimd(mk(s.prog["gpsimd"]))
    return nc


def _shard_inputs(inputs):
    x = np.ascontiguousarray(
        np.asarray(inputs["L0_attn_hidden_states"], dtype=np.float32).reshape(-1, H))
    bq = np.asarray(inputs["L0_attn_bq"], dtype=np.float32)
    bk = np.asarray(inputs["L0_attn_bk"], dtype=np.float32)
    bv = np.asarray(inputs["L0_attn_bv"], dtype=np.float32)
    com = {
        "wq_t": np.ascontiguousarray(np.asarray(inputs["L0_attn_Wq"], dtype=np.float32).T),
        "wk_t": np.ascontiguousarray(np.asarray(inputs["L0_attn_Wk"], dtype=np.float32).T),
        "wv_t": np.ascontiguousarray(np.asarray(inputs["L0_attn_Wv"], dtype=np.float32).T),
        "bq2": np.ascontiguousarray(bq.reshape(8, 128).T),
        "bk2": np.ascontiguousarray(bk.reshape(8, 128).T),
        "bv_row": np.ascontiguousarray(bv.reshape(1, H)),
        "w_gate": np.ascontiguousarray(np.asarray(inputs["L0_ffn_w_gate"], dtype=np.float32)),
        "w_up": np.ascontiguousarray(np.asarray(inputs["L0_ffn_w_up"], dtype=np.float32)),
        "w_down": np.ascontiguousarray(np.asarray(inputs["L0_ffn_w_down"], dtype=np.float32)),
        "clip2": np.array([[float(np.asarray(inputs["L0_ffn_clip_x"]).reshape(-1)[0]),
                            float(np.asarray(inputs["L0_ffn_clip_down"]).reshape(-1)[0])]],
                          dtype=np.float32),
    }
    in_maps = []
    for c in range(NCORE):
        m = dict(com)
        m["x0"] = np.ascontiguousarray(x[c * T:(c + 1) * T])
        in_maps.append(m)
    return in_maps


def kernel(**inputs):
    in_maps = _shard_inputs(inputs)
    nc = build_nc(debug=False, upto="all")
    res = run_bass_kernel_spmd(nc, in_maps, core_ids=list(range(NCORE)))
    out = np.concatenate([np.asarray(r["out"], dtype=np.float32) for r in res.results],
                         axis=0)
    return out.reshape(16, 4096, 1024)
